# version 06 (final)
# speedup vs baseline: 1.0898x; 1.0661x over previous
import numpy as np
import concourse.bass as bass
import concourse.mybir as mybir
from concourse.bass_utils import run_bass_kernel_spmd

F32 = mybir.dt.float32
BF16 = mybir.dt.bfloat16
I32 = mybir.dt.int32
U32 = mybir.dt.uint32
AF = mybir.ActivationFunctionType
ALU = mybir.AluOpType
AX = mybir.AxisListType.X

NCORES = 8
TOK = 2048
DM = 1024
NCH = DM // 128
ARENA_BYTES = 212736


class Sem:
    __slots__ = ("h", "v", "name")

    def __init__(self, h, name):
        self.h = h
        self.v = 0
        self.name = name

    def inc(self, inst, n=1):
        inst.then_inc(self.h, n)
        self.v += n
        return inst


class Arena:

    def __init__(self, handle, nbytes_per_part):
        self.t = handle
        self.nbytes = nbytes_per_part
        self.off = 0

    def take(self, dtype, cols):
        esz = 2 if dtype == BF16 else 4
        off = (self.off + 63) & ~63
        nb = cols * esz
        assert off + nb <= self.nbytes, f"arena overflow: {off}+{nb} > {self.nbytes}"
        self.off = off + nb
        self.last_off = off
        return self.view(off, dtype, cols)

    def sub(self, byte_off, nbytes):
        a = Arena(self.t, byte_off + nbytes)
        a.off = byte_off
        return a

    def view(self, byte_off, dtype, cols):
        base = self.t.ap()
        if dtype == F32:
            return base[:, byte_off // 4: byte_off // 4 + cols]
        nb = cols * (2 if dtype == BF16 else 4)
        v = base[:, byte_off // 4: byte_off // 4 + (nb + 3) // 4].bitcast(dtype)
        return v[:, 0:cols]

    def mark(self):
        return self.off

    def reset(self, m):
        self.off = m


class Prog:
    def __init__(self, nc, block, arena):
        self.nc = nc
        self.block = block
        self.ar = arena
        self._sems = {}
        self.bankg = {}
        self.bar_sem = self.sem("bar")
        self.ps = [nc.alloc_psum_tensor(f"ps{i}", [128, 512], F32) for i in range(7)]
        self.psbf = nc.alloc_psum_tensor("psbf", [128, 1024], BF16)

    def sem(self, name):
        if name not in self._sems:
            h = self.nc.alloc_semaphore(name="sem_" + name)
            self._sems[name] = Sem(h, name)
        return self._sems[name]

    def E(self, engine):
        return getattr(self.block, engine)

    def bar(self, engines=("tensor", "vector", "scalar", "gpsimd", "sync")):
        s = self.bar_sem
        for name in engines:
            def _go(e, name=name):
                if name == "vector":
                    e.memset(self.dummy[0:1, 0:1], 0.0).then_inc(s.h, 1)
                elif name == "scalar":
                    e.memzero(self.dummy[0:1, 2:3]).then_inc(s.h, 1)
                elif name == "gpsimd":
                    e.memset(self.dummy[0:1, 4:5], 0.0).then_inc(s.h, 1)
                else:
                    e.sem_inc(s.h, 1)
            self.E(name)(_go)
        s.v += len(engines)
        tgt = s.v
        for name in engines:
            def _go(e, tgt=tgt):
                e.wait_ge(s.h, tgt)
            self.E(name)(_go)
        self.bankg = {}



def _dram_shared(nc, name, shape, dtype):
    reg = getattr(nc, "_dram_reg", None)
    if reg is None:
        reg = {}
        nc._dram_reg = reg
    if name not in reg:
        reg[name] = nc.dram_tensor(name, shape, dtype, kind="ExternalInput").ap()
    return reg[name]


def dram_colblock(wT, oc, p=128, col0=0):
    r = wT.rearrange("(a p) n -> p a n", p=p)
    return r[:, :, col0 + oc * 128: col0 + (oc + 1) * 128]


def sb3(view, n=128):
    return view.rearrange("p (a n) -> p a n", n=n)


def emit_ln(P, XR, XL, tag, t0=0, ncols=TOK, out_f32=None, pb=4, scratch=None,
            src_bf=False, gamma=None, beta=None, src_cols=TOK, dst_cols=None,
            dst0=0):
    nc = P.nc
    ar = scratch if scratch is not None else P.ar
    m0 = ar.mark()
    SQ = ar.take(F32, 8 * 512)
    BC_R = ar.take(F32, 512)
    BC_B = ar.take(F32, 512)
    rows = ar.take(F32, 1536)
    t512 = ar.take(F32, 512)
    s_sq = P.sem("ln_sq")
    s_pe = P.sem("ln_pe")
    s_v1 = P.sem("ln_v1")
    s_s2 = P.sem("ln_s2")
    s_v2 = P.sem("ln_v2")
    s_pe2 = P.sem("ln_pe2")
    s_s3 = P.sem("ln_s3")
    s_ap = P.sem("ln_ap")
    ps_s, ps_q, ps_b = P.ps[pb], P.ps[pb + 1], P.ps[pb + 2]
    ones_col = P.ones_col_bf if src_bf else P.ones_col_f
    ones_row = P.ones_row_f
    r_m = rows[0:1, 0:512]
    r_v = rows[0:1, 512:1024]
    r_t = rows[0:1, 1024:1536]
    if dst_cols is None:
        dst_cols = ncols

    slabs = []
    done = 0
    while done < ncols:
        w = min(512, ncols - done)
        slabs.append((done, w))
        done += w

    for (sl_off, w) in slabs:
        a = t0 + sl_off

        def _sq(_e, a=a, w=w):
            s = nc.scalar
            for c in range(NCH):
                inst = s.activation(SQ[:, c * 512: c * 512 + w],
                                    XR[:, c * src_cols + a: c * src_cols + a + w],
                                    AF.Square)
            s_sq.inc(inst)
        P.E("scalar")(_sq)

        def _pe(_e, a=a, w=w, sqv=s_sq.v, apv=s_ap.v):
            t = nc.tensor
            t.wait_ge(s_sq.h, sqv)
            if apv:
                t.wait_ge(s_ap.h, apv)
            for c in range(NCH):
                t.matmul(ps_s[0:1, 0:w], ones_col,
                         XR[:, c * src_cols + a: c * src_cols + a + w],
                         start=(c == 0), stop=(c == NCH - 1))
            for c in range(NCH):
                inst = t.matmul(ps_q[0:1, 0:w], P.ones_col_f, SQ[:, c * 512: c * 512 + w],
                                start=(c == 0), stop=(c == NCH - 1))
            s_pe.inc(inst)
        P.E("tensor")(_pe)

        def _v1(_e, w=w, pev=s_pe.v):
            v = nc.vector
            v.wait_ge(s_pe.h, pev)
            v.tensor_scalar(r_m[:, 0:w], ps_s[0:1, 0:w], 1.0 / DM, None, ALU.mult)
            v.tensor_scalar(r_t[:, 0:w], ps_q[0:1, 0:w], 1.0 / DM, None, ALU.mult)
            v.tensor_tensor(r_v[:, 0:w], r_m[:, 0:w], r_m[:, 0:w], ALU.mult)
            v.tensor_tensor(r_v[:, 0:w], r_t[:, 0:w], r_v[:, 0:w], ALU.subtract)
            s_v1.inc(v.tensor_scalar(r_v[:, 0:w], r_v[:, 0:w], 1e-5, None, ALU.add))
        P.E("vector")(_v1)

        def _s2(_e, w=w, v1=s_v1.v):
            s = nc.scalar
            s.wait_ge(s_v1.h, v1)
            s.activation(r_v[:, 0:w], r_v[:, 0:w], AF.Ln)
            s_s2.inc(s.activation(r_v[:, 0:w], r_v[:, 0:w], AF.Exp, scale=-0.5))
        P.E("scalar")(_s2)

        def _v2(_e, w=w, sv=s_s2.v):
            v = nc.vector
            v.wait_ge(s_s2.h, sv)
            v.tensor_tensor(r_m[:, 0:w], r_m[:, 0:w], r_v[:, 0:w], ALU.mult)
            s_v2.inc(v.tensor_scalar(r_m[:, 0:w], r_m[:, 0:w], -1.0, None, ALU.mult))
        P.E("vector")(_v2)

        def _pe2(_e, w=w, vv=s_v2.v):
            t = nc.tensor
            t.wait_ge(s_v2.h, vv)
            t.matmul(ps_b[:, 0:w], ones_row, r_v[:, 0:w], start=True, stop=True)
            s_pe2.inc(t.matmul(ps_s[:, 0:w], ones_row, r_m[:, 0:w], start=True, stop=True))
        P.E("tensor")(_pe2)

        def _s3(_e, w=w, pv=s_pe2.v):
            s = nc.scalar
            s.wait_ge(s_pe2.h, pv)
            s.copy(BC_R[:, 0:w], ps_b[:, 0:w])
            s_s3.inc(s.copy(BC_B[:, 0:w], ps_s[:, 0:w]))
        P.E("scalar")(_s3)

        def _ap(_e, a=a, w=w, sl_off=sl_off, sv=s_s3.v):
            v = nc.vector
            v.wait_ge(s_s3.h, sv)
            for c in range(NCH):
                v.tensor_tensor(t512[:, 0:w], XR[:, c * src_cols + a: c * src_cols + a + w],
                                BC_R[:, 0:w], ALU.mult)
                d0 = c * dst_cols + dst0 + sl_off
                dst = XL[:, d0: d0 + w]
                if gamma is None:
                    inst = v.tensor_tensor(dst, t512[:, 0:w], BC_B[:, 0:w], ALU.add)
                else:
                    v.tensor_tensor(t512[:, 0:w], t512[:, 0:w], BC_B[:, 0:w], ALU.add)
                    inst = v.tensor_scalar(dst, t512[:, 0:w], gamma[:, c:c + 1],
                                           beta[:, c:c + 1], ALU.mult, ALU.add)
                if out_f32 is not None:
                    d2 = out_f32[:, c * dst_cols + dst0 + sl_off:
                                 c * dst_cols + dst0 + sl_off + w]
                    inst = v.tensor_copy(d2, dst)
            s_ap.inc(inst)
        P.E("vector")(_ap)
    ar.reset(m0)


def mm_group(P, s_pe, waits, bank, wtiles, rhs_fn, nk, mpart=128):
    nc = P.nc
    war = P.bankg.get(bank)

    def _pe(_e, war=war, waits=tuple(waits)):
        t = nc.tensor
        for (sm, val) in waits:
            if val:
                t.wait_ge(sm.h, val)
        if war:
            t.wait_ge(war[0].h, war[1])
        for k in range(nk):
            w = wtiles(k) if callable(wtiles) else wtiles[:, k * 128:(k + 1) * 128]
            inst = t.matmul(P.ps[bank][0:mpart, :], w, rhs_fn(k),
                            start=(k == 0), stop=(k == nk - 1))
        s_pe.inc(inst)
    P.E("tensor")(_pe)
    return s_pe.v


NSPILL = 6
DEBUG = {}
import os as _os
STAGE_LIMIT = 'Z'
FFN_PH = _os.environ.get('FFN_PH', 'Z')
M_N = int(_os.environ.get('M_N', '16'))
M_SKIP = _os.environ.get('M_SKIP', '')
A_OC = int(_os.environ.get('A_OC', '32'))
HALVES = int(_os.environ.get('HALVES', '2'))


def emit_l1_ffn(P, d, XR, tag):
    nc = P.nc
    ar = P.ar
    m0 = ar.mark()
    HT = 1024
    XLH = ar.take(BF16, NCH * HT)
    DEBUG["xlh_off"] = ar.last_off
    h_a = ar.take(BF16, 32 * HT)
    ha_off = ar.last_off
    DEBUG["ha_off"] = ha_off
    g1_lo = ar.view(32768, BF16, 16 * HT)
    g1_hi = ar.take(BF16, 16 * HT)
    DEBUG["g1hi_off"] = ar.last_off

    def g1_at(k, si):
        if k < 16:
            return g1_lo[:, k * HT + si * 512: k * HT + si * 512 + 512]
        return g1_hi[:, (k - 16) * HT + si * 512: (k - 16) * HT + si * 512 + 512]

    def ha_at(k, si):
        return h_a[:, k * HT + si * 512: k * HT + si * 512 + 512]
    t1b = [ar.view(24576 + i * 1024, BF16, 512) for i in range(2)]
    t2b = [ar.view(26624 + i * 1024, BF16, 512) for i in range(2)]
    wab = [ar.take(BF16, 1024) for _ in range(2)]
    wbb = [ar.take(BF16, 1024) for _ in range(2)]
    wcb = [ar.view(16384, BF16, 4096), ar.take(BF16, 4096)]
    xrs = [ar.view(28672 + i * 2048, F32, 512) for i in range(2)]
    s_w = P.sem("ffw")
    s_pe = P.sem("pe")
    s_ev = P.sem("ev")
    s_sp = P.sem("ffsp")
    s_spg = P.sem("ffspg")
    wg = {}
    spill = d["xr_spill"]

    def _spill(e):
        for j in range(NSPILL):
            c0 = NCH - NSPILL + j
            e.dma_start(out=spill.rearrange("(j p) n -> p j n", p=128)[:, j, :],
                        in_=XR[:, c0 * TOK:(c0 + 1) * TOK]).then_inc(s_sp.h, 16)
        e.wait_ge(s_sp.h, s_sp.v + 16 * NSPILL)
    P.E("sync")(_spill)
    s_sp.v += 16 * NSPILL
    P.bar()

    for half in range(HALVES):
        hoff = half * HT
        def _rl(e, gv=None):
            e.wait_ge(s_spg.h, s_spg.v)
            for j in range(NSPILL):
                c0 = NCH - NSPILL + j
                e.dma_start(out=XR[:, c0 * TOK:(c0 + 1) * TOK],
                            in_=spill.rearrange("(j p) n -> p j n", p=128)[:, j, :]
                            ).then_inc(s_sp.h, 16)
            e.wait_ge(s_sp.h, s_sp.v + 16 * NSPILL)
        P.E("sync")(_rl)
        s_sp.v += 16 * NSPILL
        P.bar()
        emit_ln(P, XR, XLH, f"{tag}ln{half}", t0=hoff, ncols=HT,
                scratch=ar.sub(ha_off, 32 * 1024))
        P.bar()

        def xl_at(k, si):
            return XLH[:, k * HT + si * 512: k * HT + si * 512 + 512]

        if STAGE_LIMIT < "A":
            continue
        st1 = P.sem("fft1")
        for oc in range(A_OC):
            bs = oc % 2

            def _dma(_e, oc=oc, bs=bs, w1=wg.get(("A", oc % 2), 0), w2=wg.get(("A2", oc % 2), 0)):
                g = nc.gpsimd
                if w1:
                    g.wait_ge(s_pe.h, w1)
                g.dma_start(out=sb3(wab[bs]), in_=dram_colblock(d["l1f_w1T"], oc)
                            ).then_inc(s_w.h, 16)
                if w2:
                    g.wait_ge(s_pe.h, w2)
                g.dma_start(out=sb3(wbb[bs]), in_=dram_colblock(d["l1f_w2T"], oc)
                            ).then_inc(s_w.h, 16)
            P.E("gpsimd")(_dma)
            s_w.v += 32
            wv = s_w.v
            for si in range(2):
                pv1 = mm_group(P, s_pe, [(s_w, wv if si == 0 else 0)], 2 * bs,
                               wab[bs], lambda k, si=si: xl_at(k, si), 8)
                pv2 = mm_group(P, s_pe, [], 2 * bs + 1,
                               wbb[bs], lambda k, si=si: xl_at(k, si), 8)
                par = si

                def _act(_e, bs=bs, oc=oc, pv=pv2, par=par,
                         tg=P.bankg.get(("t1", si))):
                    s = nc.scalar
                    s.wait_ge(s_pe.h, pv)
                    if tg:
                        s.wait_ge(tg[0].h, tg[1])
                    st1.inc(s.activation(t1b[par], P.ps[2 * bs][:, :], AF.Silu,
                                          scale=d["sb_s1"][:, oc:oc + 1]))
                P.E("scalar")(_act)

                def _vec(_e, bs=bs, oc=oc, si=si, par=par, t1v=st1.v):
                    v = nc.vector
                    v.wait_ge(st1.h, t1v)
                    v.tensor_scalar(t2b[par], P.ps[2 * bs + 1][:, :],
                                    d["sb_s2"][:, oc:oc + 1], None, ALU.mult)
                    dst = h_a[:, oc * HT + si * 512: oc * HT + si * 512 + 512]
                    s_ev.inc(v.tensor_tensor(dst, t1b[par], t2b[par], ALU.mult))
                P.E("vector")(_vec)
                P.bankg[2 * bs] = (s_ev, s_ev.v)
                P.bankg[2 * bs + 1] = (s_ev, s_ev.v)
                P.bankg[("t1", par)] = (s_ev, s_ev.v)
            wg[("A", bs)] = pv1
            wg[("A2", bs)] = pv2

        if STAGE_LIMIT < "B":
            continue
        for st, wT, sc, src_at, dst_at in (
            ("B", d["l1f_h1T"], d["sb_sh1"], ha_at, g1_at),
            ("C", d["l1f_h2T"], d["sb_sh2"], g1_at, ha_at),
        ):
            if STAGE_LIMIT < st:
                continue
            for oc in range(32):
                bs = oc % 2

                def _dma(_e, oc=oc, bs=bs, wT=wT, wgv=wg.get(("BC", oc % 2), 0)):
                    g = nc.gpsimd
                    if wgv:
                        g.wait_ge(s_pe.h, wgv)
                    g.dma_start(out=sb3(wcb[bs]), in_=dram_colblock(wT, oc)
                                ).then_inc(s_w.h, 16)
                P.E("gpsimd")(_dma)
                s_w.v += 16
                wv = s_w.v
                bank = 4 + (oc % 2)
                for si in range(2):
                    pv = mm_group(P, s_pe, [(s_w, wv if si == 0 else 0)], bank,
                                  wcb[bs],
                                  lambda k, si=si, src_at=src_at: src_at(k, si),
                                  32)

                    def _act(_e, oc=oc, si=si, pv=pv, sc=sc, dst_at=dst_at, bank=bank):
                        s = nc.scalar
                        s.wait_ge(s_pe.h, pv)
                        dd = dst_at(oc, si)
                        s_ev.inc(s.activation(dd, P.ps[bank][:, :], AF.Gelu,
                                              scale=sc[:, oc:oc + 1]))
                    P.E("scalar")(_act)
                    P.bankg[bank] = (s_ev, s_ev.v)
                wg[("BC", oc % 2)] = pv

        if STAGE_LIMIT < "D":
            continue
        for oc in range(NCH):
            bs = oc % 2

            def _dma(_e, oc=oc, bs=bs, wgv=wg.get(("BC", oc % 2), 0)):
                g = nc.gpsimd
                if wgv:
                    g.wait_ge(s_pe.h, wgv)
                g.dma_start(out=sb3(wcb[bs]), in_=dram_colblock(d["l1f_w3T"], oc)
                            ).then_inc(s_w.h, 16)
            P.E("gpsimd")(_dma)
            s_w.v += 16
            wv = s_w.v
            for si in range(2):
                a = hoff + si * 512
                pv = mm_group(P, s_pe, [(s_w, wv if si == 0 else 0)], 6,
                              wcb[bs], lambda k, si=si: ha_at(k, si), 32)
                if oc < NCH - NSPILL:
                    def _vec(_e, oc=oc, a=a, pv=pv):
                        v = nc.vector
                        v.wait_ge(s_pe.h, pv)
                        dst = XR[:, oc * TOK + a: oc * TOK + a + 512]
                        s_ev.inc(v.scalar_tensor_tensor(dst, P.ps[6][:, :],
                                                        d["sb_s3"][:, oc:oc + 1], dst,
                                                        ALU.mult, ALU.add))
                    P.E("vector")(_vec)
                else:
                    j = oc - (NCH - NSPILL)

                    def _ld(_e, j=j, a=a, si=si, evg=P.bankg.get(("xrs", si))):
                        g = nc.gpsimd
                        if evg:
                            g.wait_ge(evg[0].h, evg[1])
                        g.dma_start(out=xrs[si][:, :],
                                    in_=spill.rearrange("(j p) n -> p j n", p=128)[
                                        :, j, a:a + 512]).then_inc(s_spg.h, 16)
                    P.E("gpsimd")(_ld)
                    s_spg.v += 16
                    spv = s_spg.v

                    def _vec(_e, oc=oc, si=si, pv=pv, spv=spv):
                        v = nc.vector
                        v.wait_ge(s_pe.h, pv)
                        v.wait_ge(s_spg.h, spv)
                        s_ev.inc(v.scalar_tensor_tensor(xrs[si], P.ps[6][:, :],
                                                        d["sb_s3"][:, oc:oc + 1], xrs[si],
                                                        ALU.mult, ALU.add))
                    P.E("vector")(_vec)
                    evv = s_ev.v

                    def _st(_e, j=j, a=a, si=si, evv=evv):
                        g = nc.gpsimd
                        g.wait_ge(s_ev.h, evv)
                        g.dma_start(out=spill.rearrange("(j p) n -> p j n", p=128)[
                            :, j, a:a + 512], in_=xrs[si]).then_inc(s_spg.h, 16)
                    P.E("gpsimd")(_st)
                    s_spg.v += 16
                    P.bankg[("xrs", si)] = (s_ev, s_ev.v)
                P.bankg[6] = (s_ev, s_ev.v)
            wg[("BC", bs)] = pv
        P.bar()

    def _reload(e):
        e.wait_ge(s_sp.h, s_sp.v)
        e.wait_ge(s_spg.h, s_spg.v)
        for j in range(NSPILL):
            c0 = NCH - NSPILL + j
            e.dma_start(out=XR[:, c0 * TOK:(c0 + 1) * TOK],
                        in_=spill.rearrange("(j p) n -> p j n", p=128)[:, j, :]
                        ).then_inc(s_sp.h, 16)
        e.wait_ge(s_sp.h, s_sp.v + 16 * NSPILL)
    P.E("sync")(_reload)
    s_sp.v += 16 * NSPILL
    P.bar()
    ar.reset(m0)


def emit_proj_headln(P, d, XL, wT, bias_c, hscale, hbias, eps, sink, tag):
    nc = P.nc
    ar = P.ar
    m0 = ar.mark()
    QF = [ar.take(F32, 512) for _ in range(2)]
    SQ2 = ar.take(F32, 512)
    ROWS = ar.take(F32, 1536)
    BCR = [ar.take(F32, 512) for _ in range(2)]
    BCB = [ar.take(F32, 512) for _ in range(2)]
    TT = ar.take(F32, 512)
    wb = [ar.take(BF16, 1024) for _ in range(2)]
    s_w = P.sem("ffw")
    s_pe = P.sem("pe")
    s_ev = P.sem("ev")
    s_sq = P.sem("pqsq")
    s_r = P.sem("pqr")
    s_bc = P.sem("pqbc")
    r_m = ROWS[0:2, 0:512]
    r_v = ROWS[0:2, 512:1024]
    r_t = ROWS[0:2, 1024:1536]

    for oc in range(NCH):
        bs = oc % 2

        def _dma(_e, oc=oc, bs=bs, wgv=P.bankg.get(("projw", bs))):
            g = nc.gpsimd
            if wgv:
                g.wait_ge(wgv[0].h, wgv[1])
            g.dma_start(out=sb3(wb[bs]), in_=dram_colblock(wT, oc)).then_inc(s_w.h, 16)
        P.E("gpsimd")(_dma)
        s_w.v += 16
        wv = s_w.v
        for ti in range(4):
            ts = slice(ti * 512, (ti + 1) * 512)
            par = (oc * 4 + ti) % 2
            pv = mm_group(P, s_pe, [(s_w, wv if ti == 0 else 0)], bs, wb[bs],
                          lambda k, ts=ts: XL[:, k * TOK + ts.start: k * TOK + ts.stop], 8)

            def _v2(_e, pv=pv, par=par, oc=oc, qg=P.bankg.get(("qf", par))):
                v = nc.vector
                v.wait_ge(s_pe.h, max(pv, qg[1]) if qg else pv)
                s_ev.inc(v.tensor_scalar(QF[par], P.ps[bs][:, :],
                                         bias_c[:, oc:oc + 1], None, ALU.add))
            P.E("vector")(_v2)
            P.bankg[bs] = (s_ev, s_ev.v)

            def _s3(_e, par=par, ev=s_ev.v):
                s = nc.scalar
                s.wait_ge(s_ev.h, ev)
                s_sq.inc(s.activation(SQ2, QF[par], AF.Square))
            P.E("scalar")(_s3)

            def _pe4(_e, par=par, sqv=s_sq.v, w4=P.bankg.get(4), w5=P.bankg.get(5)):
                t = nc.tensor
                t.wait_ge(s_sq.h, sqv)
                if w4:
                    t.wait_ge(w4[0].h, w4[1])
                if w5:
                    t.wait_ge(w5[0].h, w5[1])
                t.matmul(P.ps[4][0:2, :], P.sel64, QF[par], start=True, stop=True)
                s_pe.inc(t.matmul(P.ps[5][0:2, :], P.sel64, SQ2, start=True, stop=True))
            P.E("tensor")(_pe4)
            pv2 = s_pe.v
            P.bankg[("qf", par)] = (s_pe, pv2)

            def _v5(_e, pv2=pv2, eps=eps):
                v = nc.vector
                v.wait_ge(s_pe.h, pv2)
                v.tensor_scalar(r_m, P.ps[4][0:2, :], 1.0 / 64, None, ALU.mult)
                v.tensor_scalar(r_t, P.ps[5][0:2, :], 1.0 / 64, None, ALU.mult)
                v.tensor_tensor(r_v, r_m, r_m, ALU.mult)
                v.tensor_tensor(r_v, r_t, r_v, ALU.subtract)
                s_ev.inc(v.tensor_scalar(r_v, r_v, eps, None, ALU.add))
            P.E("vector")(_v5)
            ev2 = s_ev.v

            def _s6(_e, ev2=ev2):
                s = nc.scalar
                s.wait_ge(s_ev.h, ev2)
                s.activation(r_v, r_v, AF.Ln)
                s_r.inc(s.activation(r_v, r_v, AF.Exp, scale=-0.5))
            P.E("scalar")(_s6)

            def _v7(_e, rv=s_r.v):
                v = nc.vector
                v.wait_ge(s_r.h, rv)
                v.tensor_tensor(r_m, r_m, r_v, ALU.mult)
                s_ev.inc(v.tensor_scalar(r_m, r_m, -1.0, None, ALU.mult))
            P.E("vector")(_v7)
            ev3 = s_ev.v

            def _pe8(_e, ev3=ev3):
                t = nc.tensor
                t.wait_ge(s_ev.h, ev3)
                t.matmul(P.ps[4][:, :], P.sel64T, r_v, start=True, stop=True)
                s_pe.inc(t.matmul(P.ps[5][:, :], P.sel64T, r_m, start=True, stop=True))
            P.E("tensor")(_pe8)
            pv3 = s_pe.v

            def _s9(_e, par=par, pv3=pv3, bg=P.bankg.get(("bc", par))):
                s = nc.scalar
                s.wait_ge(s_pe.h, pv3)
                if bg:
                    s.wait_ge(bg[0].h, bg[1])
                s.copy(BCR[par], P.ps[4][:, :])
                s_bc.inc(s.copy(BCB[par], P.ps[5][:, :]))
            P.E("scalar")(_s9)
            P.bankg[4] = (s_bc, s_bc.v)
            P.bankg[5] = (s_bc, s_bc.v)

            dst = sink(oc, ti, "dst")
            dguard = sink(oc, ti, "guard")

            def _v10(_e, par=par, bcv=s_bc.v, dst=dst, dguard=dguard):
                v = nc.vector
                v.wait_ge(s_bc.h, bcv)
                if dguard is not None:
                    v.wait_ge(dguard[0].h, dguard[1])
                v.tensor_tensor(TT, QF[par], BCR[par], ALU.mult)
                v.tensor_tensor(TT, TT, BCB[par], ALU.add)
                s_ev.inc(v.tensor_scalar(dst, TT, hscale[:, 0:1], hbias[:, 0:1],
                                         ALU.mult, ALU.add))
            P.E("vector")(_v10)
            P.bankg[("bc", par)] = (s_ev, s_ev.v)
            sink(oc, ti, "post")
        P.bankg[("projw", bs)] = (s_pe, pv)
    ar.reset(m0)


def emit_l0_attn_a(P, d, XR, tag):
    nc = P.nc
    ar = P.ar
    m0 = ar.mark()
    XL = ar.take(BF16, NCH * TOK)
    xl_off = ar.last_off
    emit_ln(P, XR, XL, "ln")
    P.bar()

    QT = ar.take(BF16, NCH * TOK)
    s_st = P.sem("aast")
    s_ev = P.sem("ev")
    s_pe = P.sem("pe")

    KST = [ar.take(BF16, 512) for _ in range(2)]
    VST = [ar.take(BF16, 512) for _ in range(2)]
    VSTT = [ar.take(BF16, 512) for _ in range(2)]

    kst_guard = {}
    vst_guard = {}

    def q_sink(oc, ti, phase):
        if phase == "dst":
            return QT[:, oc * TOK + ti * 512: oc * TOK + ti * 512 + 512]
        return None

    def k_sink(oc, ti, phase):
        par = (oc * 4 + ti) % 2
        if phase == "dst":
            return KST[par]
        if phase == "guard":
            g = kst_guard.get(par)
            return (s_st, g) if g else None
        ev = s_ev.v

        def _dma(_e, oc=oc, ti=ti, par=par, ev=ev):
            e = nc.sync
            e.wait_ge(s_ev.h, ev)
            e.dma_start(out=d["k_loc"][oc * 128:(oc + 1) * 128,
                                       ti * 512:(ti + 1) * 512],
                        in_=KST[par]).then_inc(s_st.h, 16)
        P.E("sync")(_dma)
        s_st.v += 16
        kst_guard[par] = s_st.v
        return None

    def v_sink(oc, ti, phase):
        par = (oc * 4 + ti) % 2
        if phase == "dst":
            return VST[par]
        if phase == "guard":
            g = vst_guard.get(par)
            return (s_pe, g) if g else None
        ev = s_ev.v

        def _tr(_e, par=par, ev=ev, pg=P.bankg.get("psbf")):
            t = nc.tensor
            t.wait_ge(s_ev.h, ev)
            if pg:
                t.wait_ge(pg[0].h, pg[1])
            for i in range(4):
                inst = t.transpose(P.psbf[:, i * 128:(i + 1) * 128],
                                   VST[par][:, i * 128:(i + 1) * 128], P.ident_bf)
            s_pe.inc(inst)
        P.E("tensor")(_tr)
        pv = s_pe.v

        def _cp(_e, par=par, pv=pv, g=vst_guard.get(("tt", par), 0)):
            s = nc.scalar
            s.wait_ge(s_pe.h, pv)
            if g:
                s.wait_ge(s_st.h, g)
            s_ev.inc(s.copy(VSTT[par], P.psbf[:, 0:512]))
        P.E("scalar")(_cp)
        evc = s_ev.v
        P.bankg["psbf"] = (s_ev, evc)

        def _dma(_e, oc=oc, ti=ti, par=par, evc=evc):
            e = nc.sync
            e.wait_ge(s_ev.h, evc)
            dst = d["v_loc"].rearrange("(i p) n -> p i n", p=128)[
                :, 4 * ti: 4 * ti + 4, oc * 128:(oc + 1) * 128]
            e.dma_start(out=dst, in_=sb3(VSTT[par])).then_inc(s_st.h, 16)
        P.E("sync")(_dma)
        s_st.v += 16
        vst_guard[par] = s_pe.v
        vst_guard[("tt", par)] = s_st.v
        return None

    emit_proj_headln(P, d, XL, d["a_wqT"], d["sb_bq"], d["sb_qs"], d["sb_qb"],
                     1e-6, q_sink, tag + "q")
    emit_proj_headln(P, d, XL, d["a_wkT"], d["sb_bk"], d["sb_ks"], d["sb_kb"],
                     1e-6, k_sink, tag + "k")
    emit_proj_headln(P, d, XL, d["a_wvT"], d["sb_bv"], d["sb_vs"], d["sb_vb"],
                     1e-6, v_sink, tag + "v")
    P.bar()

    s_cck = P.sem(tag + "cck")
    s_ccv = P.sem(tag + "ccv")

    def _cc(_e):
        g = nc.gpsimd
        g.wait_ge(s_st.h, s_st.v)
        g.collective_compute("AllGather", ALU.bypass,
                             replica_groups=[[0, 1], [2, 3], [4, 5], [6, 7]],
                             ins=[d["k_loc"][:, :]], outs=[d["k_full"][:, :]]
                             ).then_inc(s_cck.h)
        g.collective_compute("AllGather", ALU.bypass,
                             replica_groups=[[0, 1], [2, 3], [4, 5], [6, 7]],
                             ins=[d["v_loc"][:, :]], outs=[d["v_full"][:, :]]
                             ).then_inc(s_ccv.h)
        g.wait_ge(s_cck.h, 1)
        g.wait_ge(s_ccv.h, 1)
    P.E("gpsimd")(_cc)
    s_cck.v = 1
    s_ccv.v = 1
    P.bar()

    OT = ar.view(xl_off, BF16, NCH * TOK)
    kbuf = [ar.take(BF16, 4096) for _ in range(2)]
    vbuf = [ar.take(BF16, 32 * 130) for _ in range(2)]
    PA = [ar.take(BF16, 512) for _ in range(2)]
    PB = [ar.take(BF16, 512) for _ in range(2)]
    SCR = ar.take(F32, 2048)
    NRM = ar.take(F32, 64)
    BCN = ar.take(F32, 512)
    s_kv = P.sem(tag + "kv")
    s_x = P.sem(tag + "x")
    s_pv = P.sem(tag + "pv")

    def _ones(_e):
        v = nc.vector
        for b in range(2):
            v.memset(vbuf[b].rearrange("p (a n) -> p a n", n=130)[:, :, 64:65], 1.0)
            v.memset(vbuf[b].rearrange("p (a n) -> p a n", n=130)[:, :, 129:130], 1.0)
        v.memset(NRM[:, :], 0.0)
        s_ev.inc(v.memset(SCR[:, :], 0.0))
    P.E("vector")(_ones)
    P.bar()

    def load_kv(hp, buf, guard=0):
        def _d(_e, hp=hp, buf=buf, guard=guard):
            e = nc.scalar
            if guard:
                e.wait_ge(s_pe.h, guard)
            e.dma_start(out=kbuf[buf][:, 0:2048],
                        in_=d["k_full"][hp * 128:(hp + 1) * 128, :]).then_inc(s_kv.h, 16)
            e.dma_start(out=kbuf[buf][:, 2048:4096],
                        in_=d["k_full"][1024 + hp * 128: 1024 + (hp + 1) * 128, :]
                        ).then_inc(s_kv.h, 16)
            for h in range(2):
                src = d["v_full"].rearrange("(a p) n -> p a n", p=128)[
                    :, :, hp * 128 + h * 64: hp * 128 + h * 64 + 64]
                dst = vbuf[buf].rearrange("p (a n) -> p a n", n=130)[
                    :, :, h * 65: h * 65 + 64]
                e.dma_start(out=dst, in_=src).then_inc(s_kv.h, 16)
        P.E("scalar")(_d)
        s_kv.v += 64
        return s_kv.v

    kvv = load_kv(0, 0)
    buf_last_use = {0: 0, 1: 0}
    for hp in range(NCH):
        buf = hp % 2
        if hp + 1 < NCH:
            nxt = load_kv(hp + 1, 1 - buf, buf_last_use[1 - buf])
        def _mb_sq(_e, hp=hp, kvv=kvv, ev=s_ev.v):
            s = nc.scalar
            s.wait_ge(s_kv.h, kvv)
            s.wait_ge(s_ev.h, ev)
            s_x.inc(s.activation(SCR[:, 0:2048], QT[:, hp * TOK: (hp + 1) * TOK], AF.Square))
        P.E("scalar")(_mb_sq)
        xv = s_x.v

        def _mb_pe(_e, xv=xv, w4=P.bankg.get(4)):
            t = nc.tensor
            t.wait_ge(s_x.h, xv)
            if w4:
                t.wait_ge(w4[0].h, w4[1])
            for i in range(4):
                inst = t.matmul(P.ps[4][0:2, :], P.sel64, SCR[:, i * 512:(i + 1) * 512],
                                start=(i == 0), stop=(i == 3))
            s_pe.inc(inst)
        P.E("tensor")(_mb_pe)
        pv = s_pe.v

        def _mb_v(_e, pv=pv):
            v = nc.vector
            v.wait_ge(s_pe.h, pv)
            s_ev.inc(v.reduce_max(NRM[0:2, 0:1], P.ps[4][0:2, :], axis=AX))
        P.E("vector")(_mb_v)
        P.bankg[4] = (s_ev, s_ev.v)

        for kh in range(2):
            def _mb_sqk(_e, buf=buf, kh=kh, pv_prev=s_pe.v):
                s = nc.scalar
                s.wait_ge(s_pe.h, pv_prev)
                s_x.inc(s.activation(SCR[:, 0:2048],
                                     kbuf[buf][:, kh * 2048:(kh + 1) * 2048], AF.Square))
            P.E("scalar")(_mb_sqk)
            xv2 = s_x.v

            def _mb_pek(_e, xv2=xv2, w4=P.bankg.get(4)):
                t = nc.tensor
                t.wait_ge(s_x.h, xv2)
                if w4:
                    t.wait_ge(w4[0].h, w4[1])
                for i in range(4):
                    inst = t.matmul(P.ps[4][0:2, :], P.sel64, SCR[:, i * 512:(i + 1) * 512],
                                    start=(i == 0), stop=(i == 3))
                s_pe.inc(inst)
            P.E("tensor")(_mb_pek)
            pv = s_pe.v

            def _mb_vk(_e, pv=pv, kh=kh):
                v = nc.vector
                v.wait_ge(s_pe.h, pv)
                if kh == 0:
                    s_ev.inc(v.reduce_max(NRM[0:2, 1:2], P.ps[4][0:2, :], axis=AX))
                else:
                    v.reduce_max(NRM[0:2, 3:4], P.ps[4][0:2, :], axis=AX)
                    s_ev.inc(v.tensor_tensor(NRM[0:2, 1:2], NRM[0:2, 1:2],
                                             NRM[0:2, 3:4], ALU.max))
            P.E("vector")(_mb_vk)
            P.bankg[4] = (s_ev, s_ev.v)

        def _mb_fin(_e, ev=s_ev.v):
            s = nc.scalar
            s.wait_ge(s_ev.h, ev)
            s.activation(NRM[0:2, 0:2], NRM[0:2, 0:2], AF.Sqrt)
            s_x.inc(s.copy(NRM[0:2, 2:3], NRM[0:2, 0:1]))
        P.E("scalar")(_mb_fin)
        xv3 = s_x.v

        def _mb_v2(_e, xv3=xv3):
            v = nc.vector
            v.wait_ge(s_x.h, xv3)
            v.tensor_tensor(NRM[0:2, 2:3], NRM[0:2, 2:3], NRM[0:2, 1:2], ALU.mult)
            s_ev.inc(v.tensor_scalar(NRM[0:2, 2:3], NRM[0:2, 2:3], -0.125, None, ALU.mult))
        P.E("vector")(_mb_v2)
        evm = s_ev.v

        def _mb_pe2(_e, evm=evm, w4=P.bankg.get(4)):
            t = nc.tensor
            t.wait_ge(s_ev.h, evm)
            if w4:
                t.wait_ge(w4[0].h, w4[1])
            t.matmul(P.ps[4][0:1, 0:2], NRM[0:2, 2:3], P.I2, start=True, stop=True)
            s_pe.inc(t.matmul(P.ps[4][0:1, 2:4], NRM[0:2, 2:3], P.I2, start=True, stop=True))
        P.E("tensor")(_mb_pe2)
        pv = s_pe.v

        def _mb_cp(_e, pv=pv):
            s = nc.scalar
            s.wait_ge(s_pe.h, pv)
            s_x.inc(s.copy(NRM[0:1, 4:6], P.ps[4][0:1, 0:2]))
        P.E("scalar")(_mb_cp)
        xv4 = s_x.v
        P.bankg[4] = (s_x, s_x.v)

        def _mb_pe3(_e, xv4=xv4, w4=P.bankg.get(4)):
            t = nc.tensor
            t.wait_ge(s_x.h, xv4)
            if w4:
                t.wait_ge(w4[0].h, w4[1])
            t.matmul(P.ps[4][:, 0:1], P.ones_row_f, NRM[0:1, 4:5], start=True, stop=True)
            s_pe.inc(t.matmul(P.ps[4][:, 1:2], P.ones_row_f, NRM[0:1, 5:6],
                              start=True, stop=True))
        P.E("tensor")(_mb_pe3)
        pv = s_pe.v

        def _mb_cp2(_e, pv=pv):
            s = nc.scalar
            s.wait_ge(s_pe.h, pv)
            s_x.inc(s.copy(NRM[:, 8:10], P.ps[4][:, 0:2]))
        P.E("scalar")(_mb_cp2)
        P.bankg[4] = (s_x, s_x.v)
        biasv = s_x.v

        for qt in range(4):
            qs = slice(qt * 512, (qt + 1) * 512)
            base_pe = s_pe.v
            base_x = s_x.v
            base_pv = s_pv.v
            for kc in range(32):
                bA = [0, 1][kc % 2]
                bB = [5, 6][kc % 2]
                par = kc % 2

                def _smm(_e, hp=hp, buf=buf, kc=kc, qs=qs, bA=bA, bB=bB,
                         kvv=kvv, bx=base_x, biasv=biasv):
                    t = nc.tensor
                    if kc == 0:
                        t.wait_ge(s_kv.h, kvv)
                        t.wait_ge(s_x.h, biasv)
                    if kc >= 2:
                        t.wait_ge(s_x.h, bx + 2 * (kc - 1))
                    t.matmul(P.ps[bA][:, :], kbuf[buf][0:64, kc * 128:(kc + 1) * 128],
                             QT[0:64, hp * TOK + qs.start: hp * TOK + qs.stop],
                             start=True, stop=True, tile_position=(0, 0))
                    s_pe.inc(t.matmul(P.ps[bB][:, :],
                                      kbuf[buf][64:128, kc * 128:(kc + 1) * 128],
                                      QT[64:128, hp * TOK + qs.start: hp * TOK + qs.stop],
                                      start=True, stop=True, tile_position=(64, 0)))
                P.E("tensor")(_smm)
                pev = s_pe.v

                if kc % 2 == 0:
                    continue

                def _exp2(_e, kc=kc, pev=pev, bpv=base_pv, biasv=biasv):
                    s = nc.scalar
                    s.wait_ge(s_pe.h, pev)
                    if kc == 1:
                        s.wait_ge(s_x.h, biasv)
                    if kc >= 3:
                        s.wait_ge(s_pv.h, bpv + (kc - 1))
                    for jj, pr in ((kc - 1, 0), (kc, 1)):
                        s.activation(PA[pr], P.ps[[0, 1][jj % 2]][:, :], AF.Exp,
                                     bias=NRM[:, 8:9], scale=0.125).then_inc(s_x.h, 1)
                        s_x.v += 1
                        s.activation(PB[pr], P.ps[[5, 6][jj % 2]][:, :], AF.Exp,
                                     bias=NRM[:, 9:10], scale=0.125).then_inc(s_x.h, 1)
                        s_x.v += 1
                P.E("scalar")(_exp2)

                for jj in (kc - 1, kc):
                    xev = s_x.v - (2 if jj == kc - 1 else 0)

                    def _pv(_e, kc=jj, par=jj % 2, buf=buf, xev=xev,
                            g2=P.bankg.get(2), g3=P.bankg.get(3)):
                        t = nc.tensor
                        t.wait_ge(s_x.h, xev)
                        if kc == 0 and g2:
                            t.wait_ge(g2[0].h, g2[1])
                        if kc == 0 and g3:
                            t.wait_ge(g3[0].h, g3[1])
                        t.matmul(P.ps[2][0:65, :], vbuf[buf][:, kc * 130: kc * 130 + 65],
                                 PA[par], start=(kc == 0), stop=(kc == 31))
                        inst = t.matmul(P.ps[3][0:65, :],
                                        vbuf[buf][:, kc * 130 + 65: kc * 130 + 130],
                                        PB[par], start=(kc == 0), stop=(kc == 31))
                        s_pv.inc(inst)
                    P.E("tensor")(_pv)

            pvv = s_pv.v

            def _rec(_e, pvv=pvv):
                v = nc.vector

                def edge(inst):
                    s_ev.inc(inst)
                    v.wait_ge(s_ev.h, s_ev.v)
                v.wait_ge(s_pv.h, pvv)
                edge(v.tensor_scalar(BCN[0:1, :], P.ps[2][64:65, :], 1e-30, None, ALU.add))
                v.reciprocal(BCN[0:1, :], BCN[0:1, :])
                edge(v.tensor_scalar(SCR[0:1, 0:512], P.ps[3][64:65, :], 1e-30, None, ALU.add))
                s_ev.inc(v.reciprocal(SCR[0:1, 0:512], SCR[0:1, 0:512]))
            P.E("vector")(_rec)
            evr = s_ev.v

            def _bc(_e, evr=evr, w4=P.bankg.get(4)):
                t = nc.tensor
                t.wait_ge(s_ev.h, evr)
                if w4:
                    t.wait_ge(w4[0].h, w4[1])
                t.matmul(P.ps[4][0:64, :], P.ones_row_f[0:1, 0:64], BCN[0:1, :],
                         start=True, stop=True, tile_position=(0, 0))
                s_pe.inc(t.matmul(P.ps[4][64:128, :], P.ones_row_f[0:1, 0:64],
                                  SCR[0:1, 0:512], start=True, stop=True,
                                  tile_position=(0, 64)))
            P.E("tensor")(_bc)
            pvb = s_pe.v

            def _cpb(_e, pvb=pvb):
                s = nc.scalar
                s.wait_ge(s_pe.h, pvb)
                s_x.inc(s.copy(BCN[:, :], P.ps[4][:, :]))
            P.E("scalar")(_cpb)
            xvb = s_x.v
            P.bankg[4] = (s_x, s_x.v)

            def _nrm(_e, hp=hp, qs=qs, xvb=xvb):
                v = nc.vector
                v.wait_ge(s_x.h, xvb)
                dst = OT[:, hp * TOK + qs.start: hp * TOK + qs.stop]
                v.tensor_tensor(dst[0:64, :], P.ps[2][0:64, :], BCN[0:64, :], ALU.mult)
                s_ev.inc(v.tensor_tensor(dst[64:128, :], P.ps[3][0:64, :],
                                         BCN[64:128, :], ALU.mult))
            P.E("vector")(_nrm)
            P.bankg[2] = (s_ev, s_ev.v)
            P.bankg[3] = (s_ev, s_ev.v)
        buf_last_use[buf] = s_pe.v
        kvv = nxt if hp + 1 < NCH else kvv
    P.bar()

    wb2 = [ar.take(BF16, 1024) for _ in range(2)]
    s_w = P.sem("ffw")
    wg = {}
    for oc in range(NCH):
        bs = oc % 2

        def _dma(_e, oc=oc, bs=bs, wgv=wg.get(bs, 0)):
            g = nc.gpsimd
            if wgv:
                g.wait_ge(s_pe.h, wgv)
            g.dma_start(out=sb3(wb2[bs]), in_=dram_colblock(d["a_woT"], oc)
                        ).then_inc(s_w.h, 16)
        P.E("gpsimd")(_dma)
        s_w.v += 16
        wv = s_w.v
        for ti in range(4):
            ts = slice(ti * 512, (ti + 1) * 512)
            pv = mm_group(P, s_pe, [(s_w, wv if ti == 0 else 0)], 2 + (oc % 2),
                          wb2[bs],
                          lambda k, ts=ts: OT[:, k * TOK + ts.start: k * TOK + ts.stop], 8)

            def _vec(_e, oc=oc, ts=ts, pv=pv, bank=2 + (oc % 2)):
                v = nc.vector
                v.wait_ge(s_pe.h, pv)
                dst = XR[:, oc * TOK + ts.start: oc * TOK + ts.stop]
                s_ev.inc(v.scalar_tensor_tensor(dst, P.ps[bank][:, :],
                                                d["sb_bo"][:, oc:oc + 1], dst,
                                                ALU.add, ALU.add))
            P.E("vector")(_vec)
            P.bankg[2 + (oc % 2)] = (s_ev, s_ev.v)
        wg[bs] = pv
    ar.reset(m0)


def _sl_l0_attn_a(P, d, XR, tag):
    ar = P.ar
    m0 = ar.mark()
    names = [("sb_bq", "a_bq", 8), ("sb_bk", "a_bk", 8), ("sb_bv", "a_bv", 8),
             ("sb_bo", "a_bo", 8), ("sb_qs", "a_qs", 1), ("sb_qb", "a_qb", 1),
             ("sb_ks", "a_ks", 1), ("sb_kb", "a_kb", 1), ("sb_vs", "a_vs", 1),
             ("sb_vb", "a_vb", 1)]
    for nm, src_nm, n in names:
        d[nm] = ar.take(F32, n)
    s_ld = P.sem("sbld")

    def _l(e):
        cnt = 0
        for nm, src_nm, n in names:
            e.dma_start(out=d[nm][:, :], in_=d[src_nm][:, :]).then_inc(s_ld.h, 16)
            cnt += 16
        e.wait_ge(s_ld.h, s_ld.v + cnt)
    P.E("sync")(_l)
    s_ld.v += 160
    P.bar()
    emit_l0_attn_a(P, d, XR, tag)
    ar.reset(m0)


def decl_l0_attn_a(nc):
    d = {}
    for nm in ["a_wqT", "a_wkT", "a_wvT", "a_woT"]:
        d[nm] = nc.dram_tensor(nm, [DM, DM], F32, kind="ExternalInput").ap()
    for nm in ["a_bq", "a_bk", "a_bv", "a_bo"]:
        d[nm] = nc.dram_tensor(nm, [128, 8], F32, kind="ExternalInput").ap()
    for nm in ["a_qs", "a_qb", "a_ks", "a_kb", "a_vs", "a_vb"]:
        d[nm] = nc.dram_tensor(nm, [128, 1], F32, kind="ExternalInput").ap()
    d["k_loc"] = nc.dram_tensor("k_loc", [DM, TOK], BF16).ap()
    d["k_full"] = nc.dram_tensor("k_full", [2 * DM, TOK], BF16).ap()
    d["v_loc"] = nc.dram_tensor("v_loc", [TOK, DM], BF16).ap()
    d["v_full"] = nc.dram_tensor("v_full", [2 * TOK, DM], BF16).ap()
    return d


def host_l0_attn_a_inputs(I):
    f = np.float32

    def tile64(v):
        return np.ascontiguousarray(np.tile(np.asarray(v, f), 2)[:, None])
    return {
        "a_wqT": np.ascontiguousarray(np.asarray(I["L0_attn_a_Wq"], f)),
        "a_wkT": np.ascontiguousarray(np.asarray(I["L0_attn_a_Wk"], f)),
        "a_wvT": np.ascontiguousarray(np.asarray(I["L0_attn_a_Wv"], f)),
        "a_woT": np.ascontiguousarray(np.asarray(I["L0_attn_a_Wo"], f)),
        "a_bq": pack_partition_vec(I["L0_attn_a_bq"]),
        "a_bk": pack_partition_vec(I["L0_attn_a_bk"]),
        "a_bv": pack_partition_vec(I["L0_attn_a_bv"]),
        "a_bo": pack_partition_vec(I["L0_attn_a_bo"]),
        "a_qs": tile64(I["L0_attn_a_q_scale"]), "a_qb": tile64(I["L0_attn_a_q_bias"]),
        "a_ks": tile64(I["L0_attn_a_k_scale"]), "a_kb": tile64(I["L0_attn_a_k_bias"]),
        "a_vs": tile64(I["L0_attn_a_v_scale"]), "a_vb": tile64(I["L0_attn_a_v_bias"]),
    }


def emit_proj_plain(P, d, XL, wT, col0, sink, nk=8):
    nc = P.nc
    ar = P.ar
    m0 = ar.mark()
    wb = [ar.take(BF16, 1024) for _ in range(2)]
    s_w = P.sem("ffw")
    s_pe = P.sem("pe")
    s_ev = P.sem("ev")
    for oc in range(NCH):
        bs = oc % 2

        def _dma(_e, oc=oc, bs=bs, wgv=P.bankg.get(("projw", bs))):
            g = nc.gpsimd
            if wgv:
                g.wait_ge(wgv[0].h, wgv[1])
            g.dma_start(out=sb3(wb[bs]), in_=dram_colblock(wT, oc, col0=col0)
                        ).then_inc(s_w.h, 16)
        P.E("gpsimd")(_dma)
        s_w.v += 16
        wv = s_w.v
        for ti in range(4):
            ts = slice(ti * 512, (ti + 1) * 512)
            pv = mm_group(P, s_pe, [(s_w, wv if ti == 0 else 0)], bs, wb[bs],
                          lambda k, ts=ts: XL[:, k * TOK + ts.start: k * TOK + ts.stop], nk)
            dst = sink(oc, ti, "dst")
            dguard = sink(oc, ti, "guard")

            def _cp(_e, pv=pv, dst=dst, dguard=dguard):
                s = nc.scalar
                s.wait_ge(s_pe.h, pv)
                if dguard is not None:
                    s.wait_ge(dguard[0].h, dguard[1])
                s_ev.inc(s.copy(dst, P.ps[oc % 2][:, :]))
            P.E("scalar")(_cp)
            P.bankg[oc % 2] = (s_ev, s_ev.v)
            sink(oc, ti, "post")
        P.bankg[("projw", bs)] = (s_pe, pv)
    ar.reset(m0)


def emit_l1_attn_b(P, d, XR, tag):
    nc = P.nc
    ar = P.ar
    m0 = ar.mark()
    XL = ar.take(BF16, NCH * TOK)
    xl_off = ar.last_off
    emit_ln(P, XR, XL, "ln")
    P.bar()

    QT = ar.take(BF16, NCH * TOK)
    s_st = P.sem("aast")
    s_ev = P.sem("ev")
    s_pe = P.sem("pe")
    s_x = P.sem(tag + "x")
    s_pv = P.sem(tag + "pv")

    KST = [ar.take(BF16, 512) for _ in range(2)]
    VST = [ar.take(BF16, 512) for _ in range(2)]
    VSTT = [ar.take(BF16, 512) for _ in range(2)]
    kst_guard = {}
    vst_guard = {}

    def q_sink(oc, ti, phase):
        if phase == "dst":
            return QT[:, oc * TOK + ti * 512: oc * TOK + ti * 512 + 512]
        return None

    def k_sink(oc, ti, phase):
        par = (oc * 4 + ti) % 2
        if phase == "dst":
            return KST[par]
        if phase == "guard":
            g = kst_guard.get(par)
            return (s_st, g) if g else None
        ev = s_ev.v

        def _dma(_e, oc=oc, ti=ti, par=par, ev=ev):
            e = nc.sync
            e.wait_ge(s_ev.h, ev)
            e.dma_start(out=d["k_loc"][oc * 128:(oc + 1) * 128,
                                       ti * 512:(ti + 1) * 512],
                        in_=KST[par]).then_inc(s_st.h, 16)
        P.E("sync")(_dma)
        s_st.v += 16
        kst_guard[par] = s_st.v
        return None

    def v_sink(oc, ti, phase):
        par = (oc * 4 + ti) % 2
        if phase == "dst":
            return VST[par]
        if phase == "guard":
            g = vst_guard.get(par)
            return (s_pe, g) if g else None
        ev = s_ev.v

        def _tr(_e, par=par, ev=ev, pg=P.bankg.get("psbf")):
            t = nc.tensor
            t.wait_ge(s_ev.h, ev)
            if pg:
                t.wait_ge(pg[0].h, pg[1])
            for i in range(4):
                inst = t.transpose(P.psbf[:, i * 128:(i + 1) * 128],
                                   VST[par][:, i * 128:(i + 1) * 128], P.ident_bf)
            s_pe.inc(inst)
        P.E("tensor")(_tr)
        pv = s_pe.v

        def _cp(_e, par=par, pv=pv, g=vst_guard.get(("tt", par), 0)):
            s = nc.scalar
            s.wait_ge(s_pe.h, pv)
            if g:
                s.wait_ge(s_st.h, g)
            s_ev.inc(s.copy(VSTT[par], P.psbf[:, 0:512]))
        P.E("scalar")(_cp)
        evc = s_ev.v
        P.bankg["psbf"] = (s_ev, evc)

        def _dma(_e, oc=oc, ti=ti, par=par, evc=evc):
            e = nc.sync
            e.wait_ge(s_ev.h, evc)
            dst = d["v_loc"].rearrange("(i p) n -> p i n", p=128)[
                :, 4 * ti: 4 * ti + 4, oc * 128:(oc + 1) * 128]
            e.dma_start(out=dst, in_=sb3(VSTT[par])).then_inc(s_st.h, 16)
        P.E("sync")(_dma)
        s_st.v += 16
        vst_guard[par] = s_pe.v
        vst_guard[("tt", par)] = s_st.v
        return None

    emit_proj_plain(P, d, XL, d["b_wqkvT"], 0, q_sink)
    emit_proj_plain(P, d, XL, d["b_wqkvT"], 1024, k_sink)
    emit_proj_plain(P, d, XL, d["b_wqkvT"], 2048, v_sink)
    P.bar()

    g0 = ar.mark()
    grows = ar.take(F32, 2056)
    s_g = P.sem(tag + "g")

    def _gd(_e):
        e = nc.sync
        cyc = [5, 6, 7, 8]
        for h in range(16):
            p = cyc[h % 4]
            reps = (TOK + p - 1) // p
            src_ap = bass.AP(d["bgrot"].tensor, h * 16, [[1, 1], [0, reps], [1, p]])
            dst_ap = bass.AP(grows.tensor, grows.offset + h * grows.tensor.shape[1]
                             if False else 0, [[1, 1], [1, reps * p]])
            e.dma_start(out=grows[h:h + 1, 0:reps * p].rearrange("a (r q) -> a r q", q=p),
                        in_=d["bgrot"][h:h + 1, 0:p].rearrange("a q -> a q")[
                            0:1, :].broadcast_to((1, reps, p)) if False else
                        bass.AP(d["bgrot"].tensor, h * 16, [[1, 1], [0, reps], [1, p]])
                        ).then_inc(s_g.h, 16)
    P.E("sync")(_gd)
    s_g.v += 16 * 16

    def _gs(_e):
        s = nc.scalar
        s.wait_ge(s_g.h, s_g.v)
        s_x.inc(s.activation(grows[0:16, 0:TOK], grows[0:16, 0:TOK], AF.Sigmoid))
    P.E("scalar")(_gs)
    gsv = s_x.v

    for hp in range(NCH):
        for ti in range(4):
            ts = slice(ti * 512, (ti + 1) * 512)

            def _gb(_e, hp=hp, ts=ts, gsv=gsv, w4=P.bankg.get(4)):
                t = nc.tensor
                t.wait_ge(s_x.h, gsv)
                if w4:
                    t.wait_ge(w4[0].h, w4[1])
                s_pe.inc(t.matmul(P.ps[4][:, :], P.SEL16[0:16, hp * 128:(hp + 1) * 128],
                                  grows[0:16, ts], start=True, stop=True))
            P.E("tensor")(_gb)
            pv = s_pe.v

            def _gm(_e, hp=hp, ts=ts, pv=pv):
                v = nc.vector
                v.wait_ge(s_pe.h, pv)
                dst = QT[:, hp * TOK + ts.start: hp * TOK + ts.stop]
                s_ev.inc(v.tensor_tensor(dst, P.ps[4][:, :], dst, ALU.mult))
            P.E("vector")(_gm)
            P.bankg[4] = (s_ev, s_ev.v)
    P.bar()
    ar.reset(g0)

    s_cck = P.sem(tag + "cck")
    s_ccv = P.sem(tag + "ccv")

    def _cc(_e):
        g = nc.gpsimd
        g.wait_ge(s_st.h, s_st.v)
        g.collective_compute("AllGather", ALU.bypass,
                             replica_groups=[[0, 1], [2, 3], [4, 5], [6, 7]],
                             ins=[d["k_loc"][:, :]], outs=[d["k_full"][:, :]]
                             ).then_inc(s_cck.h)
        g.collective_compute("AllGather", ALU.bypass,
                             replica_groups=[[0, 1], [2, 3], [4, 5], [6, 7]],
                             ins=[d["v_loc"][:, :]], outs=[d["v_full"][:, :]]
                             ).then_inc(s_ccv.h)
        g.wait_ge(s_cck.h, 1)
        g.wait_ge(s_ccv.h, 1)
    P.E("gpsimd")(_cc)
    s_cck.v = 1
    s_ccv.v = 1
    P.bar()

    OT = ar.view(xl_off, BF16, NCH * TOK)
    kbuf = [ar.take(BF16, 4096) for _ in range(2)]
    vbuf = [ar.take(BF16, 32 * 130) for _ in range(2)]
    PA = [ar.take(BF16, 512) for _ in range(2)]
    PB = [ar.take(BF16, 512) for _ in range(2)]
    MSK = ar.take(F32, 512)
    SCR = ar.take(F32, 2048)
    NRM = ar.take(F32, 64)
    BCN = ar.take(F32, 512)
    s_kv = P.sem(tag + "kv")

    def _ones(_e):
        v = nc.vector
        for b in range(2):
            v.memset(vbuf[b].rearrange("p (a n) -> p a n", n=130)[:, :, 64:65], 1.0)
            v.memset(vbuf[b].rearrange("p (a n) -> p a n", n=130)[:, :, 129:130], 1.0)
        v.memset(NRM[:, :], 0.0)
        s_ev.inc(v.memset(SCR[:, :], 0.0))
    P.E("vector")(_ones)
    P.bar()

    def load_kv(hp, buf, guard=0):
        def _d(_e, hp=hp, buf=buf, guard=guard):
            e = nc.scalar
            if guard:
                e.wait_ge(s_pe.h, guard)
            e.dma_start(out=kbuf[buf][:, 0:2048],
                        in_=d["k_full"][hp * 128:(hp + 1) * 128, :]).then_inc(s_kv.h, 16)
            e.dma_start(out=kbuf[buf][:, 2048:4096],
                        in_=d["k_full"][1024 + hp * 128: 1024 + (hp + 1) * 128, :]
                        ).then_inc(s_kv.h, 16)
            for h in range(2):
                src2 = d["v_full"].rearrange("(a p) n -> p a n", p=128)[
                    :, :, hp * 128 + h * 64: hp * 128 + h * 64 + 64]
                dst2 = vbuf[buf].rearrange("p (a n) -> p a n", n=130)[
                    :, :, h * 65: h * 65 + 64]
                e.dma_start(out=dst2, in_=src2).then_inc(s_kv.h, 16)
        P.E("scalar")(_d)
        s_kv.v += 64
        return s_kv.v

    kvv = load_kv(0, 0)
    buf_last_use = {0: 0, 1: 0}
    for hp in range(NCH):
        buf = hp % 2
        if hp + 1 < NCH:
            nxt = load_kv(hp + 1, 1 - buf, buf_last_use[1 - buf])
        def _mb_sq(_e, hp=hp, kvv=kvv, ev=s_ev.v):
            s = nc.scalar
            s.wait_ge(s_kv.h, kvv)
            s.wait_ge(s_ev.h, ev)
            s_x.inc(s.activation(SCR[:, 0:2048], QT[:, hp * TOK: (hp + 1) * TOK], AF.Square))
        P.E("scalar")(_mb_sq)
        xv = s_x.v

        def _mb_pe(_e, xv=xv, w4=P.bankg.get(4)):
            t = nc.tensor
            t.wait_ge(s_x.h, xv)
            if w4:
                t.wait_ge(w4[0].h, w4[1])
            for i in range(4):
                inst = t.matmul(P.ps[4][0:2, :], P.sel64, SCR[:, i * 512:(i + 1) * 512],
                                start=(i == 0), stop=(i == 3))
            s_pe.inc(inst)
        P.E("tensor")(_mb_pe)
        pv = s_pe.v

        def _mb_v(_e, pv=pv):
            v = nc.vector
            v.wait_ge(s_pe.h, pv)
            s_ev.inc(v.reduce_max(NRM[0:2, 0:1], P.ps[4][0:2, :], axis=AX))
        P.E("vector")(_mb_v)
        P.bankg[4] = (s_ev, s_ev.v)
        for kh in range(2):
            def _mb_sqk(_e, buf=buf, kh=kh, pv_prev=s_pe.v):
                s = nc.scalar
                s.wait_ge(s_pe.h, pv_prev)
                s_x.inc(s.activation(SCR[:, 0:2048],
                                     kbuf[buf][:, kh * 2048:(kh + 1) * 2048], AF.Square))
            P.E("scalar")(_mb_sqk)
            xv2 = s_x.v

            def _mb_pek(_e, xv2=xv2, w4=P.bankg.get(4)):
                t = nc.tensor
                t.wait_ge(s_x.h, xv2)
                if w4:
                    t.wait_ge(w4[0].h, w4[1])
                for i in range(4):
                    inst = t.matmul(P.ps[4][0:2, :], P.sel64, SCR[:, i * 512:(i + 1) * 512],
                                    start=(i == 0), stop=(i == 3))
                s_pe.inc(inst)
            P.E("tensor")(_mb_pek)
            pv = s_pe.v

            def _mb_vk(_e, pv=pv, kh=kh):
                v = nc.vector
                v.wait_ge(s_pe.h, pv)
                if kh == 0:
                    s_ev.inc(v.reduce_max(NRM[0:2, 1:2], P.ps[4][0:2, :], axis=AX))
                else:
                    v.reduce_max(NRM[0:2, 3:4], P.ps[4][0:2, :], axis=AX)
                    s_ev.inc(v.tensor_tensor(NRM[0:2, 1:2], NRM[0:2, 1:2],
                                             NRM[0:2, 3:4], ALU.max))
            P.E("vector")(_mb_vk)
            P.bankg[4] = (s_ev, s_ev.v)

        def _mb_fin(_e, ev=s_ev.v):
            s = nc.scalar
            s.wait_ge(s_ev.h, ev)
            s.activation(NRM[0:2, 0:2], NRM[0:2, 0:2], AF.Sqrt)
            s_x.inc(s.copy(NRM[0:2, 2:3], NRM[0:2, 0:1]))
        P.E("scalar")(_mb_fin)
        xv3 = s_x.v

        def _mb_v2(_e, xv3=xv3):
            v = nc.vector
            v.wait_ge(s_x.h, xv3)
            v.tensor_tensor(NRM[0:2, 2:3], NRM[0:2, 2:3], NRM[0:2, 1:2], ALU.mult)
            s_ev.inc(v.tensor_scalar(NRM[0:2, 2:3], NRM[0:2, 2:3], -0.125, None, ALU.mult))
        P.E("vector")(_mb_v2)
        evm = s_ev.v

        def _mb_pe2(_e, evm=evm, w4=P.bankg.get(4)):
            t = nc.tensor
            t.wait_ge(s_ev.h, evm)
            if w4:
                t.wait_ge(w4[0].h, w4[1])
            s_pe.inc(t.matmul(P.ps[4][0:1, 0:2], NRM[0:2, 2:3], P.I2, start=True, stop=True))
        P.E("tensor")(_mb_pe2)
        pv = s_pe.v

        def _mb_cp(_e, pv=pv):
            s = nc.scalar
            s.wait_ge(s_pe.h, pv)
            s_x.inc(s.copy(NRM[0:1, 4:6], P.ps[4][0:1, 0:2]))
        P.E("scalar")(_mb_cp)
        xv4 = s_x.v
        P.bankg[4] = (s_x, s_x.v)

        def _mb_pe3(_e, xv4=xv4, w4=P.bankg.get(4)):
            t = nc.tensor
            t.wait_ge(s_x.h, xv4)
            if w4:
                t.wait_ge(w4[0].h, w4[1])
            t.matmul(P.ps[4][:, 0:1], P.ones_row_f, NRM[0:1, 4:5], start=True, stop=True)
            s_pe.inc(t.matmul(P.ps[4][:, 1:2], P.ones_row_f, NRM[0:1, 5:6],
                              start=True, stop=True))
        P.E("tensor")(_mb_pe3)
        pv = s_pe.v

        def _mb_cp2(_e, pv=pv):
            s = nc.scalar
            s.wait_ge(s_pe.h, pv)
            s_x.inc(s.copy(NRM[:, 8:10], P.ps[4][:, 0:2]))
        P.E("scalar")(_mb_cp2)
        P.bankg[4] = (s_x, s_x.v)
        biasv = s_x.v

        for qt in range(4):
            qs = slice(qt * 512, (qt + 1) * 512)
            base_pv = s_pv.v
            nkc = 20 + 4 * qt
            x_hist = {}
            for kc in range(nkc):
                bA = [0, 1][kc % 2]
                bB = [5, 6][kc % 2]
                par = kc % 2
                masked = kc >= 4 * qt
                midx = d["cm_idx"][(qt, kc)] if masked else None

                def _smm(_e, hp=hp, buf=buf, kc=kc, qs=qs, bA=bA, bB=bB,
                         kvv=kvv, biasv=biasv, xh=x_hist.get(kc - 2, 0)):
                    t = nc.tensor
                    if kc == 0:
                        t.wait_ge(s_kv.h, kvv)
                        t.wait_ge(s_x.h, biasv)
                    if xh:
                        t.wait_ge(s_x.h, xh)
                    t.matmul(P.ps[bA][:, :], kbuf[buf][0:64, kc * 128:(kc + 1) * 128],
                             QT[0:64, hp * TOK + qs.start: hp * TOK + qs.stop],
                             start=True, stop=True, tile_position=(0, 0))
                    s_pe.inc(t.matmul(P.ps[bB][:, :],
                                      kbuf[buf][64:128, kc * 128:(kc + 1) * 128],
                                      QT[64:128, hp * TOK + qs.start: hp * TOK + qs.stop],
                                      start=True, stop=True, tile_position=(64, 0)))
                P.E("tensor")(_smm)
                pev = s_pe.v

                if masked:
                    def _msk(_e, midx=midx, bA=bA, bB=bB, pev=pev):
                        v = nc.vector
                        v.wait_ge(s_pe.h, pev)
                        v.tensor_scalar(MSK, P.Tij, d["sb_cmD"][:, midx:midx + 1],
                                        None, ALU.is_gt)
                        v.scalar_tensor_tensor(P.ps[bA][:, :], MSK, -1e9,
                                               P.ps[bA][:, :], ALU.mult, ALU.add)
                        s_x.inc(v.scalar_tensor_tensor(P.ps[bB][:, :], MSK, -1e9,
                                                       P.ps[bB][:, :], ALU.mult, ALU.add))
                    P.E("vector")(_msk)
                mskv = s_x.v

                if kc % 2 == 0:
                    continue

                def _exp2(_e, kc=kc, pev=pev, mskv=mskv, bpv=base_pv,
                          biasv=biasv, any_masked=(kc >= 4 * qt)):
                    s = nc.scalar
                    s.wait_ge(s_pe.h, pev)
                    if any_masked:
                        s.wait_ge(s_x.h, mskv)
                    if kc == 1:
                        s.wait_ge(s_x.h, biasv)
                    if kc >= 3:
                        s.wait_ge(s_pv.h, bpv + (kc - 1))
                    for jj, pr in ((kc - 1, 0), (kc, 1)):
                        s.activation(PA[pr], P.ps[[0, 1][jj % 2]][:, :], AF.Exp,
                                     bias=NRM[:, 8:9], scale=0.125).then_inc(s_x.h, 1)
                        s_x.v += 1
                        s.activation(PB[pr], P.ps[[5, 6][jj % 2]][:, :], AF.Exp,
                                     bias=NRM[:, 9:10], scale=0.125).then_inc(s_x.h, 1)
                        s_x.v += 1
                P.E("scalar")(_exp2)

                for jj in (kc - 1, kc):
                    xev = s_x.v - (2 if jj == kc - 1 else 0)

                    def _pv(_e, kc=jj, par=jj % 2, buf=buf, xev=xev, nkc=nkc,
                            g2=P.bankg.get(2), g3=P.bankg.get(3)):
                        t = nc.tensor
                        t.wait_ge(s_x.h, xev)
                        if kc == 0 and g2:
                            t.wait_ge(g2[0].h, g2[1])
                        if kc == 0 and g3:
                            t.wait_ge(g3[0].h, g3[1])
                        t.matmul(P.ps[2][0:65, :], vbuf[buf][:, kc * 130: kc * 130 + 65],
                                 PA[par], start=(kc == 0), stop=(kc == nkc - 1))
                        inst = t.matmul(P.ps[3][0:65, :],
                                        vbuf[buf][:, kc * 130 + 65: kc * 130 + 130],
                                        PB[par], start=(kc == 0), stop=(kc == nkc - 1))
                        s_pv.inc(inst)
                    P.E("tensor")(_pv)
                    x_hist[jj] = xev

            pvv = s_pv.v

            def _rec(_e, pvv=pvv):
                v = nc.vector

                def edge(inst):
                    s_ev.inc(inst)
                    v.wait_ge(s_ev.h, s_ev.v)
                v.wait_ge(s_pv.h, pvv)
                edge(v.tensor_scalar(BCN[0:1, :], P.ps[2][64:65, :], 1e-30, None, ALU.add))
                v.reciprocal(BCN[0:1, :], BCN[0:1, :])
                edge(v.tensor_scalar(SCR[0:1, 0:512], P.ps[3][64:65, :], 1e-30, None, ALU.add))
                s_ev.inc(v.reciprocal(SCR[0:1, 0:512], SCR[0:1, 0:512]))
            P.E("vector")(_rec)
            evr = s_ev.v

            def _bc(_e, evr=evr, w4=P.bankg.get(4)):
                t = nc.tensor
                t.wait_ge(s_ev.h, evr)
                if w4:
                    t.wait_ge(w4[0].h, w4[1])
                t.matmul(P.ps[4][0:64, :], P.ones_row_f[0:1, 0:64], BCN[0:1, :],
                         start=True, stop=True, tile_position=(0, 0))
                s_pe.inc(t.matmul(P.ps[4][64:128, :], P.ones_row_f[0:1, 0:64],
                                  SCR[0:1, 0:512], start=True, stop=True,
                                  tile_position=(0, 64)))
            P.E("tensor")(_bc)
            pvb = s_pe.v

            def _cpb(_e, pvb=pvb):
                s = nc.scalar
                s.wait_ge(s_pe.h, pvb)
                s_x.inc(s.copy(BCN[:, :], P.ps[4][:, :]))
            P.E("scalar")(_cpb)
            xvb = s_x.v
            P.bankg[4] = (s_x, s_x.v)

            def _nrm(_e, hp=hp, qs=qs, xvb=xvb):
                v = nc.vector
                v.wait_ge(s_x.h, xvb)
                dst = OT[:, hp * TOK + qs.start: hp * TOK + qs.stop]
                v.tensor_tensor(dst[0:64, :], P.ps[2][0:64, :], BCN[0:64, :], ALU.mult)
                s_ev.inc(v.tensor_tensor(dst[64:128, :], P.ps[3][0:64, :],
                                         BCN[64:128, :], ALU.mult))
            P.E("vector")(_nrm)
            P.bankg[2] = (s_ev, s_ev.v)
            P.bankg[3] = (s_ev, s_ev.v)
        buf_last_use[buf] = s_pe.v
        kvv = nxt if hp + 1 < NCH else kvv
    P.bar()

    wb2 = [ar.take(BF16, 1024) for _ in range(2)]
    s_w = P.sem("ffw")
    wg = {}
    for oc in range(NCH):
        bs = oc % 2

        def _dma(_e, oc=oc, bs=bs, wgv=wg.get(bs, 0)):
            g = nc.gpsimd
            if wgv:
                g.wait_ge(s_pe.h, wgv)
            g.dma_start(out=sb3(wb2[bs]), in_=dram_colblock(d["b_woutT"], oc)
                        ).then_inc(s_w.h, 16)
        P.E("gpsimd")(_dma)
        s_w.v += 16
        wv = s_w.v
        for ti in range(4):
            ts = slice(ti * 512, (ti + 1) * 512)
            pv = mm_group(P, s_pe, [(s_w, wv if ti == 0 else 0)], 2 + (oc % 2),
                          wb2[bs],
                          lambda k, ts=ts: OT[:, k * TOK + ts.start: k * TOK + ts.stop], 8)

            def _vec(_e, oc=oc, ts=ts, pv=pv, bank=2 + (oc % 2)):
                v = nc.vector
                v.wait_ge(s_pe.h, pv)
                dst = XR[:, oc * TOK + ts.start: oc * TOK + ts.stop]
                s_ev.inc(v.tensor_tensor(dst, P.ps[bank][:, :], dst, ALU.add))
            P.E("vector")(_vec)
            P.bankg[2 + (oc % 2)] = (s_ev, s_ev.v)
        wg[bs] = pv
    ar.reset(m0)


def _sl_l1_attn_b(P, d, XR, tag):
    ar = P.ar
    m0 = ar.mark()
    d["sb_cmD"] = ar.take(F32, 80)
    s_ld = P.sem("sbld")

    def _l(e):
        e.dma_start(out=d["sb_cmD"][:, :], in_=d["cmD"][:, :]).then_inc(s_ld.h, 16)
        e.wait_ge(s_ld.h, s_ld.v + 16)
    P.E("sync")(_l)
    s_ld.v += 16
    P.bar()
    d["cm_idx"] = {}
    idx = 0
    for qt in range(4):
        for kc in range(4 * qt, 20 + 4 * qt):
            d["cm_idx"][(qt, kc)] = idx
            idx += 1
    emit_l1_attn_b(P, d, XR, tag)
    ar.reset(m0)


def decl_l1_attn_b(nc):
    d = {}
    d["b_wqkvT"] = nc.dram_tensor("b_wqkvT", [DM, 3 * DM], F32, kind="ExternalInput").ap()
    d["b_woutT"] = nc.dram_tensor("b_woutT", [DM, DM], F32, kind="ExternalInput").ap()
    d["bgrot"] = nc.dram_tensor("bgrot", [16, 16], F32, kind="ExternalInput").ap()
    d["cmD"] = _dram_shared(nc, "cmD", [128, 80], F32)
    if "k_loc" not in getattr(nc, "_declared", {}):
        pass
    d["k_loc"] = nc.dram_tensor("k_locB", [DM, TOK], BF16).ap()
    d["k_full"] = nc.dram_tensor("k_fullB", [2 * DM, TOK], BF16).ap()
    d["v_loc"] = nc.dram_tensor("v_locB", [TOK, DM], BF16).ap()
    d["v_full"] = nc.dram_tensor("v_fullB", [2 * TOK, DM], BF16).ap()
    return d


def host_cmD(core):
    h = core % 2
    D = np.zeros((128, 80), np.float32)
    idx = 0
    for qt in range(4):
        for kc in range(4 * qt, 20 + 4 * qt):
            D[:, idx] = 128 * (16 * h + 4 * qt - kc)
            idx += 1
    return D


def host_l1_attn_b_inputs(I, core):
    f = np.float32
    bg = np.asarray(I["L1_attn_b_bandha_gate"], f)
    cyc = [5, 6, 7, 8]
    t0 = (core % 2) * TOK
    rot = np.zeros((16, 16), f)
    for h in range(16):
        p = cyc[h % 4]
        for j in range(16):
            rot[h, j] = bg[h, (t0 + j) % p]
    return {
        "b_wqkvT": np.ascontiguousarray(np.asarray(I["L1_attn_b_w_qkv"], f)),
        "b_woutT": np.ascontiguousarray(np.asarray(I["L1_attn_b_w_out"], f)),
        "bgrot": rot,
        "cmD": host_cmD(core),
    }


def emit_l1_attn_a(P, d, XR, tag):
    nc = P.nc
    ar = P.ar
    m0 = ar.mark()
    XL = ar.take(BF16, NCH * TOK)
    emit_ln(P, XR, XL, "ln")
    P.bar()

    QT1 = ar.take(BF16, TOK)
    DEBUG["qt1"] = ar.last_off
    OT1 = ar.take(BF16, TOK)
    DEBUG["ot1"] = ar.last_off
    KST = [ar.take(BF16, 512) for _ in range(2)]
    VST = [ar.take(BF16, 512) for _ in range(2)]
    VSTT = [ar.take(BF16, 256) for _ in range(2)]
    wbq = ar.take(BF16, 3 * 64)
    wqb = [ar.take(BF16, 8 * 64) for _ in range(3)]
    wo_sb = ar.take(BF16, 1024)
    s_st = P.sem("aast")
    s_ev = P.sem("ev")
    s_pe = P.sem("pe")
    s_x = P.sem(tag + "x")
    s_pv = P.sem(tag + "pv")
    s_w = P.sem("ffw")
    s_kv = P.sem(tag + "kv")

    def _wd(_e):
        g = nc.gpsimd
        for i, nm in enumerate(["aa_wq", "aa_wk", "aa_wv"]):
            g.dma_start(out=sb3(wqb[i], n=64), in_=d[nm].rearrange("(a p) n -> p a n", p=128)
                        ).then_inc(s_w.h, 16)
        g.dma_start(out=wo_sb[0:64, :], in_=d["aa_wo"][:, :]).then_inc(s_w.h, 16)
    P.E("gpsimd")(_wd)
    s_w.v += 64
    wv0 = s_w.v
    P.bar()

    kst_guard = {}
    vst_guard = {}
    for pi, (bias_nm, kind) in enumerate((("sb_abq", "q"), ("sb_abk", "k"), ("sb_abv", "v"))):
        for ti in range(4):
            ts = slice(ti * 512, (ti + 1) * 512)
            par = ti % 2
            pv = mm_group(P, s_pe, [(s_w, wv0 if (pi == 0 and ti == 0) else 0)],
                          pi % 2, lambda k, pi=pi: wqb[pi][:, k * 64:(k + 1) * 64],
                          lambda k, ts=ts: XL[:, k * TOK + ts.start: k * TOK + ts.stop], 8,
                          mpart=64)
            if kind == "q":
                dst = QT1[:, ts]
                dg = None
            elif kind == "k":
                dst = KST[par]
                g = kst_guard.get(par)
                dg = (s_st, g) if g else None
            else:
                dst = VST[par]
                g = vst_guard.get(par)
                dg = (s_pe, g) if g else None

            def _ev(_e, pv=pv, dst=dst, dg=dg, bias_nm=bias_nm, bank=pi % 2):
                v = nc.vector
                v.wait_ge(s_pe.h, pv)
                if dg is not None:
                    v.wait_ge(dg[0].h, dg[1])
                s_ev.inc(v.tensor_scalar(dst[0:64, :] if dst.shape[0] == 128 else dst,
                                         P.ps[bank][0:64, :],
                                         d[bias_nm][0:64, 0:1], None, ALU.add))
            P.E("vector")(_ev)
            P.bankg[pi % 2] = (s_ev, s_ev.v)
            if kind == "k":
                ev = s_ev.v

                def _dma(_e, ti=ti, par=par, ev=ev):
                    e = nc.sync
                    e.wait_ge(s_ev.h, ev)
                    e.dma_start(out=d["k2_loc"][0:64, ti * 512:(ti + 1) * 512],
                                in_=KST[par][0:64, :]).then_inc(s_st.h, 16)
                P.E("sync")(_dma)
                s_st.v += 16
                kst_guard[par] = s_st.v
            elif kind == "v":
                ev = s_ev.v

                def _tr(_e, par=par, ev=ev, pg=P.bankg.get("psbf")):
                    t = nc.tensor
                    t.wait_ge(s_ev.h, ev)
                    if pg:
                        t.wait_ge(pg[0].h, pg[1])
                    for i in range(4):
                        inst = t.transpose(P.psbf[:, i * 64:(i + 1) * 64],
                                           VST[par][0:64, i * 128:(i + 1) * 128],
                                           P.ident_bf[0:64, 0:64])
                    s_pe.inc(inst)
                P.E("tensor")(_tr)
                pvt = s_pe.v

                def _cp(_e, par=par, pvt=pvt, g=vst_guard.get(("tt", par), 0)):
                    s = nc.scalar
                    s.wait_ge(s_pe.h, pvt)
                    if g:
                        s.wait_ge(s_st.h, g)
                    s_ev.inc(s.copy(VSTT[par], P.psbf[:, 0:256]))
                P.E("scalar")(_cp)
                evc = s_ev.v
                P.bankg["psbf"] = (s_ev, evc)

                def _dma(_e, ti=ti, par=par, evc=evc):
                    e = nc.sync
                    e.wait_ge(s_ev.h, evc)
                    dst2 = d["v2_loc"].rearrange("(i p) n -> p i n", p=128)[
                        :, 4 * ti: 4 * ti + 4, :]
                    e.dma_start(out=dst2, in_=sb3(VSTT[par], n=64)).then_inc(s_st.h, 16)
                P.E("sync")(_dma)
                s_st.v += 16
                vst_guard[par] = s_pe.v
                vst_guard[("tt", par)] = s_st.v
    P.bar()

    s_cck = P.sem(tag + "cck")
    s_ccv = P.sem(tag + "ccv")

    def _cc(_e):
        g = nc.gpsimd
        g.wait_ge(s_st.h, s_st.v)
        g.collective_compute("AllGather", ALU.bypass,
                             replica_groups=[[0, 1], [2, 3], [4, 5], [6, 7]],
                             ins=[d["k2_loc"][:, :]], outs=[d["k2_full"][:, :]]
                             ).then_inc(s_cck.h)
        g.collective_compute("AllGather", ALU.bypass,
                             replica_groups=[[0, 1], [2, 3], [4, 5], [6, 7]],
                             ins=[d["v2_loc"][:, :]], outs=[d["v2_full"][:, :]]
                             ).then_inc(s_ccv.h)
        g.wait_ge(s_cck.h, 1)
        g.wait_ge(s_ccv.h, 1)
    P.E("gpsimd")(_cc)
    s_cck.v = 1
    s_ccv.v = 1
    P.bar()

    kbuf = ar.take(BF16, 4096)
    DEBUG["kbuf"] = ar.last_off
    vbuf = ar.take(BF16, 32 * 65)
    DEBUG["vbuf"] = ar.last_off
    PA = [ar.take(BF16, 512) for _ in range(2)]
    DEBUG["pa"] = ar.last_off
    MSK = ar.take(F32, 512)
    SCR = ar.take(F32, 2048)
    NRM = ar.take(F32, 192)
    DEBUG["nrm"] = ar.last_off
    BCN = ar.take(F32, 512)

    def _ones(_e):
        v = nc.vector
        v.memset(vbuf.rearrange("p (a n) -> p a n", n=65)[:, :, 64:65], 1.0)
        s_ev.inc(v.memset(NRM[:, :], 0.0))
    P.E("vector")(_ones)

    def _kvl(_e):
        e = nc.scalar
        e.dma_start(out=kbuf[0:64, 0:2048], in_=d["k2_full"][0:64, :]).then_inc(s_kv.h, 16)
        e.dma_start(out=kbuf[0:64, 2048:4096], in_=d["k2_full"][64:128, :]).then_inc(s_kv.h, 16)
        src2 = d["v2_full"].rearrange("(a p) n -> p a n", p=128)
        dst2 = vbuf.rearrange("p (a n) -> p a n", n=65)[:, :, 0:64]
        e.dma_start(out=dst2, in_=src2).then_inc(s_kv.h, 16)
    P.E("scalar")(_kvl)
    s_kv.v += 48
    kvv = s_kv.v
    P.bar()

    s_mx = P.sem(tag + "mx")
    for qt in range(4):
        qs = slice(qt * 512, (qt + 1) * 512)
        nkc = 20 + 4 * qt
        x_hist = {}
        for kc in range(nkc):
            bA = [0, 1][kc % 2]
            masked = kc >= 4 * qt
            midx = d["cm_idx"][(qt, kc)] if masked else None

            def _smm(_e, kc=kc, qs=qs, bA=bA, kvv=kvv, xh=x_hist.get(kc - 2, 0),
                     ev=P.bankg.get("rmx")):
                t = nc.tensor
                if kc == 0:
                    t.wait_ge(s_kv.h, kvv)
                    if ev:
                        t.wait_ge(ev[0].h, ev[1])
                if xh:
                    t.wait_ge(s_x.h, xh)
                s_pe.inc(t.matmul(P.ps[bA][:, :], kbuf[0:64, kc * 128:(kc + 1) * 128],
                                  QT1[0:64, qs], start=True, stop=True,
                                  tile_position=(0, 0)))
            P.E("tensor")(_smm)
            pev = s_pe.v

            def _mx(_e, kc=kc, bA=bA, pev=pev, masked=masked, midx=midx):
                v = nc.vector
                v.wait_ge(s_pe.h, pev)
                if masked:
                    v.tensor_scalar(MSK, P.Tij, d["sb_cmD"][:, midx:midx + 1],
                                    None, ALU.is_gt)
                    v.scalar_tensor_tensor(P.ps[bA][:, :], MSK, -1e9,
                                           P.ps[bA][:, :], ALU.mult, ALU.add)
                if kc == 0:
                    s_x.inc(v.reduce_max(NRM[:, 0:1], P.ps[bA][:, :], axis=AX))
                else:
                    v.reduce_max(NRM[:, 1:2], P.ps[bA][:, :], axis=AX)
                    s_x.inc(v.tensor_tensor(NRM[:, 0:1], NRM[:, 0:1],
                                            NRM[:, 1:2], ALU.max))
            P.E("vector")(_mx)
            x_hist[kc] = s_x.v
        xvl = s_x.v

        def _mdma(_e, xvl=xvl):
            e = nc.sync
            e.wait_ge(s_x.h, xvl)
            e.dma_start(out=d["mrow"][0:128, 0:1], in_=NRM[:, 0:1]).then_inc(s_mx.h, 16)
            e.wait_ge(s_mx.h, s_mx.v + 16)
            e.dma_start(out=NRM[0:1, 16:144],
                        in_=d["mrow"].rearrange("a b -> (a b)")[None, :]
                        ).then_inc(s_mx.h, 16)
        P.E("sync")(_mdma)
        s_mx.v += 32
        mxv = s_mx.v

        def _mfin(_e, mxv=mxv):
            v = nc.vector
            v.wait_ge(s_mx.h, mxv)
            v.reduce_max(NRM[0:1, 2:3], NRM[0:1, 16:144], axis=AX)
            s_ev.inc(v.tensor_scalar(NRM[0:1, 2:3], NRM[0:1, 2:3], -0.125, None,
                                     ALU.mult))
        P.E("vector")(_mfin)
        evm = s_ev.v

        def _mpe(_e, evm=evm, w4=P.bankg.get(4)):
            t = nc.tensor
            t.wait_ge(s_ev.h, evm)
            if w4:
                t.wait_ge(w4[0].h, w4[1])
            s_pe.inc(t.matmul(P.ps[4][:, 0:1], P.ones_row_f, NRM[0:1, 2:3],
                              start=True, stop=True))
        P.E("tensor")(_mpe)
        pv = s_pe.v

        def _mcp(_e, pv=pv):
            s = nc.scalar
            s.wait_ge(s_pe.h, pv)
            s_x.inc(s.copy(NRM[:, 8:9], P.ps[4][:, 0:1]))
        P.E("scalar")(_mcp)
        P.bankg[4] = (s_x, s_x.v)
        biasv = s_x.v

        base_pv = s_pv.v
        x_hist = {}
        for kc in range(nkc):
            bA = [0, 1][kc % 2]
            par = kc % 2
            masked = kc >= 4 * qt
            midx = d["cm_idx"][(qt, kc)] if masked else None

            def _smm(_e, kc=kc, qs=qs, bA=bA, biasv=biasv, xh=x_hist.get(kc - 2, 0)):
                t = nc.tensor
                if kc == 0:
                    t.wait_ge(s_x.h, biasv)
                if xh:
                    t.wait_ge(s_x.h, xh)
                s_pe.inc(t.matmul(P.ps[bA][:, :], kbuf[0:64, kc * 128:(kc + 1) * 128],
                                  QT1[0:64, qs], start=True, stop=True,
                                  tile_position=(0, 0)))
            P.E("tensor")(_smm)
            pev = s_pe.v

            if masked:
                def _msk(_e, midx=midx, bA=bA, pev=pev):
                    v = nc.vector
                    v.wait_ge(s_pe.h, pev)
                    v.tensor_scalar(MSK, P.Tij, d["sb_cmD"][:, midx:midx + 1],
                                    None, ALU.is_gt)
                    s_x.inc(v.scalar_tensor_tensor(P.ps[bA][:, :], MSK, -1e9,
                                                   P.ps[bA][:, :], ALU.mult, ALU.add))
                P.E("vector")(_msk)
            mskv = s_x.v

            def _exp(_e, kc=kc, par=par, bA=bA, pev=pev, mskv=mskv,
                     bpv=base_pv, biasv=biasv, masked=masked):
                s = nc.scalar
                s.wait_ge(s_pe.h, pev)
                if masked:
                    s.wait_ge(s_x.h, mskv)
                if kc == 0:
                    s.wait_ge(s_x.h, biasv)
                if kc >= 2:
                    s.wait_ge(s_pv.h, bpv + (kc - 1))
                s_x.inc(s.activation(PA[par], P.ps[bA][:, :], AF.Exp,
                                     bias=NRM[:, 8:9], scale=0.125))
            P.E("scalar")(_exp)
            xev = s_x.v

            def _pv(_e, kc=kc, par=par, xev=xev, nkc=nkc, g2=P.bankg.get(2)):
                t = nc.tensor
                t.wait_ge(s_x.h, xev)
                if kc == 0 and g2:
                    t.wait_ge(g2[0].h, g2[1])
                s_pv.inc(t.matmul(P.ps[2][0:65, :], vbuf[:, kc * 65: kc * 65 + 65],
                                  PA[par], start=(kc == 0), stop=(kc == nkc - 1)))
            P.E("tensor")(_pv)
            x_hist[kc] = s_x.v

        pvv = s_pv.v

        def _rec(_e, pvv=pvv):
            v = nc.vector
            v.wait_ge(s_pv.h, pvv)
            s_ev.inc(v.tensor_single_scalar(BCN[0:1, :], P.ps[2][64:65, :], 1e-37, ALU.max))
            v.wait_ge(s_ev.h, s_ev.v)
            s_ev.inc(v.reciprocal(BCN[0:1, :], BCN[0:1, :]))
        P.E("vector")(_rec)
        evr = s_ev.v

        def _bc(_e, evr=evr, w4=P.bankg.get(4)):
            t = nc.tensor
            t.wait_ge(s_ev.h, evr)
            if w4:
                t.wait_ge(w4[0].h, w4[1])
            s_pe.inc(t.matmul(P.ps[4][0:64, :], P.ones_row_f[0:1, 0:64], BCN[0:1, :],
                              start=True, stop=True, tile_position=(0, 0)))
        P.E("tensor")(_bc)
        pvb = s_pe.v

        def _cpb(_e, pvb=pvb):
            s = nc.scalar
            s.wait_ge(s_pe.h, pvb)
            s_x.inc(s.copy(BCN[0:64, :], P.ps[4][0:64, :]))
        P.E("scalar")(_cpb)
        xvb = s_x.v
        P.bankg[4] = (s_x, s_x.v)

        def _nrm(_e, qs=qs, xvb=xvb):
            v = nc.vector
            v.wait_ge(s_x.h, xvb)
            s_ev.inc(v.tensor_tensor(OT1[0:64, qs], P.ps[2][0:64, :],
                                     BCN[0:64, :], ALU.mult))
        P.E("vector")(_nrm)
        P.bankg[2] = (s_ev, s_ev.v)
        P.bankg["rmx"] = (s_ev, s_ev.v)
    P.bar()

    for oc in range(NCH):
        for ti in range(4):
            ts = slice(ti * 512, (ti + 1) * 512)
            pv = mm_group(P, s_pe, [], 2 + (oc % 2),
                          lambda k, oc=oc: wo_sb[0:64, oc * 128:(oc + 1) * 128],
                          lambda k, ts=ts: OT1[0:64, ts], 1)

            def _vec(_e, oc=oc, ts=ts, pv=pv, bank=2 + (oc % 2)):
                v = nc.vector
                v.wait_ge(s_pe.h, pv)
                dst = XR[:, oc * TOK + ts.start: oc * TOK + ts.stop]
                s_ev.inc(v.scalar_tensor_tensor(dst, P.ps[bank][:, :],
                                                d["sb_abo"][:, oc:oc + 1], dst,
                                                ALU.add, ALU.add))
            P.E("vector")(_vec)
            P.bankg[2 + (oc % 2)] = (s_ev, s_ev.v)
    ar.reset(m0)


def _sl_l1_attn_a(P, d, XR, tag):
    ar = P.ar
    m0 = ar.mark()
    d["sb_abq"] = ar.take(F32, 1)
    d["sb_abk"] = ar.take(F32, 1)
    d["sb_abv"] = ar.take(F32, 1)
    d["sb_abo"] = ar.take(F32, 8)
    d["sb_cmD"] = ar.take(F32, 80)
    s_ld = P.sem("sbld")

    def _l(e):
        for nm, src_nm in (("sb_abq", "aa_bq"), ("sb_abk", "aa_bk"),
                           ("sb_abv", "aa_bv"), ("sb_abo", "aa_bo"),
                           ("sb_cmD", "cmD")):
            e.dma_start(out=d[nm][:, :], in_=d[src_nm][:, :]).then_inc(s_ld.h, 16)
        e.wait_ge(s_ld.h, s_ld.v + 80)
    P.E("sync")(_l)
    s_ld.v += 80
    P.bar()
    d["cm_idx"] = {}
    idx = 0
    for qt in range(4):
        for kc in range(4 * qt, 20 + 4 * qt):
            d["cm_idx"][(qt, kc)] = idx
            idx += 1
    emit_l1_attn_a(P, d, XR, tag)
    ar.reset(m0)


def decl_l1_attn_a(nc):
    d = {}
    d["aa_wq"] = nc.dram_tensor("aa_wq", [DM, 64], F32, kind="ExternalInput").ap()
    d["aa_wk"] = nc.dram_tensor("aa_wk", [DM, 64], F32, kind="ExternalInput").ap()
    d["aa_wv"] = nc.dram_tensor("aa_wv", [DM, 64], F32, kind="ExternalInput").ap()
    d["aa_wo"] = nc.dram_tensor("aa_wo", [64, DM], F32, kind="ExternalInput").ap()
    d["aa_bq"] = nc.dram_tensor("aa_bq", [128, 1], F32, kind="ExternalInput").ap()
    d["aa_bk"] = nc.dram_tensor("aa_bk", [128, 1], F32, kind="ExternalInput").ap()
    d["aa_bv"] = nc.dram_tensor("aa_bv", [128, 1], F32, kind="ExternalInput").ap()
    d["aa_bo"] = nc.dram_tensor("aa_bo", [128, 8], F32, kind="ExternalInput").ap()
    d["cmD"] = _dram_shared(nc, "cmD", [128, 80], F32)
    d["k2_loc"] = nc.dram_tensor("k2_loc", [64, TOK], BF16).ap()
    d["mrow"] = nc.dram_tensor("mrow", [128, 1], F32).ap()
    d["k2_full"] = nc.dram_tensor("k2_full", [128, TOK], BF16).ap()
    d["v2_loc"] = nc.dram_tensor("v2_loc", [TOK, 64], BF16).ap()
    d["v2_full"] = nc.dram_tensor("v2_full", [2 * TOK, 64], BF16).ap()
    return d


def host_l1_attn_a_inputs(I, core):
    f = np.float32

    def pad64(v):
        o = np.zeros((128, 1), f)
        o[0:64, 0] = np.asarray(v, f)
        return o
    return {
        "aa_wq": np.ascontiguousarray(np.asarray(I["L1_attn_a_wq"], f)),
        "aa_wk": np.ascontiguousarray(np.asarray(I["L1_attn_a_wk"], f)),
        "aa_wv": np.ascontiguousarray(np.asarray(I["L1_attn_a_wv"], f)),
        "aa_wo": np.ascontiguousarray(np.asarray(I["L1_attn_a_wo"], f)),
        "aa_bq": pad64(I["L1_attn_a_bq"]),
        "aa_bk": pad64(I["L1_attn_a_bk"]),
        "aa_bv": pad64(I["L1_attn_a_bv"]),
        "aa_bo": pack_partition_vec(I["L1_attn_a_bo"]),
        "cmD": host_cmD(core),
    }


CAP = 5120


def emit_l0_ffn(P, d, XR, tag):
    nc = P.nc
    ar = P.ar
    m0 = ar.mark()
    s_ev = P.sem("ev")
    s_pe = P.sem("pe")
    s_x = P.sem(tag + "x")
    s_st = P.sem("aast")
    s_w = P.sem("ffw")
    s_g = P.sem(tag + "gg")

    XF = ar.take(BF16, NCH * TOK)
    xf_off = ar.last_off
    QKVT = ar.take(BF16, 16 * TOK)
    qkv_off = ar.last_off
    HA = None

    def qkv_at(c, ts=None):
        v = ar.view(qkv_off, BF16, 16 * TOK)[:, c * TOK:(c + 1) * TOK]
        return v if ts is None else v[:, ts]

    emit_ln(P, XR, XF, "ln")
    P.bar()

    c_off = ar.mark()
    XL1 = ar.take(BF16, NCH * TOK)
    emit_ln(P, XF, XL1, tag + "l1", src_bf=True,
            gamma=d["sb_ln1w"], beta=d["sb_ln1b"],
            scratch=ar.sub(qkv_off, 16 * TOK * 2))
    P.bar()
    wb = [ar.take(BF16, 1024) for _ in range(2)]
    for oc in range(16):
        bs = oc % 2

        def _dma(_e, oc=oc, bs=bs, wgv=P.bankg.get(("projw", bs))):
            g = nc.gpsimd
            if wgv:
                g.wait_ge(wgv[0].h, wgv[1])
            g.dma_start(out=sb3(wb[bs]), in_=dram_colblock(d["f_qkvT"], oc)
                        ).then_inc(s_w.h, 16)
        P.E("gpsimd")(_dma)
        s_w.v += 16
        wv = s_w.v
        for ti in range(4):
            ts = slice(ti * 512, (ti + 1) * 512)
            pv = mm_group(P, s_pe, [(s_w, wv if ti == 0 else 0)], bs, wb[bs],
                          lambda k, ts=ts: XL1[:, k * TOK + ts.start: k * TOK + ts.stop], 8)

            def _cp(_e, oc=oc, ts=ts, pv=pv, bank=bs):
                s = nc.scalar
                s.wait_ge(s_pe.h, pv)
                s_ev.inc(s.copy(qkv_at(oc, ts), P.ps[bank][:, :]))
            P.E("scalar")(_cp)
            P.bankg[bs] = (s_ev, s_ev.v)
            if oc % 2 == 1:
                ev = s_ev.v

                def _kvst(_e, oc=oc, ts=ts, ev=ev):
                    e = nc.sync
                    e.wait_ge(s_ev.h, ev)
                    e.dma_start(out=d["kv_loc"][(oc // 2) * 128:(oc // 2) * 128 + 128,
                                                ts], in_=qkv_at(oc, ts)
                                ).then_inc(s_st.h, 16)
                P.E("sync")(_kvst)
                s_st.v += 16
        P.bankg[("projw", bs)] = (s_pe, pv)
    P.bar()
    ar.reset(c_off)

    if FFN_PH < "D":
        ar.reset(m0)
        return
    s_cc1 = P.sem(tag + "cc1")

    def _cc(_e):
        g = nc.gpsimd
        g.wait_ge(s_st.h, s_st.v)
        g.collective_compute("AllGather", ALU.bypass,
                             replica_groups=[[0, 2, 4, 6], [1, 3, 5, 7]],
                             ins=[d["kv_loc"][:, :]], outs=[d["kvA_full"][:, :]]
                             ).then_inc(s_cc1.h)
        g.wait_ge(s_cc1.h, 1)
    P.E("gpsimd")(_cc)
    s_cc1.v = 1
    P.bar()

    if FFN_PH < "E":
        ar.reset(m0)
        return
    s_sp0 = P.sem("ffsp")

    def _spl(e):
        for j in range(2):
            e.dma_start(out=d["xr_spill"].rearrange("(j p) n -> p j n", p=128)[:, j, :],
                        in_=XR[:, (6 + j) * TOK:(7 + j) * TOK]).then_inc(s_sp0.h, 16)
        e.wait_ge(s_sp0.h, s_sp0.v + 32)
    P.E("sync")(_spl)
    s_sp0.v += 32
    P.bar()
    KD = ar.take(BF16, 4 * TOK)
    VD = ar.take(BF16, 4 * TOK)
    QK = ar.take(F32, 512)
    AOA = ar.take(F32, 512)
    PBC = ar.take(F32, 512)
    ROWS = ar.view(6 * TOK * 4, F32, 4 * 512)
    RW2 = ar.view(6 * TOK * 4 + 8192, F32, 512)
    s_kv = P.sem(tag + "kv")

    def load_g(g_, ev_guard):
        def _d(_e, g_=g_, ev_guard=ev_guard):
            e = nc.scalar
            if ev_guard:
                e.wait_ge(ev_guard[0].h, ev_guard[1])
            for bp in range(4):
                base = bp * 1024 + g_ * 128
                for half in range(2):
                    e.dma_start(out=KD[half * 64:half * 64 + 64, bp * TOK:(bp + 1) * TOK],
                                in_=d["kvA_full"][base:base + 64, :]).then_inc(s_kv.h, 16)
                    e.dma_start(out=VD[half * 64:half * 64 + 64, bp * TOK:(bp + 1) * TOK],
                                in_=d["kvA_full"][base + 64:base + 128, :]).then_inc(s_kv.h, 16)
        P.E("scalar")(_d)
        s_kv.v += 16 * 16
        return s_kv.v

    for g_ in range(8):
        kvv = load_g(g_, P.bankg.get(("kvd",)))
        for ti in range(4):
            ts = slice(ti * 512, (ti + 1) * 512)
            e_rows = [ROWS[0:2, i * 512:(i + 1) * 512] for i in range(4)]
            r_den = RW2[0:2, 0:512]
            for bp in range(4):
                def _qk(_e, g_=g_, ts=ts, bp=bp, kvv=kvv,
                        pg=P.bankg.get(("qk",))):
                    v = nc.vector
                    if bp == 0 and ti == 0:
                        v.wait_ge(s_kv.h, kvv)
                    if pg:
                        v.wait_ge(pg[0].h, pg[1])
                    s_ev.inc(v.tensor_tensor(QK, qkv_at(2 * g_, ts),
                                             KD[:, bp * TOK + ts.start: bp * TOK + ts.stop],
                                             ALU.mult))
                P.E("vector")(_qk)
                evq = s_ev.v

                def _sc(_e, bp=bp, evq=evq, w4=P.bankg.get(4)):
                    t = nc.tensor
                    t.wait_ge(s_ev.h, evq)
                    if w4:
                        t.wait_ge(w4[0].h, w4[1])
                    s_pe.inc(t.matmul(P.ps[4][0:2, :], P.sel64, QK,
                                      start=True, stop=True))
                P.E("tensor")(_sc)
                pv = s_pe.v
                P.bankg[("qk",)] = (s_pe, pv)

                def _ex(_e, bp=bp, pv=pv):
                    s = nc.scalar
                    s.wait_ge(s_pe.h, pv)
                    s_x.inc(s.activation(e_rows[bp], P.ps[4][0:2, :], AF.Exp,
                                         bias=d["sb_bmask"][0:2, bp:bp + 1],
                                         scale=0.125))
                P.E("scalar")(_ex)
                P.bankg[4] = (s_x, s_x.v)
            xv = s_x.v

            def _den(_e, xv=xv):
                v = nc.vector
                v.wait_ge(s_x.h, xv)
                v.tensor_tensor(r_den, e_rows[0], e_rows[1], ALU.add)
                v.tensor_tensor(r_den, r_den, e_rows[2], ALU.add)
                s_ev.inc(v.tensor_tensor(r_den, r_den, e_rows[3], ALU.add))
                v.wait_ge(s_ev.h, s_ev.v)
                v.reciprocal(r_den, r_den)
                for bp in range(4):
                    inst = v.tensor_tensor(e_rows[bp], e_rows[bp], r_den, ALU.mult)
                s_ev.inc(inst)
            P.E("vector")(_den)
            evp = s_ev.v

            for bp in range(4):
                def _pb(_e, bp=bp, evp=evp, w4=P.bankg.get(4)):
                    t = nc.tensor
                    t.wait_ge(s_ev.h, evp)
                    if w4:
                        t.wait_ge(w4[0].h, w4[1])
                    s_pe.inc(t.matmul(P.ps[4][:, :], P.sel64T, e_rows[bp],
                                      start=True, stop=True))
                P.E("tensor")(_pb)
                pv = s_pe.v

                def _cpp(_e, pv=pv, bg=P.bankg.get(("pbc",))):
                    s = nc.scalar
                    s.wait_ge(s_pe.h, pv)
                    if bg:
                        s.wait_ge(bg[0].h, bg[1])
                    s_x.inc(s.copy(PBC, P.ps[4][:, :]))
                P.E("scalar")(_cpp)
                xvp = s_x.v
                P.bankg[4] = (s_x, s_x.v)

                def _acc(_e, g_=g_, ts=ts, bp=bp, xvp=xvp):
                    v = nc.vector
                    v.wait_ge(s_x.h, xvp)
                    v.tensor_tensor(PBC, PBC,
                                    VD[:, bp * TOK + ts.start: bp * TOK + ts.stop],
                                    ALU.mult)
                    if bp == 0:
                        s_ev.inc(v.tensor_copy(AOA, PBC))
                    elif bp < 3:
                        s_ev.inc(v.tensor_tensor(AOA, AOA, PBC, ALU.add))
                    else:
                        s_ev.inc(v.tensor_tensor(qkv_at(2 * g_ + 1, ts), AOA, PBC,
                                                 ALU.add))
                P.E("vector")(_acc)
                P.bankg[("pbc",)] = (s_ev, s_ev.v)
        P.bankg[("kvd",)] = (s_ev, s_ev.v)
    P.bar()

    def _rsl(e):
        for j in range(2):
            e.dma_start(out=XR[:, (6 + j) * TOK:(7 + j) * TOK],
                        in_=d["xr_spill"].rearrange("(j p) n -> p j n", p=128)[:, j, :]
                        ).then_inc(s_sp0.h, 16)
        e.wait_ge(s_sp0.h, s_sp0.v + 32)
    P.E("sync")(_rsl)
    s_sp0.v += 32
    P.bar()
    ar.reset(c_off)

    if FFN_PH < "F":
        ar.reset(m0)
        return
    HA = ar.take(BF16, NCH * TOK)
    ha_off = ar.last_off
    wb = [ar.take(BF16, 1024) for _ in range(2)]
    for oc in range(NCH):
        bs = oc % 2

        def _dma(_e, oc=oc, bs=bs, wgv=P.bankg.get(("projw2", bs))):
            g = nc.gpsimd
            if wgv:
                g.wait_ge(wgv[0].h, wgv[1])
            g.dma_start(out=sb3(wb[bs]), in_=dram_colblock(d["f_projT"], oc)
                        ).then_inc(s_w.h, 16)
        P.E("gpsimd")(_dma)
        s_w.v += 16
        wv = s_w.v
        for ti in range(4):
            ts = slice(ti * 512, (ti + 1) * 512)
            pv = mm_group(P, s_pe, [(s_w, wv if ti == 0 else 0)], bs,
                          wb[bs], lambda k, ts=ts: qkv_at(2 * k + 1, ts), 8)

            def _ha(_e, oc=oc, ts=ts, pv=pv, bank=bs):
                v = nc.vector
                v.wait_ge(s_pe.h, pv)
                s_ev.inc(v.tensor_tensor(HA[:, oc * TOK + ts.start: oc * TOK + ts.stop],
                                         P.ps[bank][:, :],
                                         XF[:, oc * TOK + ts.start: oc * TOK + ts.stop],
                                         ALU.add))
            P.E("vector")(_ha)
            P.bankg[bs] = (s_ev, s_ev.v)
        P.bankg[("projw2", bs)] = (s_pe, pv)
    P.bar()

    XM = ar.view(xf_off, BF16, NCH * TOK)
    if FFN_PH < "G":
        ar.reset(m0)
        return
    P.xf_off_cur = xf_off
    emit_ln(P, HA, XM, tag + "l2", src_bf=True,
            gamma=d["sb_ln2w"], beta=d["sb_ln2b"],
            scratch=ar.sub(qkv_off, 16 * TOK * 2))
    P.bar()
    if FFN_PH >= "H":
        emit_l0_ffn_moe(P, d, XR, XM, HA, qkv_off, ha_off, tag)
    ar.reset(m0)


def emit_l0_ffn_moe(P, d, XR, XM, HA, qkv_off, ha_off, tag):
    nc = P.nc
    ar = P.ar
    s_ev = P.sem("ev")
    s_pe = P.sem("pe")
    s_x = P.sem(tag + "x")
    s_st = P.sem("aast")
    s_w = P.sem("ffw")
    s_g = P.sem("ffg")
    mmoe = ar.mark()

    rsb = ar.take(BF16, 64)
    E12 = ar.take(F32, 2)
    PW = ar.take(F32, 2)
    RT = ar.take(F32, 64)
    s_rt = P.sem(tag + "rt")

    def _rw(_e):
        g = nc.gpsimd
        g.dma_start(out=sb3(rsb, n=8), in_=d["f_router"].rearrange("(a p) n -> p a n", p=128)
                    ).then_inc(s_w.h, 16)
    P.E("gpsimd")(_rw)
    s_w.v += 16
    wv = s_w.v
    for tt in range(16):
        def _pe(_e, tt=tt, wv=wv, w4=P.bankg.get(4)):
            t = nc.tensor
            if tt == 0:
                t.wait_ge(s_w.h, wv)
            if w4:
                t.wait_ge(w4[0].h, w4[1])
            for c in range(NCH):
                inst = t.matmul(P.ps[4][:, 0:8], XM[:, c * TOK + tt * 128: c * TOK + (tt + 1) * 128],
                                rsb[:, c * 8:(c + 1) * 8], start=(c == 0), stop=(c == NCH - 1))
            s_pe.inc(inst)
        P.E("tensor")(_pe)
        pv = s_pe.v

        def _sm(_e, tt=tt, pv=pv, rg=P.bankg.get(("rt",))):
            v = nc.vector
            v.wait_ge(s_pe.h, pv)
            if rg:
                v.wait_ge(rg[0].h, rg[1])
            v.reduce_max(RT[:, 0:1], P.ps[4][:, 0:8], axis=AX)
            s_ev.inc(v.tensor_scalar(RT[:, 1:2], RT[:, 0:1], -1.0, None, ALU.mult))
        P.E("vector")(_sm)
        evn = s_ev.v

        def _se(_e, tt=tt, evn=evn):
            s = nc.scalar
            s.wait_ge(s_ev.h, evn)
            s_x.inc(s.activation(RT[:, 8:16], P.ps[4][:, 0:8], AF.Exp,
                                 bias=RT[:, 1:2]))
        P.E("scalar")(_se)
        xv = s_x.v
        P.bankg[4] = (s_x, s_x.v)

        def _sp(_e, tt=tt, xv=xv):
            v = nc.vector

            def edge(inst):
                s_ev.inc(inst)
                v.wait_ge(s_ev.h, s_ev.v)
            v.wait_ge(s_x.h, xv)
            edge(v.reduce_sum(RT[:, 2:3], RT[:, 8:16], axis=AX))
            edge(v.reciprocal(RT[:, 3:4], RT[:, 2:3]))
            edge(v.tensor_scalar(RT[:, 8:16], RT[:, 8:16], RT[:, 3:4], None, ALU.mult))
            edge(v.max(RT[:, 16:24], RT[:, 8:16]))
            edge(v.tensor_scalar(RT[:, 24:32], RT[:, 8:16], RT[:, 16:17], None, ALU.is_equal))
            edge(v.tensor_tensor(RT[:, 32:40], RT[:, 24:32], P.IOT8, ALU.mult))
            edge(v.tensor_scalar(RT[:, 40:48], RT[:, 24:32], -1.0, 1.0, ALU.mult, ALU.add))
            edge(v.tensor_scalar(RT[:, 40:48], RT[:, 40:48], 100.0, None, ALU.mult))
            edge(v.tensor_tensor(RT[:, 40:48], RT[:, 40:48], RT[:, 32:40], ALU.add))
            edge(v.tensor_reduce(E12[:, 0:1], RT[:, 40:48], axis=AX, op=ALU.min))
            edge(v.tensor_scalar(RT[:, 48:56], P.IOT8, E12[:, 0:1], None, ALU.is_equal))
            edge(v.tensor_scalar(RT[:, 24:32], RT[:, 8:16], RT[:, 17:18], None, ALU.is_equal))
            edge(v.tensor_scalar(RT[:, 48:56], RT[:, 48:56], -1.0, 1.0, ALU.mult, ALU.add))
            edge(v.tensor_tensor(RT[:, 24:32], RT[:, 24:32], RT[:, 48:56], ALU.mult))
            edge(v.tensor_tensor(RT[:, 32:40], RT[:, 24:32], P.IOT8, ALU.mult))
            edge(v.tensor_scalar(RT[:, 40:48], RT[:, 24:32], -1.0, 1.0, ALU.mult, ALU.add))
            edge(v.tensor_scalar(RT[:, 40:48], RT[:, 40:48], 100.0, None, ALU.mult))
            edge(v.tensor_tensor(RT[:, 40:48], RT[:, 40:48], RT[:, 32:40], ALU.add))
            edge(v.tensor_reduce(E12[:, 1:2], RT[:, 40:48], axis=AX, op=ALU.min))
            edge(v.tensor_tensor(RT[:, 4:5], RT[:, 16:17], RT[:, 17:18], ALU.add))
            edge(v.reciprocal(RT[:, 4:5], RT[:, 4:5]))
            edge(v.tensor_scalar(PW[:, 0:1], RT[:, 16:17], RT[:, 4:5], None, ALU.mult))
            s_ev.inc(v.tensor_scalar(PW[:, 1:2], RT[:, 17:18], RT[:, 4:5], None, ALU.mult))
        P.E("vector")(_sp)
        evp = s_ev.v
        if _os.environ.get("M_DBG") and tt == 0:
            def _d2(_e, evp=evp):
                e = nc.sync
                e.wait_ge(s_ev.h, evp)
                e.dma_start(out=d["dbg2"][:, 0:56], in_=RT[:, 0:56]).then_inc(s_rt.h, 16)
                e.dma_start(out=d["dbg2"][:, 56:58], in_=E12[:, 0:2]).then_inc(s_rt.h, 16)
                e.wait_ge(s_rt.h, s_rt.v + 32)
            P.E("sync")(_d2)
            s_rt.v += 32

        def _std(_e, tt=tt, evp=evp):
            e = nc.sync
            e.wait_ge(s_ev.h, evp)
            e.dma_start(out=d["rt_ep"][tt * 128:(tt + 1) * 128, :], in_=E12[:, :]
                        ).then_inc(s_rt.h, 16)
            e.dma_start(out=d["rt_pw"][tt * 128:(tt + 1) * 128, :], in_=PW[:, :]
                        ).then_inc(s_rt.h, 16)
        P.E("sync")(_std)
        s_rt.v += 32
        P.bankg[("rt",)] = (s_rt, s_rt.v)
    P.bar()

    XMT = ar.take(BF16, 1024)
    s_sp = P.sem("ffsp")
    for tt in range(16):
        def _tr(_e, tt=tt, pg=P.bankg.get("psbf")):
            t = nc.tensor
            if pg:
                t.wait_ge(pg[0].h, pg[1])
            for c in range(NCH):
                inst = t.transpose(P.psbf[:, c * 128:(c + 1) * 128],
                                   XM[:, c * TOK + tt * 128: c * TOK + (tt + 1) * 128],
                                   P.ident_bf)
            s_pe.inc(inst)
        P.E("tensor")(_tr)
        pv = s_pe.v

        def _cp(_e, tt=tt, pv=pv, xg=P.bankg.get(("xmt",))):
            s = nc.scalar
            s.wait_ge(s_pe.h, pv)
            if xg:
                s.wait_ge(xg[0].h, xg[1])
            s_x.inc(s.copy(XMT, P.psbf[:, 0:1024]))
        P.E("scalar")(_cp)
        xv = s_x.v
        P.bankg["psbf"] = (s_x, s_x.v)

        def _dm(_e, tt=tt, xv=xv):
            e = nc.sync
            e.wait_ge(s_x.h, xv)
            e.dma_start(out=d["xm_loc"][tt * 128:(tt + 1) * 128, :], in_=XMT
                        ).then_inc(s_sp.h, 16)
        P.E("sync")(_dm)
        s_sp.v += 16
        P.bankg[("xmt",)] = (s_sp, s_sp.v)

    def _has(_e):
        e = nc.sync
        for c in range(NCH):
            e.dma_start(out=d["ha_spill"][c * 128:(c + 1) * 128, :],
                        in_=HA[:, c * TOK:(c + 1) * TOK]).then_inc(s_sp.h, 16)
        e.wait_ge(s_sp.h, s_sp.v + 16 * NCH)
    P.E("sync")(_has)
    s_sp.v += 16 * NCH
    s_cc2 = P.sem(tag + "cc2")

    def _cc(_e):
        g = nc.gpsimd
        g.wait_ge(s_sp.h, s_sp.v)
        g.collective_compute("AllGather", ALU.bypass,
                             replica_groups=[list(range(8))],
                             ins=[d["xm_loc"][:, :]], outs=[d["xm_full"][0:16384, :]]
                             ).then_inc(s_cc2.h)
        g.wait_ge(s_cc2.h, 1)
    P.E("gpsimd")(_cc)
    s_cc2.v = 1
    P.bar()

    IT = ar.view(qkv_off, F32, 4096)
    CUM = ar.view(qkv_off + 16384, F32, 4096)
    EPR = ar.view(qkv_off + 32768, F32, 4096)
    GR = ar.view(qkv_off + 49152, F32, 4096)
    KP = ar.view(ha_off, F32, 4096)
    SLT = ar.view(ha_off + 16384, F32, 4096)
    ZR = ar.view(P.xf_off_cur, F32, 336)
    C8 = ar.view(P.xf_off_cur + 1600, F32, 8)
    CA = ar.view(P.xf_off_cur + 1664, F32, 8)
    MC = ar.view(P.xf_off_cur + 1728, F32, 8)
    OFFm1 = ar.view(P.xf_off_cur + 1792, F32, 2)
    s_cc3 = P.sem(tag + "cc3")
    s_cc4 = P.sem(tag + "cc4")

    def _lep(_e):
        e = nc.sync
        e.wait_ge(s_rt.h, s_rt.v)
        e.dma_start(out=EPR[0:1, :], in_=d["rt_ep"].rearrange("a b -> (a b)")[None, :]
                    ).then_inc(s_rt.h, 16)
        e.wait_ge(s_rt.h, s_rt.v + 16)
    P.E("sync")(_lep)
    s_rt.v += 16
    epv = s_rt.v

    for sl in range(8):
        def _bc(_e, sl=sl, epv=epv, w4=P.bankg.get(4)):
            t = nc.tensor
            if sl == 0:
                t.wait_ge(s_rt.h, epv)
            if w4:
                t.wait_ge(w4[0].h, w4[1])
            s_pe.inc(t.matmul(P.ps[4][0:8, :], P.ones_row_f[0:1, 0:8],
                              EPR[0:1, sl * 512:(sl + 1) * 512], start=True, stop=True))
        P.E("tensor")(_bc)
        pv = s_pe.v

        def _cp(_e, sl=sl, pv=pv):
            s = nc.scalar
            s.wait_ge(s_pe.h, pv)
            s_x.inc(s.copy(IT[0:8, sl * 512:(sl + 1) * 512], P.ps[4][0:8, :]))
        P.E("scalar")(_cp)
        P.bankg[4] = (s_x, s_x.v)
    xv = s_x.v

    def _cum(_e, xv=xv):
        v = nc.vector
        v.wait_ge(s_x.h, xv)
        s_ev.inc(v.tensor_scalar(IT[0:8, :], IT[0:8, :], P.EIO8[0:8, 0:1], None,
                                 ALU.is_equal))
        v.wait_ge(s_ev.h, s_ev.v)
        s_ev.inc(v.tensor_tensor_scan(CUM[0:8, :], IT[0:8, :], IT[0:8, :], 0.0,
                                      ALU.add, ALU.bypass))
        v.wait_ge(s_ev.h, s_ev.v)
        v.memset(C8[0:8, 0:8], 0.0)
        s_ev.inc(v.tensor_copy(C8[0:8, 0:1], CUM[0:8, 4095:4096]))
    P.E("vector")(_cum)
    evc = s_ev.v

    def _cdma(_e, evc=evc):
        e = nc.sync
        e.wait_ge(s_ev.h, evc)
        e.dma_start(out=d["cnt_loc"][:, :], in_=C8[0:8, 0:8]).then_inc(s_rt.h, 16)
        e.wait_ge(s_rt.h, s_rt.v + 16)
    P.E("sync")(_cdma)
    s_rt.v += 16

    def _ccc(_e):
        g = nc.gpsimd
        g.wait_ge(s_rt.h, s_rt.v)
        g.collective_compute("AllGather", ALU.bypass,
                             replica_groups=[list(range(8))],
                             ins=[d["cnt_loc"][:, :]], outs=[d["cnt_all"][:, :]]
                             ).then_inc(s_cc3.h)
        g.wait_ge(s_cc3.h, 1)
    P.E("gpsimd")(_ccc)
    s_cc3.v = 1

    def _lca(_e):
        e = nc.sync
        e.wait_ge(s_cc3.h, 1)
        e.dma_start(out=CA[0:64, 0:8], in_=d["cnt_all"][:, :]).then_inc(s_rt.h, 16)
        e.wait_ge(s_rt.h, s_rt.v + 16)
    P.E("sync")(_lca)
    s_rt.v += 16
    cav = s_rt.v

    def _mc(_e, cav=cav):
        v = nc.vector
        v.wait_ge(s_rt.h, cav)
        s_ev.inc(v.tensor_tensor(CA[0:64, 1:2], CA[0:64, 0:1],
                                 d["sb_coremask"][0:64, 0:1], ALU.mult))
    P.E("vector")(_mc)
    evm = s_ev.v

    def _mcd(_e, evm=evm):
        e = nc.sync
        e.wait_ge(s_ev.h, evm)
        e.dma_start(out=d["rt_mc"][:, 0:1], in_=CA[0:64, 1:2]).then_inc(s_rt.h, 16)
        e.wait_ge(s_rt.h, s_rt.v + 16)
        e.dma_start(out=MC[0:8, 0:8], in_=d["rt_mc"].rearrange("(c e) x -> c (e x)", e=8)
                    ).then_inc(s_rt.h, 16)
        e.wait_ge(s_rt.h, s_rt.v + 32)
    P.E("sync")(_mcd)
    s_rt.v += 32
    mcv = s_rt.v

    OFR = ar.view(P.xf_off_cur + 1856, F32, 8)

    def _offs(_e, mcv=mcv, w4=P.bankg.get(4)):
        t = nc.tensor
        t.wait_ge(s_rt.h, mcv)
        if w4:
            t.wait_ge(w4[0].h, w4[1])
        s_pe.inc(t.matmul(P.ps[4][0:1, 0:8], P.ones_col_f[0:8, 0:1], MC[0:8, 0:8],
                          start=True, stop=True))
    P.E("tensor")(_offs)
    pv = s_pe.v

    def _ofc(_e, pv=pv):
        s = nc.scalar
        s.wait_ge(s_pe.h, pv)
        s_x.inc(s.copy(OFR[0:1, :], P.ps[4][0:1, 0:8]))
    P.E("scalar")(_ofc)
    xvo = s_x.v

    def _offs2(_e, xvo=xvo, w5=P.bankg.get(5)):
        t = nc.tensor
        t.wait_ge(s_x.h, xvo)
        if w5:
            t.wait_ge(w5[0].h, w5[1])
        s_pe.inc(t.matmul(P.ps[5][0:8, 0:1], OFR[0:1, 0:8], P.ones_col_f[0:1, 0:1],
                          start=True, stop=True))
    P.E("tensor")(_offs2)
    pv = s_pe.v

    def _off2(_e, pv=pv):
        v = nc.vector
        v.wait_ge(s_pe.h, pv)
        s_ev.inc(v.tensor_scalar(OFFm1[0:8, 0:1], P.ps[5][0:8, 0:1], -1.0, None, ALU.add))
    P.E("vector")(_off2)
    P.bankg[4] = (s_x, s_x.v)
    P.bankg[5] = (s_ev, s_ev.v)
    evo = s_ev.v

    if _os.environ.get("M_DBG"):
        def _dbo(e, evo2=s_ev.v):
            e.wait_ge(s_ev.h, evo2)
            e.dma_start(out=d["dbg2"][0:8, 56:58], in_=OFFm1[0:8, 0:1].broadcast_to((8, 2))
                        ).then_inc(P.sem(tag + "dbo").h, 16) if False else None
            e.dma_start(out=d["dbg2"][0:8, 0:8], in_=MC[0:8, 0:8]).then_inc(P.sem(tag + "dbo").h, 16)
            e.dma_start(out=d["dbg2"][0:64, 8:16], in_=CA[0:64, 0:8]).then_inc(P.sem(tag + "dbo").h, 16)
            e.dma_start(out=d["dbg2"][0:8, 16:24], in_=C8[0:8, 0:8]).then_inc(P.sem(tag + "dbo").h, 16)
            e.dma_start(out=d["dbg2"][0:8, 24:26], in_=P.ar.view(P.xf_off_cur + 1792, F32, 2)[0:8, 0:2]
                        ).then_inc(P.sem(tag + "dbo").h, 16)
            e.wait_ge(P.sem(tag + "dbo").h, 64)
        P.E("sync")(_dbo)
        P.sem(tag + "dbo").v += 64
    def _gr(_e, evo=evo):
        v = nc.vector
        v.wait_ge(s_ev.h, evo)
        s_ev.inc(v.tensor_scalar(CUM[0:8, :], CUM[0:8, :], OFFm1[0:8, 0:1], None, ALU.add))
        v.wait_ge(s_ev.h, s_ev.v)
        s_ev.inc(v.tensor_tensor(CUM[0:8, :], CUM[0:8, :], IT[0:8, :], ALU.mult))
    P.E("vector")(_gr)
    evg = s_ev.v
    for sl in range(8):
        def _gm(_e, sl=sl, evg=evg, w4=P.bankg.get(4)):
            t = nc.tensor
            t.wait_ge(s_ev.h, evg)
            if w4:
                t.wait_ge(w4[0].h, w4[1])
            s_pe.inc(t.matmul(P.ps[4][0:1, :], P.ones_col_f[0:8, 0:1],
                              CUM[0:8, sl * 512:(sl + 1) * 512], start=True, stop=True))
        P.E("tensor")(_gm)
        pv = s_pe.v

        def _gc(_e, sl=sl, pv=pv):
            s = nc.scalar
            s.wait_ge(s_pe.h, pv)
            s_x.inc(s.copy(GR[0:1, sl * 512:(sl + 1) * 512], P.ps[4][0:1, :]))
        P.E("scalar")(_gc)
        P.bankg[4] = (s_x, s_x.v)
    xvg = s_x.v

    def _slots(_e, xvg=xvg):
        v = nc.vector

        def edge(inst):
            s_ev.inc(inst)
            v.wait_ge(s_ev.h, s_ev.v)
        v.wait_ge(s_x.h, xvg)
        edge(v.tensor_single_scalar(KP[0:1, :], GR[0:1, :], float(CAP), ALU.is_lt))
        edge(v.tensor_scalar(SLT[0:1, :], EPR[0:1, :], float(CAP), None, ALU.mult))
        edge(v.tensor_tensor(SLT[0:1, :], SLT[0:1, :], GR[0:1, :], ALU.add))
        edge(v.tensor_single_scalar(GR[0:1, :], SLT[0:1, :], -40960.0, ALU.add))
        edge(v.tensor_tensor(GR[0:1, :], GR[0:1, :], KP[0:1, :], ALU.mult))
        edge(v.tensor_single_scalar(GR[0:1, :], GR[0:1, :], 40960.0, ALU.add))
        s_ev.inc(v.tensor_tensor(SLT[0:1, :], SLT[0:1, :], KP[0:1, :], ALU.mult))
    P.E("vector")(_slots)
    evs = s_ev.v

    def _std2(_e, evs=evs):
        e = nc.sync
        e.wait_ge(s_ev.h, evs)
        e.dma_start(out=d["rt_sl"][None, :], in_=GR[0:1, :]).then_inc(s_rt.h, 16)
        e.dma_start(out=d["rt_rg"][None, :], in_=SLT[0:1, :]).then_inc(s_rt.h, 16)
        e.wait_ge(s_rt.h, s_rt.v + 32)
    P.E("sync")(_std2)
    s_rt.v += 32

    PWR = EPR
    def _lpw(_e):
        e = nc.sync
        e.dma_start(out=PWR[0:1, :], in_=d["rt_pw"].rearrange("a b -> (a b)")[None, :]
                    ).then_inc(s_rt.h, 16)
        e.wait_ge(s_rt.h, s_rt.v + 16)
    P.E("sync")(_lpw)
    s_rt.v += 16
    pwv = s_rt.v

    def _pw(_e, pwv=pwv):
        v = nc.vector
        v.wait_ge(s_rt.h, pwv)
        s_ev.inc(v.tensor_tensor(PWR[0:1, :], PWR[0:1, :], KP[0:1, :], ALU.mult))
    P.E("vector")(_pw)
    evw = s_ev.v

    def _stw(_e, evw=evw):
        e = nc.sync
        e.wait_ge(s_ev.h, evw)
        e.dma_start(out=d["rt_pw2"][None, :], in_=PWR[0:1, :]).then_inc(s_rt.h, 16)
        e.wait_ge(s_rt.h, s_rt.v + 16)
    P.E("sync")(_stw)
    s_rt.v += 16

    SIDX = ar.take(F32, 32)
    SIDXi = ar.take(F32, 32).bitcast(I32)
    T1C = ar.take(F32, 32)

    def _zt(_e):
        v = nc.vector
        s_ev.inc(v.memset(ZR[:, :], 0.0))
    P.E("vector")(_zt)
    evz = s_ev.v

    def _ztd(_e, evz=evz):
        e = nc.sync
        e.wait_ge(s_ev.h, evz)
        e.dma_start(out=d["tbl_loc"].rearrange("(p a) x -> p (a x)", p=128), in_=ZR[:, 0:321]
                    ).then_inc(s_rt.h, 16)
        e.dma_start(out=SIDX[:, :], in_=d["rt_sl"].rearrange("(p a) -> p a", p=128)
                    ).then_inc(s_rt.h, 16)
        e.dma_start(out=T1C[:, :], in_=d["tokid1"][:, :]).then_inc(s_rt.h, 16)
        e.wait_ge(s_rt.h, s_rt.v + 48)
    P.E("sync")(_ztd)
    s_rt.v += 48
    ztv = s_rt.v

    def _cvt(_e, ztv=ztv):
        v = nc.vector
        v.wait_ge(s_rt.h, ztv)
        s_ev.inc(v.tensor_copy(SIDXi[:, :], SIDX[:, :]))
    P.E("vector")(_cvt)
    evi = s_ev.v

    def _scat(_e, evi=evi):
        g = nc.gpsimd
        g.wait_ge(s_ev.h, evi)
        for ci in range(32):
            g.indirect_dma_start(
                d["tbl_loc"][:, :],
                bass.IndirectOffsetOnAxis(ap=SIDXi[:, ci:ci + 1], axis=0),
                T1C[:, ci:ci + 1], None,
            ).then_inc(s_g.h, 16)
        g.wait_ge(s_g.h, s_g.v + 32 * 16)
    P.E("gpsimd")(_scat)
    s_g.v += 32 * 16

    def _ccrs(_e):
        g = nc.gpsimd
        g.collective_compute("ReduceScatter", ALU.add,
                             replica_groups=[list(range(8))],
                             ins=[d["tbl_loc"][0:40960, :]], outs=[d["tbl_rs"][:, :]]
                             ).then_inc(s_cc4.h)
        g.wait_ge(s_cc4.h, 1)
    P.E("gpsimd")(_ccrs)
    s_cc4.v = 1
    P.bar()
    if _os.environ.get("M_DBG"):
        s_db = P.sem(tag + "db")

        def _dbg(e):
            e.dma_start(out=d["dbg_out"][0:1, 0:4096],
                        in_=d["rt_rg"][None, :]).then_inc(s_db.h, 16)
            e.dma_start(out=d["dbg_out"][0:1, 4096:8192],
                        in_=d["rt_sl"][None, :]).then_inc(s_db.h, 16)
            e.dma_start(out=d["dbg_out"][0:1, 8192:12288],
                        in_=d["rt_pw2"][None, :]).then_inc(s_db.h, 16)
            e.dma_start(out=d["dbg_out"][0:1, 12288:17408],
                        in_=d["tbl_rs"].rearrange("a b -> (a b)")[None, :]).then_inc(s_db.h, 16)
            e.dma_start(out=d["dbg_out"].rearrange("a (p n) -> (a p) n", n=160)[0:128, 110:140].rearrange("p n -> p n") if False else d["dbg_out"][0:1, 17408:17409], in_=d["rt_ep"][0:1, 0:1]).then_inc(s_db.h, 16)
            e.wait_ge(s_db.h, 80)
        P.E("sync")(_dbg)
        s_db.v += 64
        P.bar()
    ar.reset(mmoe)
    if FFN_PH >= "J":
        emit_l0_ffn_expert(P, d, XR, qkv_off, ha_off, tag)


def emit_l0_ffn_expert(P, d, XR, qkv_off, ha_off, tag):
    nc = P.nc
    ar = P.ar
    s_ev = P.sem("ev")
    s_pe = P.sem("pe")
    s_x = P.sem(tag + "x")
    s_w = P.sem("ffw")
    s_g = P.sem("ffg")
    s_rt = P.sem(tag + "rt")
    s_st = P.sem("aast")
    mX = ar.mark()

    XT = ar.view(qkv_off, BF16, 8 * CAP)
    xf = P.xf_off_cur

    def xt_at(k, col0, n=512):
        return XT[:, k * CAP + col0: k * CAP + col0 + n]

    RSI = ar.view(xf + 2048, F32, 40)
    RSIi = ar.view(xf + 2304, F32, 40).bitcast(I32)
    XE4 = ar.view(xf + 2560, BF16, 4096)
    s_cc5 = P.sem(tag + "cc5")
    s_cc6 = P.sem(tag + "cc6")

    def _zrow(_e):
        v = nc.vector
        s_ev.inc(v.memset(XE4[0:1, 0:1024], 0.0))
    P.E("vector")(_zrow)
    evz = s_ev.v

    def _zd(_e, evz=evz):
        e = nc.sync
        e.wait_ge(s_ev.h, evz)
        e.dma_start(out=d["xm_full"][16384:16385, :], in_=XE4[0:1, 0:1024]
                    ).then_inc(s_rt.h, 16)
        e.dma_start(out=RSI[:, :],
                    in_=d["tbl_rs"].rearrange("(p a) x -> p (a x)", p=128)
                    ).then_inc(s_rt.h, 16)
        e.wait_ge(s_rt.h, s_rt.v + 32)
    P.E("sync")(_zd)
    s_rt.v += 32
    rsv = s_rt.v

    def _fix(_e, rsv=rsv):
        v = nc.vector

        def edge(inst):
            s_ev.inc(inst)
            v.wait_ge(s_ev.h, s_ev.v)
        v.wait_ge(s_rt.h, rsv)
        edge(v.tensor_single_scalar(XE4.bitcast(F32)[:, 0:40], RSI[:, :], 0.0,
                                    ALU.is_equal))
        edge(v.tensor_single_scalar(RSI[:, :], RSI[:, :], -1.0, ALU.add))
        edge(v.scalar_tensor_tensor(RSI[:, :], XE4.bitcast(F32)[:, 0:40], 16385.0,
                                    RSI[:, :], ALU.mult, ALU.add))
        s_ev.inc(v.tensor_copy(RSIi[:, :], RSI[:, :]))
    P.E("vector")(_fix)
    evf = s_ev.v

    for tg in range(10):
        for q in range(4):
            ci = 4 * tg + q

            def _ga(_e, ci=ci, q=q, evf=evf, gg=P.bankg.get(("xe4", q))):
                g = nc.gpsimd
                if ci == 0:
                    g.wait_ge(s_ev.h, evf)
                if gg:
                    g.wait_ge(gg[0].h, gg[1])
                g.indirect_dma_start(
                    XE4[:, q * 1024:(q + 1) * 1024], None,
                    d["xm_full"][:, :],
                    bass.IndirectOffsetOnAxis(ap=RSIi[:, ci:ci + 1], axis=0),
                ).then_inc(s_g.h, 16)
            P.E("gpsimd")(_ga)
            s_g.v += 16
            gv = s_g.v

            def _tr(_e, q=q, gv=gv, pg=P.bankg.get("psbf")):
                t = nc.tensor
                t.wait_ge(s_g.h, gv)
                if pg:
                    t.wait_ge(pg[0].h, pg[1])
                for c in range(NCH):
                    inst = t.transpose(P.psbf[:, c * 128:(c + 1) * 128],
                                       XE4[:, q * 1024 + c * 128: q * 1024 + (c + 1) * 128],
                                       P.ident_bf)
                s_pe.inc(inst)
            P.E("tensor")(_tr)
            pv = s_pe.v

            def _cp(_e, tg=tg, q=q, pv=pv):
                s = nc.scalar
                s.wait_ge(s_pe.h, pv)
                for c in range(NCH):
                    inst = s.copy(xt_at(c, tg * 512 + q * 128, 128),
                                  P.psbf[:, c * 128:(c + 1) * 128])
                s_x.inc(inst)
            P.E("scalar")(_cp)
            P.bankg["psbf"] = (s_x, s_x.v)
            P.bankg[("xe4", q)] = (s_pe, pv)
    P.bar()

    if FFN_PH < "K":
        ar.reset(mX)
        return
    w1b = [ar.view(xf + 10752 + i * 2048, BF16, 1024) for i in range(2)]
    HP = [ar.view(xf + 14848 + i * 1024, BF16, 512) for i in range(2)]
    wg = {}
    for oc in range(32):
        bs = oc % 2

        def _dma(_e, oc=oc, bs=bs, wgv=wg.get(("w1", bs), 0)):
            g = nc.gpsimd
            if wgv:
                g.wait_ge(s_pe.h, wgv)
            g.dma_start(out=sb3(w1b[bs]), in_=dram_colblock(d["mw1"], oc)
                        ).then_inc(s_w.h, 16)
        P.E("gpsimd")(_dma)
        s_w.v += 16
        wv = s_w.v
        for tg in range(10):
            pv = mm_group(P, s_pe, [(s_w, wv if tg == 0 else 0)], bs, w1b[bs],
                          lambda k, tg=tg: xt_at(k, tg * 512), 8)
            par = tg % 2

            def _ge(_e, oc=oc, tg=tg, pv=pv, bank=bs, par=par,
                    hg=P.bankg.get(("hp", par))):
                s = nc.scalar
                s.wait_ge(s_pe.h, pv)
                if hg:
                    s.wait_ge(hg[0].h, hg[1])
                s_x.inc(s.activation(HP[par], P.ps[bank][:, :], AF.Gelu))
            P.E("scalar")(_ge)
            xv = s_x.v
            P.bankg[bs] = (s_x, s_x.v)

            def _hd(_e, oc=oc, tg=tg, par=par, xv=xv):
                e = nc.sync
                e.wait_ge(s_x.h, xv)
                e.dma_start(out=d["h_buf"][oc * 128:(oc + 1) * 128,
                                           tg * 512:(tg + 1) * 512],
                            in_=HP[par]).then_inc(s_st.h, 16)
            P.E("sync")(_hd)
            s_st.v += 16
            P.bankg[("hp", par)] = (s_st, s_st.v)
        wg[("w1", bs)] = pv
    P.bar()
    ar.reset(mX)

    if FFN_PH < "L":
        ar.reset(mX)
        return
    W2R = ar.view(qkv_off, BF16, 32768)
    HT2 = ar.view(ha_off, BF16, 32 * 512)
    EOP = ar.view(xf + 16896, BF16, 512)
    EOT = ar.view(xf + 17920, BF16, 512)

    def _w2(_e):
        g = nc.gpsimd
        for oc in range(NCH):
            g.dma_start(out=sb3(W2R[:, oc * 4096:(oc + 1) * 4096]),
                        in_=dram_colblock(d["mw2"], oc)).then_inc(s_w.h, 16)
    P.E("gpsimd")(_w2)
    s_w.v += 16 * NCH
    w2v = s_w.v
    s_hh = P.sem(tag + "hh")
    for tg in range(10):
        def _hl(_e, tg=tg, hg=P.bankg.get(("ht2",))):
            e = nc.scalar
            if hg:
                e.wait_ge(hg[0].h, hg[1])
            e.dma_start(out=sb3(HT2, n=512),
                        in_=d["h_buf"].rearrange("(a p) n -> p a n", p=128)[
                            :, :, tg * 512:(tg + 1) * 512]).then_inc(s_hh.h, 16)
        P.E("scalar")(_hl)
        s_hh.v += 16
        hv = s_hh.v
        for oc in range(NCH):
            pv = mm_group(P, s_pe, [(s_w, w2v if (tg == 0 and oc == 0) else 0),
                                    (s_hh, hv if oc == 0 else 0)], 4 + (oc % 2),
                          lambda k, oc=oc: W2R[:, oc * 4096 + k * 128: oc * 4096 + (k + 1) * 128],
                          lambda k: HT2[:, k * 512:(k + 1) * 512], 32)

            def _cpe(_e, oc=oc, pv=pv, bank=4 + (oc % 2), eg=P.bankg.get(("eop",))):
                s = nc.scalar
                s.wait_ge(s_pe.h, pv)
                if eg:
                    s.wait_ge(eg[0].h, eg[1])
                s_x.inc(s.copy(EOP, P.ps[bank][:, :]))
            P.E("scalar")(_cpe)
            xv = s_x.v
            P.bankg[4 + (oc % 2)] = (s_x, s_x.v)

            def _tre(_e, xv=xv, pg=P.bankg.get("psbf")):
                t = nc.tensor
                t.wait_ge(s_x.h, xv)
                if pg:
                    t.wait_ge(pg[0].h, pg[1])
                for i in range(4):
                    inst = t.transpose(P.psbf[:, i * 128:(i + 1) * 128],
                                       EOP[:, i * 128:(i + 1) * 128], P.ident_bf)
                s_pe.inc(inst)
            P.E("tensor")(_tre)
            pvt = s_pe.v
            P.bankg[("eop",)] = (s_pe, pvt)

            def _cpt(_e, pvt=pvt, tg2=P.bankg.get(("eot",))):
                s = nc.scalar
                s.wait_ge(s_pe.h, pvt)
                if tg2:
                    s.wait_ge(tg2[0].h, tg2[1])
                s_x.inc(s.copy(EOT, P.psbf[:, 0:512]))
            P.E("scalar")(_cpt)
            xvt = s_x.v
            P.bankg["psbf"] = (s_x, s_x.v)

            def _eod(_e, tg=tg, oc=oc, xvt=xvt):
                e = nc.sync
                e.wait_ge(s_x.h, xvt)
                for i in range(4):
                    e.dma_start(
                        out=d["eo_loc"].rearrange("(r c) n -> r c n", c=40)[
                            :, 4 * tg + i, oc * 128:(oc + 1) * 128],
                        in_=EOT[:, i * 128:(i + 1) * 128]).then_inc(s_st.h, 16)
            P.E("sync")(_eod)
            s_st.v += 64
            P.bankg[("eot",)] = (s_st, s_st.v)
    P.bar()

    if _os.environ.get("M_DBG"):
        s_db2 = P.sem(tag + "db2")

        def _dbg2(e):
            e.wait_ge(s_st.h, s_st.v)
            e.dma_start(out=d["dbg_out"][0:1, 17408:18432],
                        in_=d["h_buf"][0:1, 0:1024]).then_inc(s_db2.h, 16)
            e.dma_start(out=d["dbg_out"][0:1, 18432:19456],
                        in_=d["eo_loc"][0:1, :]).then_inc(s_db2.h, 16)
            e.dma_start(out=d["dbg_out"][0:1, 19456:20480],
                        in_=d["xm_full"][100:101, :]).then_inc(s_db2.h, 16)
            e.wait_ge(s_db2.h, 48)
        P.E("gpsimd")(_dbg2)
        s_db2.v = 48
        P.bar()
    s_cc7 = P.sem(tag + "cc7")

    def _ceo(_e):
        g = nc.gpsimd
        g.wait_ge(s_st.h, s_st.v)
        g.collective_compute("AllGather", ALU.bypass,
                             replica_groups=[list(range(8))],
                             ins=[d["eo_loc"][:, :]], outs=[d["eo_full"][:, :]]
                             ).then_inc(s_cc7.h)
        g.wait_ge(s_cc7.h, 1)
    P.E("gpsimd")(_ceo)
    s_cc7.v = 1
    P.bar()

    if FFN_PH < "M":
        ar.reset(mX)
        return
    RG = ar.view(qkv_off, F32, 2)
    RGi = ar.view(qkv_off + 64, F32, 2).bitcast(I32)
    PW2 = ar.view(qkv_off + 128, F32, 2)
    G1 = ar.view(xf + 18944, BF16, 1024)
    G2 = ar.view(xf + 20992, BF16, 1024)
    MOA = ar.view(xf + 23040, F32, 1024)
    MOB = ar.view(xf + 27136, BF16, 1024)
    HAr = ar.view(xf + 29184, BF16, 1024)
    T128 = ar.view(xf + 31232, F32, 128)
    for tt in range(M_N):
        def _lrg(_e, tt=tt, gg=P.bankg.get(("rg",))):
            e = nc.sync
            if gg:
                e.wait_ge(gg[0].h, gg[1])
            e.dma_start(out=RG[:, :],
                        in_=d["rt_rg"].rearrange("(t j) -> t j", j=2)[
                            tt * 128:(tt + 1) * 128, :]).then_inc(s_rt.h, 16)
            e.dma_start(out=PW2[:, :],
                        in_=d["rt_pw2"].rearrange("(t j) -> t j", j=2)[
                            tt * 128:(tt + 1) * 128, :]).then_inc(s_rt.h, 16)
            e.dma_start(out=HAr.rearrange("p (c n) -> p c n", n=128),
                        in_=d["ha_spill"].rearrange("(c p) n -> p c n", p=128)[
                            :, :, tt * 128:(tt + 1) * 128]).then_inc(s_rt.h, 16)
        P.E("sync")(_lrg)
        s_rt.v += 48
        lv = s_rt.v

        def _cvt(_e, lv=lv):
            v = nc.vector
            v.wait_ge(s_rt.h, lv)
            s_ev.inc(v.tensor_copy(RGi[:, :], RG[:, :]))
        P.E("vector")(_cvt)
        evc = s_ev.v

        def _g12(_e, evc=evc):
            g = nc.gpsimd
            g.wait_ge(s_ev.h, evc)
            if "g" in M_SKIP:
                g.dma_start(out=G1[:, :], in_=d["eo_full"][0:128, :]).then_inc(s_g.h, 16)
                g.dma_start(out=G2[:, :], in_=d["eo_full"][0:128, :]).then_inc(s_g.h, 16)
            else:
                g.indirect_dma_start(
                    G1[:, :], None, d["eo_full"][:, :],
                    bass.IndirectOffsetOnAxis(ap=RGi[:, 0:1], axis=0)).then_inc(s_g.h, 16)
                g.indirect_dma_start(
                    G2[:, :], None, d["eo_full"][:, :],
                    bass.IndirectOffsetOnAxis(ap=RGi[:, 1:2], axis=0)).then_inc(s_g.h, 16)
        P.E("gpsimd")(_g12)
        s_g.v += 32
        gv = s_g.v

        def _wsum(_e, tt=tt, gv=gv):
            v = nc.vector
            v.wait_ge(s_g.h, gv)
            s_ev.inc(v.tensor_scalar(MOA, G1, PW2[:, 0:1], None, ALU.mult))
            v.wait_ge(s_ev.h, s_ev.v)
            s_ev.inc(v.scalar_tensor_tensor(MOA, G2, PW2[:, 1:2], MOA, ALU.mult, ALU.add))
            v.wait_ge(s_ev.h, s_ev.v)
            s_ev.inc(v.tensor_copy(MOB, MOA))
        P.E("vector")(_wsum)
        evw = s_ev.v

        def _trm(_e, evw=evw, pg=P.bankg.get("psbf")):
            t = nc.tensor
            t.wait_ge(s_ev.h, evw)
            if pg:
                t.wait_ge(pg[0].h, pg[1])
            for c in range(NCH):
                inst = t.transpose(P.psbf[:, c * 128:(c + 1) * 128],
                                   MOB[:, c * 128:(c + 1) * 128], P.ident_bf)
            s_pe.inc(inst)
        P.E("tensor")(_trm)
        pvm = s_pe.v

        def _fin(_e, tt=tt, pvm=pvm):
            v = nc.vector
            v.wait_ge(s_pe.h, pvm)
            for c in range(NCH):
                v.tensor_tensor(T128, P.psbf[:, c * 128:(c + 1) * 128],
                                HAr[:, c * 128:(c + 1) * 128], ALU.add)
                dst = XR[:, c * TOK + tt * 128: c * TOK + (tt + 1) * 128]
                inst = v.tensor_tensor(dst, dst, T128, ALU.add)
            s_ev.inc(inst)
        P.E("vector")(_fin)
        P.bankg["psbf"] = (s_ev, s_ev.v)
        P.bankg[("rg",)] = (s_ev, s_ev.v)
    ar.reset(mX)


def _sl_l0_ffn(P, d, XR, tag):
    ar = P.ar
    m0 = ar.mark()
    d["sb_ln1w"] = ar.take(F32, 8)
    d["sb_ln1b"] = ar.take(F32, 8)
    d["sb_ln2w"] = ar.take(F32, 8)
    d["sb_ln2b"] = ar.take(F32, 8)
    d["sb_bmask"] = ar.take(F32, 4)
    d["sb_coremask"] = ar.take(F32, 1)
    s_ld = P.sem("sbld")

    def _l(e):
        cnt = 0
        for nm, src_nm in (("sb_ln1w", "f_ln1w"), ("sb_ln1b", "f_ln1b"),
                           ("sb_ln2w", "f_ln2w"), ("sb_ln2b", "f_ln2b"),
                           ("sb_bmask", "f_bmask")):
            e.dma_start(out=d[nm][:, :], in_=d[src_nm][:, :]).then_inc(s_ld.h, 16)
            cnt += 16
        e.dma_start(out=d["sb_coremask"][0:64, :], in_=d["f_coremask"][:, :]
                    ).then_inc(s_ld.h, 16)
        cnt += 16
        e.wait_ge(s_ld.h, s_ld.v + cnt)
    P.E("sync")(_l)
    s_ld.v += 96
    P.bar()
    emit_l0_ffn(P, d, XR, tag)
    ar.reset(m0)


def decl_l0_ffn(nc):
    d = {}
    d["f_qkvT"] = nc.dram_tensor("f_qkvT", [DM, 2048], F32, kind="ExternalInput").ap()
    d["f_projT"] = nc.dram_tensor("f_projT", [DM, DM], F32, kind="ExternalInput").ap()
    d["f_router"] = nc.dram_tensor("f_router", [DM, 8], F32, kind="ExternalInput").ap()
    d["mw1"] = nc.dram_tensor("mw1", [DM, 4096], F32, kind="ExternalInput").ap()
    d["mw2"] = nc.dram_tensor("mw2", [4096, DM], F32, kind="ExternalInput").ap()
    for nm in ["f_ln1w", "f_ln1b", "f_ln2w", "f_ln2b"]:
        d[nm] = nc.dram_tensor(nm, [128, 8], F32, kind="ExternalInput").ap()
    d["f_bmask"] = nc.dram_tensor("f_bmask", [128, 4], F32, kind="ExternalInput").ap()
    d["f_coremask"] = nc.dram_tensor("f_coremask", [64, 1], F32, kind="ExternalInput").ap()
    d["tokid1"] = nc.dram_tensor("tokid1", [128, 32], F32, kind="ExternalInput").ap()
    d["kv_loc"] = nc.dram_tensor("kv_loc", [DM, TOK], BF16).ap()
    d["kvA_full"] = nc.dram_tensor("kvA_full", [4 * DM, TOK], BF16).ap()
    d["xm_loc"] = nc.dram_tensor("xm_loc", [TOK, DM], BF16).ap()
    d["xm_full"] = nc.dram_tensor("xm_full", [16385, DM], BF16).ap()
    d["ha_spill"] = nc.dram_tensor("ha_spill", [DM, TOK], BF16).ap()
    d["h_buf"] = nc.dram_tensor("h_buf", [4096, CAP], BF16).ap()
    d["eo_loc"] = nc.dram_tensor("eo_loc", [CAP, DM], BF16).ap()
    d["eo_full"] = nc.dram_tensor("eo_full", [8 * CAP, DM], BF16).ap()
    d["tbl_loc"] = nc.dram_tensor("tbl_loc", [41088, 1], F32).ap()
    d["tbl_rs"] = nc.dram_tensor("tbl_rs", [CAP, 1], F32).ap()
    d["cnt_loc"] = nc.dram_tensor("cnt_loc", [8, 8], F32).ap()
    d["cnt_all"] = nc.dram_tensor("cnt_all", [64, 8], F32).ap()
    d["rt_ep"] = nc.dram_tensor("rt_ep", [TOK, 2], F32).ap()
    d["rt_pw"] = nc.dram_tensor("rt_pw", [TOK, 2], F32).ap()
    d["rt_pw2"] = nc.dram_tensor("rt_pw2", [4096], F32).ap()
    d["rt_sl"] = nc.dram_tensor("rt_sl", [4096], F32).ap()
    d["rt_rg"] = nc.dram_tensor("rt_rg", [4096], F32).ap()
    d["rt_mc"] = nc.dram_tensor("rt_mc", [64, 1], F32).ap()
    return d


def host_l0_ffn_inputs(I, core):
    f = np.float32
    b, h = core // 2, core % 2
    bmask = np.zeros((128, 4), f)
    for bp in range(4):
        if bp > b:
            bmask[:, bp] = -1e9
    cm = np.zeros((64, 1), f)
    cm[0:core * 8, 0] = 1.0
    base = core * TOK
    tok1 = np.zeros((128, 32), f)
    for r in range(128):
        for ci in range(32):
            p = r * 32 + ci
            tok1[r, ci] = base + p // 2 + 1
    return {
        "f_qkvT": np.ascontiguousarray(np.asarray(I["L0_ffn_qkv_weight"], f).T),
        "f_projT": np.ascontiguousarray(np.asarray(I["L0_ffn_proj_weight"], f).T),
        "f_router": np.ascontiguousarray(np.asarray(I["L0_ffn_router_weight"], f)),
        "mw1": np.ascontiguousarray(np.asarray(I["L0_ffn_moe_w1"], f)[core]),
        "mw2": np.ascontiguousarray(np.asarray(I["L0_ffn_moe_w2"], f)[core]),
        "f_ln1w": pack_partition_vec(I["L0_ffn_ln1_weight"]),
        "f_ln1b": pack_partition_vec(I["L0_ffn_ln1_bias"]),
        "f_ln2w": pack_partition_vec(I["L0_ffn_ln2_weight"]),
        "f_ln2b": pack_partition_vec(I["L0_ffn_ln2_bias"]),
        "f_bmask": bmask,
        "f_coremask": cm,
        "tokid1": tok1,
    }


AB_PH = int(_os.environ.get('AB_PH', '8'))
AB_C = 2115
AB_W = 67
AB_SEGS = 4
AB_SC = 580


def emit_l0_attn_b(P, d, XR, tag):
    nc = P.nc
    ar = P.ar
    m0 = ar.mark()

    s_d = P.sem(tag + "d")
    s_g = P.sem(tag + "g")
    s_v = P.sem(tag + "v")
    s_s = P.sem(tag + "s")
    s_pe = P.sem(tag + "pe")
    s_cc = P.sem(tag + "cc")
    s_gd = P.sem(tag + "gd")

    WCb = ar.take(F32, 96)
    eA = ar.take(F32, 8)
    DTBb = ar.take(F32, 8)
    ONWb = ar.take(F32, 1)
    HMb = ar.take(F32, 1)
    MKb = ar.take(F32, 640)
    SLB = ar.take(F32, 8 * AB_W)
    XLF = ar.take(BF16, 8 * AB_C)
    BCN = ar.take(F32, 8)

    def _c0(_e):
        e = nc.sync
        for nm, dst in (("abWC", WCb), ("abALOG", eA), ("abDTB", DTBb),
                        ("abONW", ONWb), ("abHM", HMb), ("abMASK", MKb)):
            e.dma_start(out=dst[:, :], in_=d[nm][:, :]).then_inc(s_d.h, 16)
        e.dma_start(out=d["ab_slab"].rearrange("(a p) n -> p a n", p=128),
                    in_=XR.rearrange("p (a n) -> p a n", n=TOK)[:, :, TOK - AB_W:]
                    ).then_inc(s_d.h, 16)
        e.wait_ge(s_d.h, s_d.v + 7 * 16)
    P.E("sync")(_c0)
    s_d.v += 7 * 16

    def _c1(_e):
        g = nc.gpsimd
        g.wait_ge(s_d.h, s_d.v)
        g.collective_compute("AllGather", ALU.bypass,
                             replica_groups=[[0, 1], [2, 3], [4, 5], [6, 7]],
                             ins=[d["ab_slab"][:, :]], outs=[d["ab_slabF"][:, :]]
                             ).then_inc(s_cc.h)
        g.wait_ge(s_cc.h, 1)
        g.dma_start(out=sb3(SLB, AB_W),
                    in_=d["ab_slabF"].rearrange("(a p) n -> p a n", p=128)[:, 0:8, :]
                    ).then_inc(s_gd.h, 16)
        g.wait_ge(s_gd.h, s_gd.v + 16)
    P.E("gpsimd")(_c1)
    s_cc.v = 1
    s_gd.v += 16

    def _c2(_e):
        v = nc.vector
        v.wait_ge(s_gd.h, s_gd.v)
        v.wait_ge(s_d.h, s_d.v)
        s_v.inc(v.tensor_scalar(SLB[:, :], SLB[:, :], HMb[:, 0:1], None, ALU.mult))
        v.memset(BCN[:, 0:1], 1.0)
        v.memset(BCN[:, 1:2], 1e-6)
        v.memset(BCN[:, 2:3], 1e-5)
        v.memset(BCN[:, 3:4], float(np.log(1.0 / 128.0)))
        s_v.inc(v.memset(BCN[:, 4:5], float(np.log(1.0 / np.sqrt(128.0)))))
        v.wait_ge(s_v.h, s_v.v)
    P.E("vector")(_c2)

    def _c3(_e):
        s = nc.scalar
        s.wait_ge(s_d.h, s_d.v)
        s_s.inc(s.activation(eA[:, :], eA[:, :], AF.Exp))
        s.wait_ge(s_s.h, s_s.v)
    P.E("scalar")(_c3)
    P.bar()

    emit_ln(P, SLB, XLF, tag + "lns", t0=0, ncols=AB_W, src_cols=AB_W,
            dst_cols=AB_C, dst0=0)
    emit_ln(P, XR, XLF, tag + "lnm", t0=0, ncols=TOK,
            dst_cols=AB_C, dst0=AB_W)
    P.bar()

    def _sp(_e):
        e = nc.sync
        e.dma_start(out=d["xr_spill"].rearrange("(a p) n -> p a n", p=128)[:, 0:4, :],
                    in_=XR.rearrange("p (a n) -> p a n", n=TOK)[:, 4:8, :]
                    ).then_inc(s_d.h, 16)
        e.wait_ge(s_d.h, s_d.v + 16)
    P.E("sync")(_sp)
    s_d.v += 16
    P.bar()

    ov = ar.sub(4 * TOK * 4, 4 * TOK * 4)
    VB = ov.take(BF16, 8 * AB_SC)
    F1b = ov.take(BF16, AB_SC)
    F1gb = ov.take(BF16, AB_SC)
    BT = ov.take(BF16, AB_SC)
    OG = ov.take(BF16, 8 * 512)
    PT = ov.take(BF16, AB_SC)
    QS = ov.take(BF16, AB_SC)
    SQ2 = ov.take(F32, 576)
    GIN = ov.take(F32, AB_SC)
    XF = ov.take(F32, 576)
    KTP = ov.take(BF16, 128)
    OSC = ov.take(BF16, 128)
    ORS = ov.take(F32, 1)
    TT = [ov.take(BF16, 128) for _ in range(5)]

    QN = ar.take(BF16, 8 * AB_SC)
    KN = ar.take(BF16, 8 * AB_SC)
    KB = ar.take(BF16, 8 * AB_SC)
    CA = ar.take(F32, 512)
    CB = ar.take(F32, 512)
    RR = ar.take(F32, AB_SC)
    BRb = ar.take(BF16, AB_SC)
    Bbb = ar.take(BF16, AB_SC)
    BROW = ar.take(BF16, 576)

    m_seg = ar.mark()

    def h_v(t, h):
        return t[:, h * AB_SC: h * AB_SC + AB_SC]

    tiles = ((0, 512), (512, 67))

    for seg in range(AB_SEGS):
        ar.reset(m_seg)
        x0 = seg * 512

        WPB = [ar.take(BF16, 1024) for _ in range(2)]
        WSM = ar.take(BF16, 1024)
        WSM2 = ar.take(BF16, 1024)
        WBTb = ar.take(BF16, 64)

        def _wbg(_e):
            g = nc.gpsimd
            g.dma_start(out=sb3(WSM), in_=dram_colblock(d["abWfaT"], 0)
                        ).then_inc(s_gd.h, 16)
            g.dma_start(out=sb3(WSM2), in_=dram_colblock(d["abWgaT"], 0)
                        ).then_inc(s_gd.h, 16)
            g.dma_start(out=WBTb.rearrange("p (a n) -> p a n", n=8),
                        in_=d["abWbT"].rearrange("(a p) n -> p a n", p=128)
                        ).then_inc(s_gd.h, 16)
        P.E("gpsimd")(_wbg)
        s_gd.v += 48
        wv0 = s_d.v
        wv0g = s_gd.v

        for ti, (tc0, tw) in enumerate(tiles) if AB_PH >= 2 else ():
            for di, (wt, dst, b0) in enumerate(((WBTb, BT, 0), (WSM, F1b, 2),
                                                (WSM2, F1gb, 3))):
                mw = 8 if wt is WBTb else 128
                bk, pc = (b0, 0) if ti == 0 else (1, di * 128)

                def _p1(_e, tc0=tc0, tw=tw, pc=pc, wt=wt, mw=mw, bk=bk, wv0=wv0,
                        wv0g=wv0g, x0=x0, g=P.bankg.get(("ab", bk))):
                    t = nc.tensor
                    t.wait_ge(s_d.h, wv0)
                    t.wait_ge(s_gd.h, wv0g)
                    if g:
                        t.wait_ge(g[0].h, g[1])
                    for c in range(NCH):
                        w = (wt.rearrange("p (a n) -> p a n", n=8)[:, c, :] if mw == 8
                             else wt[:, c * 128:(c + 1) * 128])
                        inst = t.matmul(P.ps[bk][0:mw, pc:pc + tw], w,
                                        XLF[:, c * AB_C + x0 + tc0: c * AB_C + x0 + tc0 + tw],
                                        start=(c == 0), stop=(c == NCH - 1))
                    s_pe.inc(inst)
                P.E("tensor")(_p1)
                pv = s_pe.v

                def _p2(_e, tc0=tc0, tw=tw, pc=pc, dst=dst, bk=bk, mw=mw, pv=pv):
                    s = nc.scalar
                    s.wait_ge(s_pe.h, pv)
                    if dst is BT:
                        inst = s.activation(dst[0:8, tc0:tc0 + tw],
                                            P.ps[bk][0:8, pc:pc + tw], AF.Sigmoid)
                    else:
                        inst = s.copy(dst[:, tc0:tc0 + tw], P.ps[bk][0:mw, pc:pc + tw])
                    s_s.inc(inst)
                P.E("scalar")(_p2)
                P.bankg[("ab", bk)] = (s_s, s_s.v)

        for h in range(8) if AB_PH >= 2 else ():
            for pi, (wnm, dstN) in enumerate((("abWqT", QN), ("abWkT", KN),
                                              ("abWvT", VB))):
                buf = pi % 2

                def _wl(_e, wnm=wnm, h=h, buf=buf, g=P.bankg.get(("abw", buf))):
                    e = nc.gpsimd
                    if g:
                        e.wait_ge(g[0].h, g[1])
                    e.dma_start(out=sb3(WPB[buf]), in_=dram_colblock(d[wnm], h)
                                ).then_inc(s_gd.h, 16)
                P.E("gpsimd")(_wl)
                s_gd.v += 16
                wlv = s_gd.v

                for ti, (tc0, tw) in enumerate(tiles):
                    bk, pc = (5, 0) if ti == 0 else (4, 256)

                    def _pm(_e, tc0=tc0, tw=tw, bk=bk, pc=pc, buf=buf, wlv=wlv,
                            x0=x0, g=P.bankg.get(("ab", bk))):
                        t = nc.tensor
                        t.wait_ge(s_gd.h, wlv)
                        if g:
                            t.wait_ge(g[0].h, g[1])
                        for c in range(NCH):
                            inst = t.matmul(P.ps[bk][:, pc:pc + tw],
                                            WPB[buf][:, c * 128:(c + 1) * 128],
                                            XLF[:, c * AB_C + x0 + tc0: c * AB_C + x0 + tc0 + tw],
                                            start=(c == 0), stop=(c == NCH - 1))
                        s_pe.inc(inst)
                    P.E("tensor")(_pm)
                    pv = s_pe.v

                    def _pc(_e, tc0=tc0, tw=tw, bk=bk, pc=pc, pv=pv):
                        s = nc.scalar
                        s.wait_ge(s_pe.h, pv)
                        s.wait_ge(s_v.h, s_v.v)
                        s_s.inc(s.copy(PT[:, tc0:tc0 + tw], P.ps[bk][:, pc:pc + tw]))
                    P.E("scalar")(_pc)
                    P.bankg[("ab", bk)] = (s_s, s_s.v)
                P.bankg[("abw", buf)] = (s_s, s_s.v)
                ptv = s_s.v

                wc0 = pi * 32 + h * 4

                def _cv(_e, wc0=wc0, ptv=ptv):
                    v = nc.vector
                    v.wait_ge(s_s.h, ptv)
                    v.wait_ge(s_d.h, s_d.v)
                    for (o0, w) in ((3, 64), (67, 512)):
                        v.tensor_scalar(CA[:, 0:w], PT[:, o0 - 3:o0 - 3 + w],
                                        WCb[:, wc0:wc0 + 1], None, ALU.mult)
                        v.scalar_tensor_tensor(CB[:, 0:w], PT[:, o0 - 2:o0 - 2 + w],
                                               WCb[:, wc0 + 1:wc0 + 2], CA[:, 0:w],
                                               ALU.mult, ALU.add)
                        v.scalar_tensor_tensor(CA[:, 0:w], PT[:, o0 - 1:o0 - 1 + w],
                                               WCb[:, wc0 + 2:wc0 + 3], CB[:, 0:w],
                                               ALU.mult, ALU.add)
                        inst = v.scalar_tensor_tensor(CB[:, 0:w], PT[:, o0:o0 + w],
                                                      WCb[:, wc0 + 3:wc0 + 4], CA[:, 0:w],
                                                      ALU.mult, ALU.add)
                        s_v.inc(inst)
                        v.wait_ge(s_v.h, s_v.v)
                        s_v.inc(v.tensor_copy(QS[:, o0:o0 + w], CB[:, 0:w]))
                    v.wait_ge(s_v.h, s_v.v)
                P.E("vector")(_cv)
                cvv = s_v.v

                def _si(_e, cvv=cvv):
                    s = nc.scalar
                    s.wait_ge(s_v.h, cvv)
                    s_s.inc(s.activation(QS[:, 3:579], QS[:, 3:579], AF.Silu))
                    s.wait_ge(s_s.h, s_s.v)
                P.E("scalar")(_si)
                siv = s_s.v

                if h == 0 and pi == 0:
                    def _bw(_e):
                        e = nc.sync
                        e.wait_ge(s_s.h, s_s.v)
                        e.dma_start(out=d["ab_brow"][:, :], in_=BT[0:8, 3:579]
                                    ).then_inc(s_d.h, 16)
                        e.wait_ge(s_d.h, s_d.v + 16)
                    P.E("sync")(_bw)
                    s_d.v += 16

                def _bd(_e, h=h, g=P.bankg.get("abrow")):
                    e = nc.sync
                    if g:
                        e.wait_ge(g[0].h, g[1])
                    e.dma_start(out=BROW[0:1, 0:576], in_=d["ab_brow"][h:h + 1, :]
                                ).then_inc(s_d.h, 16)
                P.E("sync")(_bd)
                s_d.v += 16
                bdv = s_d.v

                def _bb(_e, bdv=bdv, g=P.bankg.get(("ab", 6)),
                        g2=P.bankg.get(("ab", 4))):
                    t = nc.tensor
                    for gg in (g, g2):
                        if gg:
                            t.wait_ge(gg[0].h, gg[1])
                    t.wait_ge(s_d.h, bdv)
                    t.matmul(P.ps[6][:, 0:512], P.ones_row_bf, BROW[0:1, 0:512],
                             start=True, stop=True)
                    s_pe.inc(t.matmul(P.ps[4][:, 0:64], P.ones_row_bf,
                                      BROW[0:1, 512:576], start=True, stop=True))
                P.E("tensor")(_bb)
                pvb = s_pe.v
                P.bankg["abrow"] = (s_pe, pvb)

                def _bc(_e, pvb=pvb):
                    s = nc.scalar
                    s.wait_ge(s_pe.h, pvb)
                    s.wait_ge(s_v.h, s_v.v)
                    s.copy(Bbb[:, 3:515], P.ps[6][:, 0:512])
                    s_s.inc(s.copy(Bbb[:, 515:579], P.ps[4][:, 0:64]))
                    s.wait_ge(s_s.h, s_s.v)
                P.E("scalar")(_bc)
                bcv = s_s.v

                if pi < 2:
                    def _sq(_e, siv=siv):
                        s = nc.scalar
                        s.wait_ge(s_s.h, siv)
                        s_s.inc(s.activation(SQ2[:, 0:576], QS[:, 3:579], AF.Square))
                    P.E("scalar")(_sq)
                    sqv = s_s.v

                    def _rm(_e, sqv=sqv, pvb=pvb):
                        t = nc.tensor
                        t.wait_ge(s_s.h, sqv)
                        t.wait_ge(s_s.h, sqv)
                        t.matmul(P.ps[6][0:1, 0:512], P.ones_col_f, SQ2[:, 0:512],
                                 start=True, stop=True)
                        s_pe.inc(t.matmul(P.ps[4][0:1, 64:128], P.ones_col_f,
                                          SQ2[:, 512:576], start=True, stop=True))
                    P.E("tensor")(_rm)
                    pv = s_pe.v

                    def _rs(_e, pv=pv, pi=pi, bcv=bcv):
                        s = nc.scalar
                        s.wait_ge(s_pe.h, pv)
                        s.wait_ge(s_s.h, bcv)
                        s.activation(RR[0:1, 3:515], P.ps[6][0:1, 0:512], AF.Ln,
                                     bias=BCN[0:1, 1:2], scale=1.0 / 128.0)
                        s.activation(RR[0:1, 515:579], P.ps[4][0:1, 64:128], AF.Ln,
                                     bias=BCN[0:1, 1:2], scale=1.0 / 128.0)
                        s_s.inc(s.activation(RR[0:1, 3:579], RR[0:1, 3:579], AF.Exp,
                                             scale=-0.5,
                                             bias=BCN[0:1, 3 + pi:4 + pi]))
                        s.wait_ge(s_s.h, s_s.v)
                    P.E("scalar")(_rs)
                    rsv = s_s.v

                    def _rb(_e, rsv=rsv):
                        t = nc.tensor
                        t.wait_ge(s_s.h, rsv)
                        t.matmul(P.ps[6][:, 0:512], P.ones_row_f, RR[0:1, 3:515],
                                 start=True, stop=True)
                        s_pe.inc(t.matmul(P.ps[4][:, 64:128], P.ones_row_f,
                                          RR[0:1, 515:579], start=True, stop=True))
                    P.E("tensor")(_rb)
                    pv2 = s_pe.v

                    def _rc(_e, pv2=pv2):
                        s = nc.scalar
                        s.wait_ge(s_pe.h, pv2)
                        s.wait_ge(s_v.h, s_v.v)
                        s.copy(BRb[:, 3:515], P.ps[6][:, 0:512])
                        s_s.inc(s.copy(BRb[:, 515:579], P.ps[4][:, 64:128]))
                        s.wait_ge(s_s.h, s_s.v)
                    P.E("scalar")(_rc)
                    P.bankg[("ab", 6)] = (s_s, s_s.v)
                    P.bankg[("ab", 4)] = (s_s, s_s.v)
                    rcv = s_s.v

                    def _rn(_e, h=h, dstN=dstN, rcv=rcv):
                        v = nc.vector
                        v.wait_ge(s_s.h, rcv)
                        s_v.inc(v.tensor_tensor(h_v(dstN, h)[:, 3:579], QS[:, 3:579],
                                                BRb[:, 3:579], ALU.mult))
                        v.wait_ge(s_v.h, s_v.v)
                    P.E("vector")(_rn)

                    if pi == 1:
                        def _kb(_e, h=h, bcv=bcv):
                            v = nc.vector
                            v.wait_ge(s_s.h, bcv)
                            v.wait_ge(s_v.h, s_v.v)
                            s_v.inc(v.tensor_tensor(h_v(KB, h)[:, 3:579],
                                                    h_v(KN, h)[:, 3:579],
                                                    Bbb[:, 3:579], ALU.mult))
                            v.wait_ge(s_v.h, s_v.v)
                        P.E("vector")(_kb)
                else:
                    P.bankg[("ab", 6)] = (s_s, bcv)
                    P.bankg[("ab", 4)] = (s_s, bcv)

                    def _vb(_e, h=h, bcv=bcv, siv=siv):
                        v = nc.vector
                        v.wait_ge(s_s.h, bcv)
                        s_v.inc(v.tensor_tensor(h_v(VB, h)[:, 3:579], QS[:, 3:579],
                                                Bbb[:, 3:579], ALU.mult))
                        v.wait_ge(s_v.h, s_v.v)
                    P.E("vector")(_vb)
        P.bar()

        ar.reset(m_seg)
        m_pair = ar.mark()
        for pr in range(4) if AB_PH >= 3 else ():
            ar.reset(m_pair)
            h0, h1 = 2 * pr, 2 * pr + 1
            GP = [ar.take(F32, AB_SC) for _ in range(2)]
            SG = [ar.take(BF16, AB_SC) for _ in range(2)]
            WFBb = ar.take(BF16, 256)
            WGBb = ar.take(BF16, 256)
            EF = ar.take(BF16, 576)
            EW = ar.take(BF16, 576)
            KQ = [[ar.take(BF16, 9 * 128) for _ in range(2)] for _ in range(5)]
            WT = [[ar.take(BF16, 576) for _ in range(2)] for _ in range(5)]
            KTOall = ar.take(BF16, 2 * 512)
            UV = [ar.take(BF16, 128) for _ in range(9)]
            UB = ar.take(BF16, 128)
            MBD = ar.take(BF16, 128)
            PBD = ar.take(BF16, 128)
            ACM = ar.take(F32, 64)
            ACP = ar.take(F32, 64)
            T2M = ar.take(F32, 64)
            T2P = ar.take(F32, 64)
            UC = [ar.take(BF16, 128) for _ in range(2)]
            SBT = [ar.take(BF16, 128) for _ in range(2)]
            TPS = ar.take(BF16, 128)

            def _wfb(_e, h0=h0):
                e = nc.gpsimd
                e.dma_start(out=WFBb[:, 0:128], in_=d["abWfbT"][:, h0 * 128:(h0 + 1) * 128]
                            ).then_inc(s_gd.h, 16)
                e.dma_start(out=WFBb[:, 128:256],
                            in_=d["abWfbT"][:, (h0 + 1) * 128:(h0 + 2) * 128]
                            ).then_inc(s_gd.h, 16)
                e.dma_start(out=WGBb[:, 0:128], in_=d["abWgbT"][:, h0 * 128:(h0 + 1) * 128]
                            ).then_inc(s_gd.h, 16)
                e.dma_start(out=WGBb[:, 128:256],
                            in_=d["abWgbT"][:, (h0 + 1) * 128:(h0 + 2) * 128]
                            ).then_inc(s_gd.h, 16)
            P.E("gpsimd")(_wfb)
            s_gd.v += 64
            wfv = s_gd.v

            for hh in range(2):
                for ti, (tc0, tw) in enumerate(tiles):
                    bkA, pcA = (0, 0) if ti == 0 else (2, 0)
                    bkG, pcG = (1, 0) if ti == 0 else (2, 128)

                    def _ga(_e, hh=hh, tc0=tc0, tw=tw, bkA=bkA, pcA=pcA, bkG=bkG,
                            pcG=pcG, wfv=wfv, g0=P.bankg.get(("ab", bkA)),
                            g1=P.bankg.get(("ab", bkG)), g2=P.bankg.get(("ab", 2))):
                        t = nc.tensor
                        t.wait_ge(s_gd.h, wfv)
                        t.wait_ge(s_s.h, s_s.v)
                        for g in (g0, g1, g2):
                            if g:
                                t.wait_ge(g[0].h, g[1])
                        s_pe.inc(t.matmul(P.ps[bkA][:, pcA:pcA + tw],
                                          WFBb[:, hh * 128:(hh + 1) * 128],
                                          F1b[:, tc0:tc0 + tw], start=True, stop=True))
                        s_pe.inc(t.matmul(P.ps[bkG][:, pcG:pcG + tw],
                                          WGBb[:, hh * 128:(hh + 1) * 128],
                                          F1gb[:, tc0:tc0 + tw], start=True, stop=True))
                    P.E("tensor")(_ga)
                    pv = s_pe.v

                    def _gs(_e, hh=hh, tc0=tc0, tw=tw, bkA=bkA, pcA=pcA, bkG=bkG,
                            pcG=pcG, pv=pv):
                        s = nc.scalar
                        s.wait_ge(s_pe.h, pv)
                        s.wait_ge(s_d.h, s_d.v)
                        s.wait_ge(s_v.h, s_v.v)
                        h = (h0 if hh == 0 else h1)
                        s.activation(GIN[:, tc0:tc0 + tw], P.ps[bkA][:, pcA:pcA + tw],
                                     AF.Identity, bias=DTBb[:, h:h + 1])
                        s.activation(SG[hh][:, tc0:tc0 + tw], P.ps[bkG][:, pcG:pcG + tw],
                                     AF.Sigmoid)
                        s.activation(RR[:, tc0:tc0 + tw], GIN[:, tc0:tc0 + tw], AF.Abs)
                        s.activation(RR[:, tc0:tc0 + tw], RR[:, tc0:tc0 + tw], AF.Exp,
                                     scale=-1.0)
                        s_s.inc(s.activation(RR[:, tc0:tc0 + tw], RR[:, tc0:tc0 + tw],
                                             AF.Ln, bias=BCN[:, 0:1]))
                        s.wait_ge(s_s.h, s_s.v)
                    P.E("scalar")(_gs)
                    sv = s_s.v
                    P.bankg[("ab", bkA)] = (s_s, sv)
                    P.bankg[("ab", bkG)] = (s_s, sv)
                    P.bankg[("ab", 2)] = (s_s, sv)

                    def _gv(_e, hh=hh, tc0=tc0, tw=tw, sv=sv):
                        v = nc.vector
                        v.wait_ge(s_s.h, sv)
                        h = (h0 if hh == 0 else h1)
                        v.tensor_scalar(GIN[:, tc0:tc0 + tw], GIN[:, tc0:tc0 + tw],
                                        0.0, None, ALU.max)
                        v.tensor_tensor(GIN[:, tc0:tc0 + tw], GIN[:, tc0:tc0 + tw],
                                        RR[:, tc0:tc0 + tw], ALU.add)
                        s_v.inc(v.tensor_scalar(GIN[:, tc0:tc0 + tw],
                                                GIN[:, tc0:tc0 + tw],
                                                eA[:, h:h + 1], None, ALU.mult))
                        v.wait_ge(s_v.h, s_v.v)
                    P.E("vector")(_gv)

                def _gc(_e, hh=hh):
                    v = nc.vector
                    v.wait_ge(s_v.h, s_v.v)
                    s_v.inc(v.tensor_tensor_scan(GP[hh][:, 2:579], GIN[:, 2:579],
                                                 GIN[:, 2:579], 0.0, ALU.add, ALU.bypass))
                    v.wait_ge(s_v.h, s_v.v)
                P.E("vector")(_gc)
            gpv = s_v.v

            def r3(ap, m, off=3, w=576):
                return ap[:, off:off + w].rearrange("p (b m) -> p b m", m=m)

            for hh in range(2):
                h = (h0 if hh == 0 else h1)
                for si, m in enumerate((4, 8, 16, 32, 64)):
                    def _fx(_e, hh=hh, m=m, gpv=gpv):
                        v = nc.vector
                        v.wait_ge(s_v.h, gpv)
                        v.wait_ge(s_s.h, s_s.v)
                        if m == 64:
                            base = r3(GP[hh], 64, 2)[:, :, 0:1]
                        else:
                            base = r3(GP[hh], m)[:, :, 0:1]
                        nb = 576 // m
                        s_v.inc(v.tensor_tensor(r3(XF, m, 0),
                                                base.broadcast_to((128, nb, m)),
                                                r3(GP[hh], m), ALU.subtract))
                        v.wait_ge(s_v.h, s_v.v)
                    P.E("vector")(_fx)
                    xv = s_v.v

                    def _fe(_e, xv=xv):
                        s = nc.scalar
                        s.wait_ge(s_v.h, xv)
                        s_s.inc(s.activation(EF[:, :], XF[:, :], AF.Exp))
                        s.wait_ge(s_s.h, s_s.v)
                    P.E("scalar")(_fe)
                    ev = s_s.v

                    def _fk(_e, hh=hh, si=si, ev=ev, h=h):
                        v = nc.vector
                        v.wait_ge(s_s.h, ev)
                        kq = KQ[si][hh].rearrange("p (b m) -> p b m", m=128)
                        v.tensor_tensor(kq[:, :, 0:64], r3(h_v(KB, h), 64),
                                        r3(EF, 64, 0), ALU.mult)
                        s_v.inc(v.tensor_tensor(kq[:, :, 64:128], r3(h_v(QN, h), 64),
                                                r3(EF, 64, 0), ALU.mult))
                        s_v.inc(v.tensor_scalar(XF[:, :], XF[:, :], -1.0, 86.0,
                                                ALU.mult, ALU.min))
                        v.wait_ge(s_v.h, s_v.v)
                    P.E("vector")(_fk)
                    kv = s_v.v

                    def _fw(_e, kv=kv):
                        s = nc.scalar
                        s.wait_ge(s_v.h, kv)
                        s_s.inc(s.activation(EW[:, :], XF[:, :], AF.Exp))
                        s.wait_ge(s_s.h, s_s.v)
                    P.E("scalar")(_fw)
                    wv = s_s.v

                    def _fw2(_e, hh=hh, si=si, wv=wv, h=h):
                        v = nc.vector
                        v.wait_ge(s_s.h, wv)
                        s_v.inc(v.tensor_tensor(WT[si][hh][:, :], h_v(KN, h)[:, 3:579],
                                                EW[:, :], ALU.mult))
                        v.wait_ge(s_v.h, s_v.v)
                    P.E("vector")(_fw2)

                def _kt(_e, hh=hh, gpv=gpv, h=h):
                    v = nc.vector
                    v.wait_ge(s_v.h, gpv)
                    v.wait_ge(s_s.h, s_s.v)
                    base = GP[hh][:, 66:578].rearrange("p (b m) -> p b m", m=64)[:, :, 0:1]
                    s_v.inc(v.tensor_tensor(
                        r3(XF, 64, 0, 512),
                        GP[hh][:, 3:515].rearrange("p (b m) -> p b m", m=64),
                        base.broadcast_to((128, 8, 64)), ALU.subtract))
                    v.wait_ge(s_v.h, s_v.v)
                P.E("vector")(_kt)
                xv = s_v.v

                def _ke(_e, xv=xv):
                    s = nc.scalar
                    s.wait_ge(s_v.h, xv)
                    s_s.inc(s.activation(EW[:, 0:512], XF[:, 0:512], AF.Exp))
                    s.wait_ge(s_s.h, s_s.v)
                P.E("scalar")(_ke)
                ev = s_s.v

                def _km(_e, hh=hh, ev=ev, h=h):
                    v = nc.vector
                    v.wait_ge(s_s.h, ev)
                    s_v.inc(v.tensor_tensor(KTOall[:, hh * 512:(hh + 1) * 512],
                                            h_v(KN, h)[:, 3:515],
                                            EW[:, 0:512], ALU.mult))
                    v.wait_ge(s_v.h, s_v.v)
                P.E("vector")(_km)
            fv = s_v.v

            prev_asv = 0
            prev_apv = 0
            uvprev = 0
            for j in range(9) if AB_PH >= 4 else ():
                c0 = 3 + 64 * j

                def _am(_e, j=j, fv=fv, pa=prev_asv, pb=prev_apv):
                    t = nc.tensor
                    t.wait_ge(s_v.h, fv)
                    if pa:
                        t.wait_ge(s_v.h, pa)
                    if pb:
                        t.wait_ge(s_v.h, pb)
                    for hh in range(2):
                        for si in range(5):
                            out = (P.ps[0][:, si * 128:(si + 1) * 128] if si < 4
                                   else P.ps[1][:, 0:128])
                            inst = t.matmul(
                                out[hh * 64:hh * 64 + 64, :],
                                WT[si][hh][:, j * 64:(j + 1) * 64],
                                KQ[si][hh][:, j * 128:(j + 1) * 128],
                                start=True, stop=True,
                                tile_position=(0, 64 * hh))
                    s_pe.inc(inst)
                P.E("tensor")(_am)
                amv = s_pe.v

                def _as(_e, amv=amv):
                    v = nc.vector
                    v.wait_ge(s_pe.h, amv)
                    v.tensor_tensor(ACM[:, :], P.ps[0][:, 0:64], MKb[:, 0:64], ALU.mult)
                    for si in range(1, 5):
                        src = (P.ps[0][:, si * 128:si * 128 + 64] if si < 4
                               else P.ps[1][:, 0:64])
                        v.tensor_tensor(T2M[:, :], src, MKb[:, si * 128:si * 128 + 64],
                                        ALU.mult)
                        v.tensor_tensor(ACM[:, :], ACM[:, :], T2M[:, :], ALU.add)
                    v.memset(MBD[:, :], 0.0)
                    v.tensor_copy(MBD[0:64, 0:64], ACM[0:64, :])
                    s_v.inc(v.tensor_copy(MBD[64:128, 64:128], ACM[64:128, :]))
                    v.wait_ge(s_v.h, s_v.v)
                P.E("vector")(_as)
                asv = s_v.v

                def _ap2(_e, amv=amv):
                    g = nc.vector
                    g.wait_ge(s_pe.h, amv)
                    g.tensor_tensor(ACP[:, :], P.ps[0][:, 64:128], MKb[:, 64:128],
                                    ALU.mult)
                    for si in range(1, 5):
                        src = (P.ps[0][:, si * 128 + 64:si * 128 + 128] if si < 4
                               else P.ps[1][:, 64:128])
                        g.tensor_tensor(T2P[:, :], src,
                                        MKb[:, si * 128 + 64:si * 128 + 128], ALU.mult)
                        g.tensor_tensor(ACP[:, :], ACP[:, :], T2P[:, :], ALU.add)
                    g.memset(PBD[:, :], 0.0)
                    g.tensor_copy(PBD[0:64, 0:64], ACP[0:64, :])
                    s_v.inc(g.tensor_copy(PBD[64:128, 64:128], ACP[64:128, :]))
                    g.wait_ge(s_v.h, s_v.v)
                P.E("vector")(_ap2)
                apv = s_v.v
                prev_asv = asv
                prev_apv = apv

                if AB_PH < 5:
                    continue

                def _vtf(_e, c0=c0, h0=h0, h1=h1, g=P.bankg.get("abtps")):
                    v = nc.vector
                    if g:
                        v.wait_ge(g[0].h, g[1])
                    v.tensor_copy(TPS[:, 0:64], h_v(VB, h0)[:, c0:c0 + 64])
                    s_v.inc(v.tensor_copy(TPS[:, 64:128], h_v(VB, h1)[:, c0:c0 + 64]))
                    v.wait_ge(s_v.h, s_v.v)
                P.E("vector")(_vtf)
                vtfv = s_v.v

                def _vt(_e, vtfv=vtfv, g=P.bankg.get("abps")):
                    t = nc.tensor
                    t.wait_ge(s_v.h, vtfv)
                    if g:
                        t.wait_ge(g[0].h, g[1])
                    s_pe.inc(t.transpose(P.psbf[:, 0:128], TPS[:, :], P.ident_bf))
                P.E("tensor")(_vt)
                vtv = s_pe.v
                P.bankg["abtps"] = (s_pe, vtv)

                def _vc(_e, vtv=vtv):
                    s = nc.scalar
                    s.wait_ge(s_pe.h, vtv)
                    s.wait_ge(s_v.h, s_v.v)
                    s_s.inc(s.copy(UC[0][:, :], P.psbf[:, 0:128]))
                    s.wait_ge(s_s.h, s_s.v)
                P.E("scalar")(_vc)
                ucv = s_s.v
                P.bankg["abps"] = (s_s, ucv)

                def _sq1(_e, asv=asv):
                    v = nc.vector
                    v.wait_ge(s_v.h, asv)
                    v.wait_ge(s_pe.h, s_pe.v)
                    s_v.inc(v.tensor_copy(TT[0][:, :], MBD[:, :]))
                    v.wait_ge(s_v.h, s_v.v)
                P.E("vector")(_sq1)
                t0v = s_v.v
                t0eng = "v"

                cur = 0
                lastv = ucv
                lasteng = "s"
                for i in range(5):
                    if i > 0:
                        def _tr(_e, i=i, t0v=t0v, t0eng=t0eng,
                                g=P.bankg.get("abps2")):
                            t = nc.tensor
                            t.wait_ge(s_v.h if t0eng == "v" else s_s.h, t0v)
                            if g:
                                t.wait_ge(g[0].h, g[1])
                            s_pe.inc(t.transpose(P.psbf[:, 128:256], TT[i - 1][:, :],
                                                 P.ident_bf))
                        P.E("tensor")(_tr)
                        trv = s_pe.v

                        def _tc(_e, trv=trv):
                            s = nc.scalar
                            s.wait_ge(s_pe.h, trv)
                            s_s.inc(s.copy(UB[:, :], P.psbf[:, 128:256]))
                            s.wait_ge(s_s.h, s_s.v)
                        P.E("scalar")(_tc)
                        tcv = s_s.v
                        P.bankg["abps2"] = (s_s, tcv)

                        def _sq2(_e, i=i, tcv=tcv, g=P.bankg.get(("ab", 2))):
                            t = nc.tensor
                            t.wait_ge(s_s.h, tcv)
                            if g:
                                t.wait_ge(g[0].h, g[1])
                            s_pe.inc(t.matmul(P.ps[2][:, 0:128], UB[:, :],
                                              TT[i - 1][:, :], start=True, stop=True))
                        P.E("tensor")(_sq2)
                        sqv = s_pe.v

                        def _sq3(_e, i=i, sqv=sqv):
                            s = nc.vector
                            s.wait_ge(s_pe.h, sqv)
                            s_v.inc(s.tensor_scalar(TT[i][:, :], P.ps[2][:, 0:128],
                                                    0.0, None, ALU.add))
                            s.wait_ge(s_v.h, s_v.v)
                        P.E("vector")(_sq3)
                        t0v = s_v.v
                        t0eng = "v"
                        P.bankg[("ab", 2)] = (s_v, t0v)

                    def _ua(_e, i=i, cur=cur, t0v=t0v, t0eng=t0eng, lastv=lastv,
                            lasteng=lasteng, g=P.bankg.get(("ab", 3))):
                        t = nc.tensor
                        t.wait_ge(s_v.h if t0eng == "v" else s_s.h, t0v)
                        t.wait_ge(s_v.h if lasteng == "v" else s_s.h, lastv)
                        if g:
                            t.wait_ge(g[0].h, g[1])
                        s_pe.inc(t.matmul(P.ps[3][:, 0:128], TT[i][:, :], UC[cur][:, :],
                                          start=True, stop=True))
                    P.E("tensor")(_ua)
                    uav = s_pe.v

                    def _ux(_e, i=i, j=j, cur=cur, uav=uav):
                        v = nc.vector
                        v.wait_ge(s_pe.h, uav)
                        dst = UV[j][:, :] if i == 4 else UC[1 - cur][:, :]
                        s_v.inc(v.tensor_tensor(dst, UC[cur][:, :], P.ps[3][:, 0:128],
                                                ALU.add))
                        v.wait_ge(s_v.h, s_v.v)
                    P.E("vector")(_ux)
                    lastv = s_v.v
                    lasteng = "v"
                    P.bankg[("ab", 3)] = (s_v, lastv)
                    if i < 4:
                        cur = 1 - cur
                uvv = lastv

                if j == 0 or AB_PH < 6:
                    uvprev = uvv
                    continue

                def _ktf(_e, j=j, g=P.bankg.get("abtps")):
                    v = nc.vector
                    if g:
                        v.wait_ge(g[0].h, g[1])
                    v.tensor_copy(TPS[:, 0:64], KTOall[:, (j - 1) * 64: j * 64])
                    s_v.inc(v.tensor_copy(TPS[:, 64:128],
                                          KTOall[:, 512 + (j - 1) * 64: 512 + j * 64]))
                    v.wait_ge(s_v.h, s_v.v)
                P.E("vector")(_ktf)
                ktfv = s_v.v

                def _kt2(_e, ktfv=ktfv, g=P.bankg.get("abps")):
                    t = nc.tensor
                    t.wait_ge(s_v.h, ktfv)
                    if g:
                        t.wait_ge(g[0].h, g[1])
                    s_pe.inc(t.transpose(P.psbf[:, 0:128], TPS[:, :], P.ident_bf))
                P.E("tensor")(_kt2)
                ktv = s_pe.v
                P.bankg["abtps"] = (s_pe, ktv)

                def _kc2(_e, ktv=ktv):
                    s = nc.scalar
                    s.wait_ge(s_pe.h, ktv)
                    s_s.inc(s.copy(KTP[:, :], P.psbf[:, 0:128]))
                    s.wait_ge(s_s.h, s_s.v)
                P.E("scalar")(_kc2)
                kcv = s_s.v
                P.bankg["abps"] = (s_s, kcv)

                def _sb(_e, j=j, kcv=kcv, uvw=uvprev, asv=asv, apv=apv,
                        g=P.bankg.get(("ab", 4)), g1=P.bankg.get(("ab", 1))):
                    t = nc.tensor
                    t.wait_ge(s_s.h, kcv)
                    t.wait_ge(s_v.h, uvw)
                    t.wait_ge(s_v.h, asv)
                    t.wait_ge(s_v.h, apv)
                    for gg in (g, g1):
                        if gg:
                            t.wait_ge(gg[0].h, gg[1])
                    t.matmul(P.ps[4][:, 0:128], KTP[0:64, :], UV[j - 1][0:64, :],
                             start=True, stop=True, tile_position=(0, 0))
                    s_pe.inc(t.matmul(P.ps[1][:, 128:256], KTP[64:128, :],
                                      UV[j - 1][64:128, :],
                                      start=True, stop=True, tile_position=(64, 0)))
                P.E("tensor")(_sb)
                sbv = s_pe.v

                def _sc(_e, sbv=sbv):
                    s = nc.vector
                    s.wait_ge(s_pe.h, sbv)
                    s.tensor_scalar(SBT[0][:, :], P.ps[4][:, 0:128], 0.0, None, ALU.add)
                    s_v.inc(s.tensor_scalar(SBT[1][:, :], P.ps[1][:, 128:256],
                                            0.0, None, ALU.add))
                    s.wait_ge(s_v.h, s_v.v)
                P.E("vector")(_sc)
                scv = s_v.v
                P.bankg[("ab", 4)] = (s_v, scv)
                P.bankg[("ab", 1)] = (s_v, scv)

                def _cr(_e, j=j, scv=scv, g=P.bankg.get(("ab", 5))):
                    t = nc.tensor
                    t.wait_ge(s_v.h, scv)
                    if g:
                        t.wait_ge(g[0].h, g[1])
                    for hh in range(2):
                        inst = t.matmul(
                            P.ps[5][hh * 64:hh * 64 + 64, 0:128],
                            KQ[4][hh][:, j * 128:j * 128 + 64],
                            SBT[hh][:, :], start=True, stop=True,
                            tile_position=(0, 64 * hh))
                    s_pe.inc(inst)
                P.E("tensor")(_cr)
                crv = s_pe.v

                def _cc2(_e, crv=crv):
                    s = nc.vector
                    s.wait_ge(s_pe.h, crv)
                    s.wait_ge(s_v.h, s_v.v)
                    s_v.inc(s.tensor_scalar(UC[0][:, :], P.ps[5][:, 0:128],
                                            0.0, None, ALU.add))
                    s.wait_ge(s_v.h, s_v.v)
                P.E("vector")(_cc2)
                ccv = s_v.v
                P.bankg[("ab", 5)] = (s_v, ccv)

                cur = 0
                lastv = ccv
                lasteng = "v"
                for i in range(5):
                    def _ca(_e, i=i, cur=cur, lastv=lastv, lasteng=lasteng,
                            g=P.bankg.get(("ab", 3))):
                        t = nc.tensor
                        t.wait_ge(s_v.h if lasteng == "v" else s_s.h, lastv)
                        if g:
                            t.wait_ge(g[0].h, g[1])
                        s_pe.inc(t.matmul(P.ps[3][:, 0:128], TT[i][:, :], UC[cur][:, :],
                                          start=True, stop=True))
                    P.E("tensor")(_ca)
                    cav = s_pe.v

                    def _cx(_e, i=i, cur=cur, cav=cav):
                        v = nc.vector
                        v.wait_ge(s_pe.h, cav)
                        s_v.inc(v.tensor_tensor(UC[1 - cur][:, :], UC[cur][:, :],
                                                P.ps[3][:, 0:128], ALU.add))
                        v.wait_ge(s_v.h, s_v.v)
                    P.E("vector")(_cx)
                    lastv = s_v.v
                    lasteng = "v"
                    P.bankg[("ab", 3)] = (s_v, lastv)
                    cur = 1 - cur

                if AB_PH < 7:
                    uvprev = uvv
                    continue

                def _uo(_e, j=j, cur=cur, lastv=lastv, uvv=uvv):
                    v = nc.vector
                    v.wait_ge(s_v.h, lastv)
                    v.wait_ge(s_v.h, uvv)
                    s_v.inc(v.tensor_tensor(UC[cur][:, :], UV[j][:, :], UC[cur][:, :],
                                            ALU.subtract))
                    v.wait_ge(s_v.h, s_v.v)
                P.E("vector")(_uo)
                uov = s_v.v

                def _om(_e, j=j, uov=uov, apv=apv, scv=scv, cur=cur,
                        g=P.bankg.get(("ab", 6))):
                    t = nc.tensor
                    t.wait_ge(s_v.h, uov)
                    t.wait_ge(s_v.h, apv)
                    if g:
                        t.wait_ge(g[0].h, g[1])
                    for hh in range(2):
                        t.matmul(P.ps[6][hh * 64:hh * 64 + 64, 0:128],
                                 PBD[hh * 64:hh * 64 + 64, hh * 64:hh * 64 + 64],
                                 UC[cur][hh * 64:hh * 64 + 64, :],
                                 start=True, stop=False,
                                 tile_position=(64 * hh, 64 * hh))
                        inst = t.matmul(
                            P.ps[6][hh * 64:hh * 64 + 64, 0:128],
                            KQ[4][hh][:, j * 128 + 64:(j + 1) * 128],
                            SBT[hh][:, :], start=False, stop=True,
                            tile_position=(0, 64 * hh))
                    s_pe.inc(inst)
                P.E("tensor")(_om)
                omv = s_pe.v

                def _e1(_e, omv=omv):
                    s = nc.scalar
                    s.wait_ge(s_pe.h, omv)
                    s.wait_ge(s_v.h, s_v.v)
                    s_s.inc(s.activation(SQ2[:, 0:128], P.ps[6][:, 0:128], AF.Square))
                    s.wait_ge(s_s.h, s_s.v)
                P.E("scalar")(_e1)
                e1v = s_s.v

                def _e2(_e, e1v=e1v):
                    v = nc.vector
                    v.wait_ge(s_s.h, e1v)
                    s_v.inc(v.tensor_reduce(ORS[:, 0:1], SQ2[:, 0:128],
                                            axis=AX, op=ALU.add))
                    v.wait_ge(s_v.h, s_v.v)
                P.E("vector")(_e2)
                e2v = s_v.v

                def _e3(_e, e2v=e2v):
                    s = nc.scalar
                    s.wait_ge(s_v.h, e2v)
                    s.activation(ORS[:, 0:1], ORS[:, 0:1], AF.Ln,
                                 bias=BCN[:, 2:3], scale=1.0 / 128.0)
                    s_s.inc(s.activation(ORS[:, 0:1], ORS[:, 0:1], AF.Exp, scale=-0.5))
                    s.wait_ge(s_s.h, s_s.v)
                P.E("scalar")(_e3)
                e3v = s_s.v

                def _e4(_e, e3v=e3v):
                    v = nc.vector
                    v.wait_ge(s_s.h, e3v)
                    s_v.inc(v.tensor_scalar(OSC[:, :], P.ps[6][:, 0:128],
                                            ORS[:, 0:1], None, ALU.mult))
                    v.wait_ge(s_v.h, s_v.v)
                P.E("vector")(_e4)
                e4v = s_v.v
                P.bankg[("ab", 6)] = (s_v, e4v)

                def _e5(_e, e4v=e4v, g=P.bankg.get("abps3")):
                    t = nc.tensor
                    t.wait_ge(s_v.h, e4v)
                    if g:
                        t.wait_ge(g[0].h, g[1])
                    s_pe.inc(t.transpose(P.psbf[:, 256:384], OSC[:, :], P.ident_bf))
                P.E("tensor")(_e5)
                e5v = s_pe.v

                for hh in range(2):
                    h = (h0 if hh == 0 else h1)

                    def _e6(_e, hh=hh, h=h, j=j, e5v=e5v):
                        g = nc.vector
                        g.wait_ge(s_pe.h, e5v)
                        g.wait_ge(s_s.h, s_s.v)
                        s_v.inc(g.scalar_tensor_tensor(
                            OG[:, h * 512 + (j - 1) * 64: h * 512 + j * 64],
                            SG[hh][:, 3 + 64 * j: 3 + 64 * j + 64],
                            ONWb[:, 0:1], P.psbf[:, 256 + 64 * hh: 320 + 64 * hh],
                            ALU.mult, ALU.mult))
                        g.wait_ge(s_v.h, s_v.v)
                    P.E("vector")(_e6)
                P.bankg["abps3"] = (s_v, s_v.v)
                uvprev = uvv
            P.bar()

        ar.reset(m_seg)
        WOB = [ar.take(BF16, 1024) for _ in range(2)]
        XRS = ar.take(F32, 512)
        for fc in range(NCH) if AB_PH >= 8 else ():
            buf = fc % 2

            def _wo(_e, fc=fc, buf=buf, g=P.bankg.get(("abwo", buf))):
                e = nc.gpsimd
                if g:
                    e.wait_ge(g[0].h, g[1])
                e.dma_start(out=sb3(WOB[buf]), in_=dram_colblock(d["abWoT"], fc)
                            ).then_inc(s_gd.h, 16)
            P.E("gpsimd")(_wo)
            s_gd.v += 16
            wov = s_gd.v

            bk = fc % 2 + 2

            def _wm(_e, fc=fc, buf=buf, wov=wov, bk=bk, g=P.bankg.get(("ab", bk))):
                t = nc.tensor
                t.wait_ge(s_gd.h, wov)
                t.wait_ge(s_v.h, s_v.v)
                if g:
                    t.wait_ge(g[0].h, g[1])
                for c in range(NCH):
                    inst = t.matmul(P.ps[bk][:, :], WOB[buf][:, c * 128:(c + 1) * 128],
                                    OG[:, c * 512:(c + 1) * 512],
                                    start=(c == 0), stop=(c == NCH - 1))
                s_pe.inc(inst)
            P.E("tensor")(_wm)
            wmv = s_pe.v
            P.bankg[("abwo", buf)] = (s_pe, wmv)

            if fc < 4:
                def _xa(_e, fc=fc, seg=seg, wmv=wmv, bk=bk):
                    v = nc.vector
                    v.wait_ge(s_pe.h, wmv)
                    dst = XR[:, fc * TOK + seg * 512: fc * TOK + seg * 512 + 512]
                    s_v.inc(v.tensor_tensor(dst, dst, P.ps[bk][:, :], ALU.add))
                    v.wait_ge(s_v.h, s_v.v)
                P.E("vector")(_xa)
                P.bankg[("ab", bk)] = (s_v, s_v.v)
            else:
                def _xl(_e, fc=fc, seg=seg):
                    e = nc.sync
                    e.wait_ge(s_v.h, s_v.v)
                    e.dma_start(out=XRS[:, :],
                                in_=d["xr_spill"][(fc - 4) * 128:(fc - 3) * 128,
                                                  seg * 512:seg * 512 + 512]
                                ).then_inc(s_d.h, 16)
                P.E("sync")(_xl)
                s_d.v += 16
                xlv = s_d.v

                def _xa2(_e, wmv=wmv, xlv=xlv, bk=bk):
                    v = nc.vector
                    v.wait_ge(s_pe.h, wmv)
                    v.wait_ge(s_d.h, xlv)
                    s_v.inc(v.tensor_tensor(XRS[:, :], XRS[:, :], P.ps[bk][:, :],
                                            ALU.add))
                    v.wait_ge(s_v.h, s_v.v)
                P.E("vector")(_xa2)
                P.bankg[("ab", bk)] = (s_v, s_v.v)
                xav = s_v.v

                def _xs(_e, fc=fc, seg=seg, xav=xav):
                    e = nc.sync
                    e.wait_ge(s_v.h, xav)
                    e.dma_start(out=d["xr_spill"][(fc - 4) * 128:(fc - 3) * 128,
                                                  seg * 512:seg * 512 + 512],
                                in_=XRS[:, :]).then_inc(s_d.h, 16)
                    e.wait_ge(s_d.h, s_d.v + 16)
                P.E("sync")(_xs)
                s_d.v += 16
        P.bar()

    def _rs2(_e):
        e = nc.sync
        e.dma_start(out=XR.rearrange("p (a n) -> p a n", n=TOK)[:, 4:8, :],
                    in_=d["xr_spill"].rearrange("(a p) n -> p a n", p=128)[:, 0:4, :]
                    ).then_inc(s_d.h, 16)
        e.wait_ge(s_d.h, s_d.v + 16)
    P.E("sync")(_rs2)
    s_d.v += 16
    P.bar()
    ar.reset(m0)


def _sl_l0_attn_b(P, d, XR, tag):
    emit_l0_attn_b(P, d, XR, tag)


def decl_l0_attn_b(nc):
    d = {}
    for nm in ("abWqT", "abWkT", "abWvT", "abWoT"):
        d[nm] = nc.dram_tensor(nm, [DM, DM], F32, kind="ExternalInput").ap()
    d["abWfaT"] = nc.dram_tensor("abWfaT", [DM, 128], F32, kind="ExternalInput").ap()
    d["abWfbT"] = nc.dram_tensor("abWfbT", [128, DM], F32, kind="ExternalInput").ap()
    d["abWgaT"] = nc.dram_tensor("abWgaT", [DM, 128], F32, kind="ExternalInput").ap()
    d["abWgbT"] = nc.dram_tensor("abWgbT", [128, DM], F32, kind="ExternalInput").ap()
    d["abWbT"] = nc.dram_tensor("abWbT", [DM, 8], F32, kind="ExternalInput").ap()
    d["abWC"] = nc.dram_tensor("abWC", [128, 96], F32, kind="ExternalInput").ap()
    d["abALOG"] = nc.dram_tensor("abALOG", [128, 8], F32, kind="ExternalInput").ap()
    d["abDTB"] = nc.dram_tensor("abDTB", [128, 8], F32, kind="ExternalInput").ap()
    d["abONW"] = nc.dram_tensor("abONW", [128, 1], F32, kind="ExternalInput").ap()
    d["abHM"] = nc.dram_tensor("abHM", [128, 1], F32, kind="ExternalInput").ap()
    d["abMASK"] = nc.dram_tensor("abMASK", [128, 640], F32, kind="ExternalInput").ap()
    d["ab_slab"] = nc.dram_tensor("ab_slab", [DM, AB_W], F32).ap()
    d["ab_brow"] = nc.dram_tensor("ab_brow", [8, 576], BF16).ap()
    d["ab_slabF"] = nc.dram_tensor("ab_slabF", [2 * DM, AB_W], F32).ap()
    return d


def host_l0_attn_b_inputs(I, core):
    f = np.float32
    T = lambda a: np.ascontiguousarray(np.asarray(a, f).T)
    wc = np.zeros((128, 96), f)
    for pi, nm in enumerate(("L0_attn_b_wq_conv", "L0_attn_b_wk_conv",
                             "L0_attn_b_wv_conv")):
        w = np.asarray(I[nm], f)
        for c in range(8):
            for jj in range(4):
                wc[:, pi * 32 + c * 4 + jj] = w[c * 128:(c + 1) * 128, 0, jj]
    mask = np.zeros((128, 640), f)
    t_i = np.arange(64)
    for p, m in enumerate((4, 8, 16, 32, 64)):
        for s in range(64):
            for t in range(64):
                if m == 4:
                    va = (t // 4 == s // 4) and t > s
                    vp = (t // 4 == s // 4) and t >= s
                else:
                    va = vp = (t // m == s // m) and (t // (m // 2) == s // (m // 2) + 1)
                if va:
                    mask[s, p * 128 + t] = -1.0
                    mask[s + 64, p * 128 + t] = -1.0
                if vp:
                    mask[s, p * 128 + 64 + t] = 1.0
                    mask[s + 64, p * 128 + 64 + t] = 1.0
    return {
        "abWqT": T(I["L0_attn_b_Wq"]),
        "abWkT": T(I["L0_attn_b_Wk"]),
        "abWvT": T(I["L0_attn_b_Wv"]),
        "abWoT": T(I["L0_attn_b_Wo"]),
        "abWfaT": T(I["L0_attn_b_Wfa"]),
        "abWfbT": T(I["L0_attn_b_Wfb"]),
        "abWgaT": T(I["L0_attn_b_Wga"]),
        "abWgbT": T(I["L0_attn_b_Wgb"]),
        "abWbT": T(I["L0_attn_b_Wb"]),
        "abWC": wc,
        "abALOG": np.tile(np.asarray(I["L0_attn_b_A_log"], f)[None, :], (128, 1)),
        "abDTB": T(I["L0_attn_b_dt_bias"]),
        "abONW": np.asarray(I["L0_attn_b_o_norm_w"], f)[:, None],
        "abHM": np.full((128, 1), core % 2, f),
        "abMASK": mask,
    }


def build_prog(sublayers, decls):
    nc = bass.Bass("TRN2", target_bir_lowering=False, debug=False,
                   detect_race_conditions=False)
    d = {}
    d["xT"] = nc.dram_tensor("xT", [DM, TOK], F32, kind="ExternalInput").ap()
    d["outT"] = nc.dram_tensor("outT", [DM, TOK], F32, kind="ExternalOutput").ap()
    d["consts"] = nc.dram_tensor("consts", [128, 1984], F32, kind="ExternalInput").ap()
    d["dbg_out"] = nc.dram_tensor("dbg_out", [1, 20480], F32, kind="ExternalOutput").ap()
    d["dbg2"] = nc.dram_tensor("dbg2", [128, 64], F32, kind="ExternalOutput").ap()
    d["xr_spill"] = nc.dram_tensor("xr_spill", [NSPILL * 128, TOK], F32).ap()
    for decl in decls:
        d.update(decl(nc))

    giant = nc.alloc_sbuf_tensor("ARENA", [128, ARENA_BYTES // 4], F32)

    with nc.Block() as block:
        ar = Arena(giant, ARENA_BYTES)
        XR = ar.take(F32, NCH * TOK)
        P = Prog(nc, block, ar)
        P.dummy = ar.take(F32, 16)
        P.ones_col_bf = ar.take(BF16, 32)[:, 0:1]
        onesrow = ar.take(BF16, 128)
        P.ones_row_bf = onesrow[0:1, :]
        P.ones_col_f = ar.take(F32, 16)[:, 0:1]
        onesrowf = ar.take(F32, 128)
        P.ones_row_f = onesrowf[0:1, :]
        P.sel64 = ar.take(F32, 2)
        sel64T = ar.take(F32, 128)
        P.sel64T = sel64T[0:2, :]
        ident = ar.take(BF16, 128)
        P.ident_bf = ident
        i2 = ar.take(F32, 2)
        P.I2 = i2[0:2, :]
        tij = ar.take(F32, 512)
        P.Tij = tij
        sel16 = ar.take(F32, 1024)
        P.SEL16 = sel16
        iot8 = ar.take(F32, 8)
        P.IOT8 = iot8
        eio = ar.take(F32, 1)
        P.EIO8 = eio
        P.persist = ar.mark()
        cst = ar.take(F32, 1984)

        s_in = P.sem("in")

        def _load(e):
            e.dma_start(out=sb3(XR, n=TOK),
                        in_=d["xT"].rearrange("(a p) n -> p a n", p=128)
                        ).then_inc(s_in.h, 16)
            e.dma_start(out=cst[:, :], in_=d["consts"][:, :]).then_inc(s_in.h, 16)
        P.E("sync")(_load)
        s_in.v += 32

        def _c(e):
            e.wait_ge(s_in.h, s_in.v)
            e.tensor_copy(P.ones_col_bf, cst[:, 0:1])
            e.tensor_copy(onesrow[:, :], cst[:, 0:1].broadcast_to((128, 128)))
            e.tensor_copy(P.ones_col_f, cst[:, 0:1])
            e.tensor_copy(onesrowf[:, :], cst[:, 0:1].broadcast_to((128, 128)))
            e.tensor_copy(P.sel64, cst[:, 2:4])
            e.tensor_copy(sel64T[:, :], cst[:, 4:132])
            e.tensor_copy(ident[:, :], cst[:, 132:260])
            e.tensor_copy(i2[:, :], cst[:, 260:262])
            e.tensor_copy(tij[:, :], cst[:, 384:896])
            e.tensor_copy(sel16[:, :], cst[:, 896:1920])
            e.tensor_copy(iot8[:, :], cst[:, 1920:1928])
            e.tensor_copy(eio[:, :], cst[:, 1928:1929])
        P.E("vector")(_c)
        P.bar()
        ar.reset(P.persist)

        for i, sl_name in enumerate(sublayers):
            SUBLAYER_EMITTERS[sl_name](P, d, XR, tag=f"s{i}")
            P.bar()

        s_out = P.sem("out")

        def _store(e):
            e.dma_start(out=d["outT"].rearrange("(a p) n -> p a n", p=128),
                        in_=sb3(XR, n=TOK)).then_inc(s_out.h, 16)
            e.wait_ge(s_out.h, 16)
        P.E("sync")(_store)
    return nc


def _sl_l1_ffn(P, d, XR, tag):
    ar = P.ar
    m0 = ar.mark()
    d["sb_s1"] = ar.take(F32, 32)
    d["sb_s2"] = ar.take(F32, 32)
    d["sb_sh1"] = ar.take(F32, 32)
    d["sb_sh2"] = ar.take(F32, 32)
    d["sb_s3"] = ar.take(F32, 8)
    s_ld = P.sem("sbld")

    def _l(e):
        for nm, dst in (("l1f_s1", d["sb_s1"]), ("l1f_s2", d["sb_s2"]),
                        ("l1f_sh1", d["sb_sh1"]), ("l1f_sh2", d["sb_sh2"]),
                        ("l1f_s3", d["sb_s3"])):
            e.dma_start(out=dst[:, :], in_=d[nm][:, :]).then_inc(s_ld.h, 16)
        e.wait_ge(s_ld.h, s_ld.v + 80)
    P.E("sync")(_l)
    s_ld.v += 80
    P.bar()
    emit_l1_ffn(P, d, XR, tag)
    ar.reset(m0)


SUBLAYER_EMITTERS = {
    "l1_ffn": _sl_l1_ffn,
    "l0_attn_a": _sl_l0_attn_a,
    "l1_attn_b": _sl_l1_attn_b,
    "l1_attn_a": _sl_l1_attn_a,
    "l0_ffn": _sl_l0_ffn,
    "l0_attn_b": _sl_l0_attn_b,
}


def decl_l1_ffn(nc):
    d = {}
    d["l1f_w1T"] = nc.dram_tensor("l1f_w1T", [DM, 4096], F32, kind="ExternalInput").ap()
    d["l1f_w2T"] = nc.dram_tensor("l1f_w2T", [DM, 4096], F32, kind="ExternalInput").ap()
    d["l1f_h1T"] = nc.dram_tensor("l1f_h1T", [4096, 4096], F32, kind="ExternalInput").ap()
    d["l1f_h2T"] = nc.dram_tensor("l1f_h2T", [4096, 4096], F32, kind="ExternalInput").ap()
    d["l1f_w3T"] = nc.dram_tensor("l1f_w3T", [4096, DM], F32, kind="ExternalInput").ap()
    for nm, n in (("s1", 32), ("s2", 32), ("sh1", 32), ("sh2", 32), ("s3", 8)):
        d["l1f_" + nm] = nc.dram_tensor("l1f_" + nm, [128, n], F32, kind="ExternalInput").ap()
    return d


def pack_partition_vec(v):
    n = v.shape[0] // 128
    return np.ascontiguousarray(np.asarray(v, np.float32).reshape(n, 128).T)


def host_l1_ffn_inputs(I):
    f = np.float32
    return {
        "l1f_w1T": np.ascontiguousarray(np.asarray(I["L1_ffn_w1_w"], f).T),
        "l1f_w2T": np.ascontiguousarray(np.asarray(I["L1_ffn_w2_w"], f).T),
        "l1f_h1T": np.ascontiguousarray(np.asarray(I["L1_ffn_h1_w"], f).T),
        "l1f_h2T": np.ascontiguousarray(np.asarray(I["L1_ffn_h2_w"], f).T),
        "l1f_w3T": np.ascontiguousarray(np.asarray(I["L1_ffn_w3_w"], f).T),
        "l1f_s1": pack_partition_vec(I["L1_ffn_w1_s"]),
        "l1f_s2": pack_partition_vec(I["L1_ffn_w2_s"]),
        "l1f_sh1": pack_partition_vec(I["L1_ffn_h1_s"]),
        "l1f_sh2": pack_partition_vec(I["L1_ffn_h2_s"]),
        "l1f_s3": pack_partition_vec(I["L1_ffn_w3_s"]),
    }


def host_consts():
    c = np.zeros((128, 1984), np.float32)
    c[:, 0] = 1.0
    for h in range(2):
        c[h * 64:(h + 1) * 64, 2 + h] = 1.0
        c[h, 4 + h * 64: 4 + (h + 1) * 64] = 1.0
    c[:, 132:260] = np.eye(128, dtype=np.float32)
    c[0, 260] = 1.0
    c[1, 261] = 1.0
    c[:, 384:896] = (np.arange(128)[:, None] - np.arange(512)[None, :]).astype(np.float32)
    for hp in range(8):
        for p in range(128):
            c[2 * hp + p // 64, 896 + hp * 128 + p] = 1.0
    c[:, 1920:1928] = np.arange(8, dtype=np.float32)[None, :]
    c[0:8, 1928] = np.arange(8, dtype=np.float32)
    return c


MK_SEQ = ["l0_attn_a", "l0_ffn", "l0_attn_b", "l0_ffn",
          "l1_attn_a", "l1_ffn", "l1_attn_b", "l1_ffn"]
MK_DECLS = None


def kernel(**inputs):
    I = inputs
    decls = [decl_l0_attn_a, decl_l0_ffn, decl_l0_attn_b,
             decl_l1_attn_a, decl_l1_ffn, decl_l1_attn_b]
    nc = build_prog(MK_SEQ, decls)
    x = np.asarray(I["L0_attn_a_x"], np.float32)
    consts = host_consts()
    shared = {}
    shared.update(host_l0_attn_a_inputs(I))
    shared.update(host_l1_ffn_inputs(I))
    in_maps = []
    for c in range(NCORES):
        b, h = c // 2, c % 2
        sh = np.ascontiguousarray(x[b, h * TOK:(h + 1) * TOK, :].T)
        m = dict(shared, xT=sh, consts=consts)
        m.update(host_l0_ffn_inputs(I, c))
        m.update(host_l0_attn_b_inputs(I, c))
        m.update(host_l1_attn_a_inputs(I, c))
        m.update(host_l1_attn_b_inputs(I, c))
        in_maps.append(m)
    res = run_bass_kernel_spmd(nc, in_maps, core_ids=list(range(NCORES)))
    out = np.zeros((4, 4096, DM), np.float32)
    for c in range(NCORES):
        b, h = c // 2, c % 2
        out[b, h * TOK:(h + 1) * TOK, :] = res.results[c]["outT"].T
    return out
